# version 02
# baseline (speedup 1.0000x reference)
import numpy as np
import ml_dtypes
from contextlib import ExitStack
import concourse.bass as bass
import concourse.mybir as mybir
from concourse.bass_utils import run_bass_kernel_spmd

F32 = mybir.dt.float32
BF16 = mybir.dt.bfloat16
I32 = mybir.dt.int32
AF = mybir.ActivationFunctionType
ALU = mybir.AluOpType
AX = mybir.AxisListType
BF16NP = ml_dtypes.bfloat16

T = 4096
D = 1024
NDT = 8
NCORES = 8
ARENA_F32 = 34816
LN_OFF = 20480

ENGS = ("pe", "dve", "act", "pool", "sp")


class Prog:
    def __init__(self, nc):
        self.nc = nc
        self.ctx = ExitStack()
        self.eng = {"pe": nc.tensor, "dve": nc.vector, "act": nc.scalar,
                    "pool": nc.gpsimd, "sp": nc.sync}
        self.sem = {}
        self.cnt = {}
        for e in ENGS:
            self.sem[e] = self.ctx.enter_context(nc.semaphore(f"sem_{e}"))
            self.cnt[e] = 0
        self.chan_eng = {}
        self.ch_waited = {}
        self.ch_version = {}
        self.watermark = {e: {} for e in ENGS}
        self.last_write = {}
        self.readers = {}
        self.pending = {e: False for e in ENGS}

    def chan(self, name, eng="pool"):
        key = "ch_" + name
        if key not in self.sem:
            self.sem[key] = self.ctx.enter_context(self.nc.semaphore(key))
            self.cnt[key] = 0
            self.chan_eng[key] = eng
        return key

    def _wait(self, eng, token):
        src, val = token
        if src.startswith("ch_"):
            self.ch_waited[src] = True
        if src == eng:
            if eng == "pe":
                return
            if val > self.cnt[eng]:
                raise RuntimeError(
                    f"same-engine dep on pending op ({eng}); producer must inc")
        if src.startswith("ch_"):
            val = self.cnt[src]
        wm = self.watermark[eng]
        if wm.get(src, 0) >= val:
            return
        self.eng[eng].wait_ge(self.sem[src], val)
        wm[src] = val

    def _deps(self, eng, reads, writes):
        for b in reads:
            t = self.last_write.get(b)
            if t is not None:
                self._wait(eng, t)
        for b in writes:
            t = self.last_write.get(b)
            if t is not None:
                self._wait(eng, t)
            for t in self.readers.get(b, ()):
                self._wait(eng, t)

    def _record(self, token, reads, writes):
        for b in writes:
            self.last_write[b] = token
            self.readers[b] = []
        for b in reads:
            self.readers.setdefault(b, []).append(token)

    def op(self, eng, emit, reads=(), writes=(), inc=True):
        self._deps(eng, reads, writes)
        inst = emit(self.eng[eng])
        if inc:
            inst.then_inc(self.sem[eng], 1)
            self.cnt[eng] += 1
            self.pending[eng] = False
            token = (eng, self.cnt[eng])
        else:
            self.pending[eng] = True
            token = (eng, self.cnt[eng] + 1)
        self._record(token, reads, writes)
        return token

    def _fresh_chan(self, ch):
        if "misc" not in ch:
            return ch
        if not self.ch_waited.get(ch):
            return ch
        base = ch.split("#")[0]
        v = self.ch_version.get(base, 0) + 1
        self.ch_version[base] = v
        eng = self.chan_eng[ch]
        newk = f"{base}#{v}"
        if newk not in self.sem:
            self.sem[newk] = self.ctx.enter_context(self.nc.semaphore(newk))
            self.cnt[newk] = 0
            self.chan_eng[newk] = eng
        return newk

    def dma(self, ch, out_ap, in_ap, reads=(), writes=(), **kw):
        ch = self._fresh_chan(ch)
        eng = self.chan_eng[ch]
        self._deps(eng, reads, writes)
        self.eng[eng].dma_start(out_ap, in_ap, **kw).then_inc(self.sem[ch], 16)
        self.cnt[ch] += 16
        token = (ch, self.cnt[ch])
        self._record(token, reads, writes)
        return token

    def idma(self, ch, out_ap, off_out, in_ap, off_in, reads=(), writes=()):
        ch = self._fresh_chan(ch)
        eng = self.chan_eng[ch]
        self._deps(eng, reads, writes)
        self.eng[eng].indirect_dma_start(out_ap, off_out, in_ap, off_in).then_inc(self.sem[ch], 16)
        self.cnt[ch] += 16
        token = (ch, self.cnt[ch])
        self._record(token, reads, writes)
        return token

    def flush(self, eng, reads=(), writes=()):
        self._deps(eng, reads, writes)

    def finalize(self):
        for e in ENGS:
            if self.pending[e]:
                self.op(e, lambda x: x.nop(), inc=True)
        for key, v in self.cnt.items():
            if key.startswith("ch_") and v > 0:
                self.eng["sp"].wait_ge(self.sem[key], v)
        for e in ENGS:
            if e != "sp" and self.cnt[e] > 0:
                self.eng["sp"].wait_ge(self.sem[e], self.cnt[e])


class Bufs:
    def __init__(self, nc, ctx):
        self.stream = ctx.enter_context(nc.sbuf_tensor("stream", [128, NDT * T], BF16))
        self.arena = ctx.enter_context(nc.sbuf_tensor("arena", [128, ARENA_F32], F32))
        self.arena_b = self.arena[:, :].bitcast(BF16)
        self.idn = ctx.enter_context(nc.sbuf_tensor("idn_sb", [128, 128], BF16))
        self.onesb = ctx.enter_context(nc.sbuf_tensor("onesb", [128, 2], BF16))
        self.onesrow = ctx.enter_context(nc.sbuf_tensor("onesrow", [1, 128], F32))
        self.onesbrow = ctx.enter_context(nc.sbuf_tensor("onesbrow", [1, 128], BF16))
        self.epsc = ctx.enter_context(nc.sbuf_tensor("epsc", [1, 2], F32))
        self.bias = ctx.enter_context(nc.sbuf_tensor("biasarena", [128, 96], F32))
        self.idx32 = ctx.enter_context(nc.sbuf_tensor("idx32", [128, 64], I32))
        self.ps = [ctx.enter_context(nc.psum_tensor(f"psum{i}", [128, 512], F32)) for i in range(6)]
        self.pst = [ctx.enter_context(nc.psum_tensor(f"psumt{i}", [128, 1024], BF16)) for i in range(2)]
        self._bank = 0

    def bank(self):
        i = self._bank
        self._bank = (self._bank + 1) % 6
        return self.ps[i], f"psb{i}"

    def ar(self, off, cols, p0=0, pn=128):
        assert 0 <= off and off + cols <= ARENA_F32, (off, cols)
        return self.arena[p0:p0 + pn, off:off + cols]

    def arb(self, offf32, cols_bf16, p0=0, pn=128):
        off = offf32 * 2
        assert 0 <= off and off + cols_bf16 <= ARENA_F32 * 2, (offf32, cols_bf16)
        return self.arena_b[p0:p0 + pn, off:off + cols_bf16]

    @staticmethod
    def keys(off, cols_f32):
        b0 = off // 2048
        b1 = (off + max(cols_f32, 1) - 1) // 2048
        return tuple(f"blk{i}" for i in range(b0, b1 + 1))


def ssl(B, dt, t0, n):
    return B.stream[:, dt * T + t0:dt * T + t0 + n]


def _pf(v):
    v = np.ascontiguousarray(np.asarray(v), dtype=np.float32)
    n = v.shape[0] // 128
    return np.ascontiguousarray(v.reshape(n, 128).T)


def host_prep(inputs):
    g = {}
    f32 = np.float32
    c = lambda x: np.ascontiguousarray(np.asarray(x), dtype=f32)

    g["l0_wqkv"] = c(inputs["L0_attn_w_qkv"])
    g["l0_bqkv"] = _pf(inputs["L0_attn_b_qkv"])
    g["l0_wproj"] = c(inputs["L0_attn_w_proj"])
    g["l0_bproj"] = _pf(inputs["L0_attn_b_proj"])
    g["l0_g1"] = np.stack([
        c(np.transpose(np.asarray(inputs["L0_ffn_G1_R"])[0], (1, 2, 0))),
        c(np.transpose(np.asarray(inputs["L0_ffn_G1_I"])[0], (1, 2, 0)))])
    g["l0_g2"] = np.stack([
        c(np.transpose(np.asarray(inputs["L0_ffn_G2_R"])[:, :, 0, :], (0, 2, 1))),
        c(np.transpose(np.asarray(inputs["L0_ffn_G2_I"])[:, :, 0, :], (0, 2, 1)))])
    g["l0_fbias"] = _pf(np.concatenate([c(inputs["L0_ffn_bias_real"]),
                                        c(inputs["L0_ffn_bias_imag"])]))
    w = np.asarray(inputs["L1_attn_w_qkv"], dtype=f32).reshape(3, 128, 8, 1024)
    g["l1_wqkv"] = c(np.transpose(w, (3, 0, 2, 1)).reshape(1024, 3072))
    b = np.asarray(inputs["L1_attn_b_qkv"], dtype=f32).reshape(3, 128, 8)
    g["l1_bqkv"] = c(np.transpose(b, (0, 2, 1)).reshape(24, 128).T)
    g["l1_bqkv_row"] = c(np.transpose(b, (0, 2, 1)).reshape(1, 3072))
    g["l1_wo"] = c(np.asarray(inputs["L1_attn_w_o"], dtype=f32).T)
    g["l1_bo"] = _pf(inputs["L1_attn_b_o"])
    g["l1_wg"] = c(inputs["L1_ffn_Wg"])
    g["l1_w1"] = c(inputs["L1_ffn_W1"])
    g["l1_w2"] = c(inputs["L1_ffn_W2"])
    g["l1_w3"] = c(inputs["L1_ffn_W3"])
    g["l2_uvw"] = c(np.asarray(inputs["L2_attn_uv_w"], dtype=f32).T)
    g["l2_uvb"] = _pf(inputs["L2_attn_uv_b"])
    g["l2_lnw"] = _pf(inputs["L2_attn_ln_w"])
    g["l2_lnb"] = _pf(inputs["L2_attn_ln_b"])
    gam = np.asarray(inputs["L2_attn_gamma"], dtype=f32)
    bet = np.asarray(inputs["L2_attn_beta"], dtype=f32)
    g["l2_gb"] = c(np.stack([gam[0], bet[0], gam[1], bet[1]], axis=1))
    g["l2_ow"] = c(np.asarray(inputs["L2_attn_o_w"], dtype=f32).T)
    g["l2_ob"] = _pf(inputs["L2_attn_o_b"])
    w_rel = np.asarray(inputs["L2_attn_w"], dtype=f32)
    wx = w_rel[np.clip(np.arange(-4224, 4736), 0, 1022)]
    g["l2_wxr"] = c(wx[::-1])
    dj = np.arange(128)
    idx = np.zeros((128, 63), np.int32)
    for dd in range(-31, 32):
        idx[:, dd + 31] = 4224 - 128 * dd - dj
    g["l2_bidx"] = idx
    half = 64
    inv = (np.float32(10000.0) ** (np.arange(half, dtype=f32) / np.float32(half))).astype(f32)
    ang = (np.arange(T, dtype=f32)[:, None] * inv[None, :]).astype(f32)
    cosv = np.cos(ang.astype(np.float64)).astype(f32).T
    sinv = np.sin(ang.astype(np.float64)).astype(f32).T
    g["l2_tab1"] = c(np.concatenate([cosv, cosv], axis=0))
    g["l2_tab2s"] = c(np.concatenate([-sinv, sinv], axis=0))
    g["l2_ctxw"] = c(np.asarray(inputs["L2_ffn_ctx_w"], dtype=f32).T)
    g["l2_ctxb"] = _pf(inputs["L2_ffn_ctx_b"])
    g["l2_gih"] = c(np.asarray(inputs["L2_ffn_gru_w_ih"], dtype=f32).T)
    g["l2_ghh"] = c(np.asarray(inputs["L2_ffn_gru_w_hh"], dtype=f32).T)
    g["l2_exprw"] = c(np.asarray(inputs["L2_ffn_expr_w"], dtype=f32).T)
    gw = np.asarray(inputs["L2_ffn_gate_w"], dtype=f32)
    uw = np.asarray(inputs["L2_ffn_up_w"], dtype=f32)
    dw = np.asarray(inputs["L2_ffn_down_w"], dtype=f32)
    g["l2_gate"] = c(np.concatenate([np.transpose(gw, (0, 2, 1)),
                                     np.asarray(inputs["L2_ffn_shared_gate_w"], dtype=f32).T[None]], axis=0))
    g["l2_up"] = c(np.concatenate([np.transpose(uw, (0, 2, 1)),
                                   np.asarray(inputs["L2_ffn_shared_up_w"], dtype=f32).T[None]], axis=0))
    g["l2_down"] = c(np.concatenate([np.transpose(dw, (0, 2, 1)),
                                     np.asarray(inputs["L2_ffn_shared_down_w"], dtype=f32).T[None]], axis=0))
    g["l3_nm1"] = c(inputs["L3_attn_nm_w1"]); g["l3_nm1b"] = _pf(inputs["L3_attn_nm_b1"])
    g["l3_nm2"] = c(inputs["L3_attn_nm_w2"]); g["l3_nm2b"] = _pf(inputs["L3_attn_nm_b2"])
    g["l3_qw"] = c(inputs["L3_attn_q_w"]); g["l3_qb"] = _pf(inputs["L3_attn_q_b"])
    g["l3_kw"] = c(inputs["L3_attn_k_w"]); g["l3_kb"] = _pf(inputs["L3_attn_k_b"])
    g["l3_vw"] = c(inputs["L3_attn_v_w"]); g["l3_vb"] = _pf(inputs["L3_attn_v_b"])
    masks = np.zeros((3, 128, 192), dtype=f32)
    for var, i0 in ((0, 0), (1, 128), (2, T - 128)):
        s0 = max(0, min(i0 - 32, T - 192))
        for di in range(128):
            i = i0 + di
            for w in range(-32, 33):
                j = i + w
                if 0 <= j < T and 0 <= j - s0 < 192:
                    masks[var, di, j - s0] = 1.0
    g["l3_masks"] = c(masks.reshape(3 * 128, 192))
    mq = np.asarray(inputs["L3_ffn_mosa_wqkv"], dtype=f32)
    dq = np.asarray(inputs["L3_ffn_dense_wqkv"], dtype=f32)
    g["l3_mqkv"] = c(np.transpose(mq, (1, 0, 2)))
    g["l3_dqkv"] = c(np.transpose(dq, (1, 0, 2)))
    g["l3_mwo"] = c(inputs["L3_ffn_mosa_wo"])
    g["l3_dwo"] = c(inputs["L3_ffn_dense_wo"])
    g["l3_router"] = c(inputs["L3_ffn_mosa_router_w"])
    halfr = 16
    invr = (np.float32(10000.0) ** (-(np.arange(halfr, dtype=f32)) / np.float32(halfr))).astype(f32)
    angr = (np.arange(T, dtype=f32)[:, None] * invr[None, :]).astype(f32)
    cr = np.cos(angr.astype(np.float64)).astype(f32).T
    sr = np.sin(angr.astype(np.float64)).astype(f32).T
    one = np.ones((32, T), dtype=f32); zer = np.zeros((32, T), dtype=f32)
    g["l3_t1"] = c(np.concatenate([cr, cr, one, cr, cr, one], axis=0))
    g["l3_t2"] = c(np.concatenate([-sr, sr, zer, -sr, sr, zer], axis=0))
    tri = np.zeros((128, 4, 512), dtype=f32)
    dk = np.arange(128)[:, None]; dq = np.arange(512)[None, :]
    for o in range(4):
        tri[:, o, :] = (dk <= dq - o * 128) * 1.0
    g["l3_tri"] = c(tri.reshape(128, 4 * 512))
    eye = np.zeros((128, 64), dtype=f32)
    for e in range(8):
        eye[:, e * 8 + e] = 1.0
    g["l2_eye"] = eye
    bas = np.zeros((8, 1024), dtype=f32)
    for e in range(8):
        bas[e, e * 128:(e + 1) * 128] = 1.0
    g["bas8"] = bas
    g["idn"] = np.eye(128, dtype=f32)
    return g


SLOT_INPUTS = {
    "prologue": ["idn"],
    "l0_attn": ["l0_wqkv", "l0_bqkv", "l0_wproj", "l0_bproj"],
    "l0_ffn": ["l0_g1", "l0_g2", "l0_fbias"],
    "l1_attn": ["l1_wqkv", "l1_bqkv", "l1_bqkv_row", "l1_wo", "l1_bo"],
    "l1_ffn": ["l1_wg", "l1_w1", "l1_w2", "l1_w3", "bas8"],
    "l2_attn": ["l2_uvw", "l2_uvb", "l2_lnw", "l2_lnb", "l2_gb", "l2_ow", "l2_ob",
                "l2_wxr", "l2_bidx", "l2_tab1", "l2_tab2s"],
    "l2_ffn": ["l2_ctxw", "l2_ctxb", "l2_gih", "l2_ghh", "l2_exprw", "l2_gate",
               "l2_up", "l2_down", "l2_eye", "bas8"],
    "l3_attn": ["l3_nm1", "l3_nm1b", "l3_nm2", "l3_nm2b", "l3_qw", "l3_qb",
                "l3_kw", "l3_kb", "l3_vw", "l3_vb", "l3_masks"],
    "l3_ffn": ["l3_mqkv", "l3_dqkv", "l3_mwo", "l3_dwo", "l3_router",
               "l3_t1", "l3_t2", "l3_tri"],
}


def emit_prologue(P, B, gap, x_hbm):
    P.dma(P.chan("pro_misc", "pool"), B.idn[:], gap["idn"], writes=("idn",))
    P.op("dve", lambda v: v.memset(B.onesb[:], 1.0), writes=("onesb",))
    P.op("dve", lambda v: v.memset(B.onesrow[:], 1.0), writes=("onesrow",))
    P.op("dve", lambda v: v.memset(B.onesbrow[:], 1.0), writes=("onesbrow",))
    P.op("dve", lambda v: v.memset(B.epsc[:], 1e-5), writes=("epsc",))
    for tt in range(T // 128):
        i = tt % 2
        stg = B.arb(i * 2048, 1024)
        skey = f"blk{i}"
        P.dma(P.chan(f"xstg{i}", "pool"), stg, x_hbm[tt * 128:(tt + 1) * 128, :], writes=(skey,))
        for dt in range(NDT):
            pk = f"pstb{dt % 2}"
            pst = B.pst[dt % 2]
            P.op("pe", lambda t, s=stg, d=dt, p=pst: t.transpose(
                p[:, 0:128], s[:, d * 128:(d + 1) * 128], B.idn[:]),
                reads=(skey, "idn"), writes=(pk,))
            P.op("dve", lambda v, d=dt, q=tt, p=pst: v.tensor_copy(
                ssl(B, d, q * 128, 128), p[:, 0:128]),
                reads=(pk,), writes=(f"st{dt}",))


def emit_epilogue(P, B, y_hbm):
    for tt in range(T // 128):
        i = tt % 2
        stg = B.ar(i * 2048, 1024)
        skey = f"blk{i}"
        for dt in range(NDT):
            pk = f"pstb{dt % 2}"
            pst = B.pst[dt % 2]
            P.op("pe", lambda t, d=dt, q=tt, p=pst: t.transpose(
                p[:, 0:128], ssl(B, d, q * 128, 128), B.idn[:]),
                reads=(f"st{dt}", "idn"), writes=(pk,))
            P.op("dve", lambda v, d=dt, p=pst, s=stg: v.tensor_copy(
                s[:, d * 128:(d + 1) * 128], p[:, 0:128]),
                reads=(pk,), writes=(skey,))
        P.dma(P.chan(f"ystg{i}", "sp"), y_hbm[tt * 128:(tt + 1) * 128, :], stg, reads=(skey,))


def emit_ln_stats(P, B, off, rboff, mboff, eps=1e-5):
    sq = B.arb(off, 4096)
    sqk = B.keys(off, 2048)
    r1 = off + 2048
    r1k = B.keys(r1, 4096)
    S1 = B.ar(r1, 4096, 0, 1)
    S2 = B.ar(r1, 4096, 32, 1)
    mu = B.ar(r1, 4096, 64, 1)
    r2 = r1 + 4096
    r2k = B.keys(r2, 4096)
    var = B.ar(r2, 4096, 0, 1)
    mu2 = B.ar(r2, 4096, 32, 1)
    bsc = B.ar(r2 + 4096, 512, 0, 1)
    bsck = B.keys(r2 + 4096, 512)
    rb = B.arb(rboff, 4096)
    rbk = B.keys(rboff, 2048)
    mb = B.arb(mboff, 4096)
    mbk = B.keys(mboff, 2048)

    for tc in range(8):
        b1, k1 = B.bank()
        for dt in range(NDT):
            P.op("pe", lambda t, d=dt, q=tc, b=b1: t.matmul(
                b[0:1, :], B.onesb[:, 0:1], ssl(B, d, q * 512, 512),
                start=(d == 0), stop=(d == NDT - 1)),
                reads=(f"st{dt}", "onesb"), writes=(k1,), inc=(dt == NDT - 1))
        P.op("dve", lambda v, q=tc, b=b1: v.tensor_copy(S1[:, q * 512:(q + 1) * 512], b[0:1, :]),
             reads=(k1,), writes=r1k)
        for dt in range(NDT):
            P.op("dve", lambda v, d=dt, q=tc: v.tensor_mul(
                sq[:, d * 512:(d + 1) * 512], ssl(B, d, q * 512, 512), ssl(B, d, q * 512, 512)),
                reads=(f"st{dt}",), writes=sqk)
        b2, k2 = B.bank()
        for dt in range(NDT):
            P.op("pe", lambda t, d=dt, b=b2: t.matmul(
                b[0:1, :], B.onesb[:, 0:1], sq[:, d * 512:(d + 1) * 512],
                start=(d == 0), stop=(d == NDT - 1)),
                reads=sqk + ("onesb",), writes=(k2,), inc=(dt == NDT - 1))
        P.op("dve", lambda v, q=tc, b=b2: v.tensor_copy(S2[:, q * 512:(q + 1) * 512], b[0:1, :]),
             reads=(k2,), writes=r1k)

    inv = 1.0 / D
    P.op("dve", lambda v: v.tensor_scalar_mul(mu, S1, inv), reads=r1k, writes=r1k)
    P.op("dve", lambda v: v.tensor_mul(mu2, mu, mu), reads=r1k, writes=r2k)
    P.op("dve", lambda v: v.scalar_tensor_tensor(var, S2, inv, mu2, ALU.mult, ALU.subtract),
         reads=r1k + r2k, writes=r2k)
    P.op("act", lambda s: s.activation(var, var, AF.Ln, bias=B.epsc[0:1, 0:1]),
         reads=r2k + ("epsc",), writes=r2k)
    P.op("act", lambda s: s.activation(var, var, AF.Exp, scale=-0.5), reads=r2k, writes=r2k)

    for tc in range(8):
        br, krr = B.bank()
        bm, kmm = B.bank()
        P.op("pe", lambda t, q=tc, b=br: t.matmul(
            b[:, :], B.onesrow[0:1, :], var[:, q * 512:(q + 1) * 512], start=True, stop=True),
            reads=r2k + ("onesrow",), writes=(krr,))
        P.op("dve", lambda v, q=tc: v.tensor_copy(bsc, mu[:, q * 512:(q + 1) * 512]),
             reads=r1k, writes=bsck)
        P.op("pe", lambda t, b=bm: t.matmul(
            b[:, :], B.onesrow[0:1, :], bsc, start=True, stop=True),
            reads=bsck + ("onesrow",), writes=(kmm,))
        P.op("act", lambda s, q=tc, b=br: s.activation(rb[:, q * 512:(q + 1) * 512], b[:, :], AF.Identity),
             reads=(krr,), writes=rbk)
        P.op("act", lambda s, q=tc, b=bm: s.activation(mb[:, q * 512:(q + 1) * 512], b[:, :], AF.Identity),
             reads=(kmm,), writes=mbk)
    return (rb, rbk, mb, mbk)


def emit_ln(P, B, eps=1e-5):
    off = LN_OFF
    rboff = off + 10752
    mboff = off
    rb, rbk, mb, mbk = emit_ln_stats(P, B, off, rboff, mboff, eps)
    nsc = B.arb(off + 12800, 1024)
    nsck = B.keys(off + 12800, 512)
    for dt in range(NDT):
        for tc in range(8):
            j = (dt * 8 + tc) % 2
            P.op("dve", lambda v, d=dt, q=tc, jj=j: v.tensor_sub(
                nsc[:, jj * 512:jj * 512 + 512], ssl(B, d, q * 512, 512),
                mb[:, q * 512:(q + 1) * 512]),
                reads=(f"st{dt}",) + mbk, writes=nsck)
            P.op("dve", lambda v, d=dt, q=tc, jj=j: v.tensor_mul(
                ssl(B, d, q * 512, 512), nsc[:, jj * 512:jj * 512 + 512],
                rb[:, q * 512:(q + 1) * 512]),
                reads=nsck + rbk, writes=(f"st{dt}",))


def emit_l0_attn(P, B, aps):
    ATTOUT, QK, VT, V, E, RROW, RB, WH, WP = (0, 16384, 20480, 22528, 23584,
                                              24096, 24608, 24864, 26400)
    attout = B.arb(ATTOUT, 32768)
    atk = lambda pair: B.keys(ATTOUT + pair * 2048, 2048)
    qk = B.arb(QK, 8192, 0, 64)
    qkk = B.keys(QK, 4096)
    vt = B.arb(VT, 4096, 0, 64)
    vtk = B.keys(VT, 2048)
    vre = B.arb(V, 2112)
    vk = B.keys(V, 1056)
    ebuf = [B.arb(E + i * 256, 512) for i in range(2)]
    ek = [B.keys(E + i * 256, 256) for i in range(2)]
    rrow = B.ar(RROW, 512, 0, 1)
    rrk = B.keys(RROW, 512)
    rbuf = B.arb(RB, 512, 0, 64)
    rbk = B.keys(RB, 256)
    wh = [B.arb(WH + i * 768, 1536) for i in range(2)]
    whk = [B.keys(WH + i * 768, 768) for i in range(2)]
    wp = B.arb(WP, 8192)
    wpk = B.keys(WP, 4096)

    chm = P.chan("l0a_misc", "pool")
    P.dma(chm, B.bias[:, 0:24], aps["l0_bqkv"], writes=("bias",))
    P.dma(chm, B.bias[:, 24:32], aps["l0_bproj"], writes=("bias",))
    P.dma(chm, wp.rearrange("p (ct o) -> p ct o", o=1024),
          aps["l0_wproj"].rearrange("(ct p) o -> p ct o", p=128), writes=wpk)
    wqkv = aps["l0_wqkv"]

    def load_head_w(h):
        dst = wh[h % 2]
        for comp in range(3):
            src = bass.AP(wqkv.tensor, comp * 1024 + h * 64,
                          [[3072, 128], [3072 * 128, 8], [1, 64]])
            P.dma(P.chan(f"l0a_w{h % 2}", "pool"),
                  dst.rearrange("p (ct c) -> p ct c", c=192)[:, :, comp * 64:(comp + 1) * 64],
                  src, writes=whk[h % 2])

    for tt in range(32):
        col = tt * 66 + 64
        P.op("dve", lambda v, c=col: v.memset(B.arb(V + c // 2, 1), 1.0),
             writes=vk, inc=True)

    load_head_w(0)
    for h in range(16):
        if h + 1 < 16:
            load_head_w(h + 1)
        w = wh[h % 2]
        wk = whk[h % 2]
        bq = B.bias[(h % 2) * 64:(h % 2) * 64 + 64, h // 2:h // 2 + 1]
        bk = B.bias[(h % 2) * 64:(h % 2) * 64 + 64, 8 + h // 2:8 + h // 2 + 1]
        bv = B.bias[(h % 2) * 64:(h % 2) * 64 + 64, 16 + h // 2:16 + h // 2 + 1]
        for comp, dstc, bias_ap in ((0, 0, bq), (1, 4096, bk), (2, None, bv)):
            for tc in range(8):
                pb, pk = B.bank()
                for ct in range(8):
                    P.op("pe", lambda t, c=ct, q=tc, b=pb, cp=comp: t.matmul(
                        b[0:64, :], w.rearrange("p (ct c) -> p ct c", c=192)[:, c, cp * 64:(cp + 1) * 64],
                        ssl(B, c, q * 512, 512), start=(c == 0), stop=(c == 7)),
                        reads=(f"st{ct}",) + wk, writes=(pk,), inc=(ct == 7))
                if comp < 2:
                    P.op("act", lambda s, q=tc, b=pb, dc=dstc, ba=bias_ap: s.activation(
                        qk[:, dc + q * 512:dc + q * 512 + 512], b[0:64, :], AF.Identity, bias=ba),
                        reads=(pk, "bias"), writes=qkk)
                else:
                    P.op("act", lambda s, q=tc, b=pb, ba=bias_ap: s.activation(
                        vt[:, q * 512:q * 512 + 512], b[0:64, :], AF.Identity, bias=ba),
                        reads=(pk, "bias"), writes=vtk)
        for tt in range(32):
            pj = tt % 2
            pst = B.pst[pj]
            P.op("pe", lambda t, q=tt, p=pst: t.transpose(
                p[:, 0:64], vt[:, q * 128:(q + 1) * 128], B.idn[0:64, 0:64]),
                reads=vtk + ("idn",), writes=(f"pstb{pj}",))
            P.op("dve", lambda v, q=tt, p=pst: v.tensor_copy(
                B.arb(V + q * 33, 64), p[:, 0:64]),
                reads=(f"pstb{pj}",), writes=vk)
        pair = h // 2
        prow = (h % 2) * 64
        for tc in range(8):
            nb, nk = B.bank()
            for kt in range(32):
                sb, sk = B.bank()
                if sk == nk:
                    sb, sk = B.bank()
                P.op("pe", lambda t, k_=kt, q=tc, b=sb: t.matmul(
                    b[:, :], qk[:, 4096 + k_ * 128:4096 + k_ * 128 + 128],
                    qk[:, q * 512:q * 512 + 512],
                    start=True, stop=True),
                    reads=qkk, writes=(sk,))
                P.op("act", lambda s, k_=kt, b=sb: s.activation(
                    ebuf[k_ % 2], b[:, :], AF.Exp, scale=0.125),
                    reads=(sk,), writes=ek[kt % 2])
                P.op("pe", lambda t, k_=kt, b=nb: t.matmul(
                    b[0:65, :], B.arb(V + k_ * 33, 65), ebuf[k_ % 2],
                    start=(k_ == 0), stop=(k_ == 31)),
                    reads=ek[kt % 2] + vk, writes=(nk,), inc=(kt == 31))
            P.op("dve", lambda v, b=nb: v.reciprocal(rrow, b[64:65, :]), reads=(nk,), writes=rrk)
            bb, bbk = B.bank()
            P.op("pe", lambda t, b=bb: t.matmul(
                b[0:64, :], B.onesrow[0:1, 0:64], rrow, start=True, stop=True),
                reads=rrk + ("onesrow",), writes=(bbk,))
            P.op("act", lambda s, b=bb: s.activation(rbuf, b[0:64, :], AF.Copy),
                 reads=(bbk,), writes=rbk)
            P.op("dve", lambda v, q=tc, b=nb: v.tensor_mul(
                B.arb(ATTOUT + pair * 2048 + q * 256, 512, prow, 64),
                b[0:64, :], rbuf),
                reads=(nk,) + rbk, writes=atk(pair))
    for tc in range(8):
        for oc in range(8):
            pb, pk = B.bank()
            for ct in range(8):
                P.op("pe", lambda t, c=ct, o=oc, q=tc, b=pb: t.matmul(
                    b[:, :], wp[:, c * 1024 + o * 128:c * 1024 + o * 128 + 128],
                    B.arb(ATTOUT + c * 2048 + q * 256, 512), start=(c == 0), stop=(c == 7)),
                    reads=atk(ct) + wpk, writes=(pk,), inc=(ct == 7))
            P.op("dve", lambda v, o=oc, q=tc, b=pb: v.scalar_tensor_tensor(
                ssl(B, o, q * 512, 512), b[:, :], B.bias[:, 24 + o:25 + o],
                ssl(B, o, q * 512, 512), ALU.add, ALU.add),
                reads=(pk, "bias", f"st{oc}"), writes=(f"st{oc}",))


def emit_l0_ffn(P, B, aps):
    A_, K1, FACC, M, ZST = 0, 128, 160, 672, 4768
    a_r = B.ar(A_, 128)
    ak = B.keys(A_, 128)
    k1 = B.arb(K1, 64)
    k1k = B.keys(K1, 32)
    facc = B.ar(FACC, 512)
    fk = B.keys(FACC, 512)
    m = B.arb(M, 8192)
    mk = lambda ct: B.keys(M + ct * 512, 512)
    mkall = B.keys(M, 4096)
    zst = B.arb(ZST, 4096)
    zk = B.keys(ZST, 2048)

    chw = P.chan("l0f_misc", "pool")
    P.dma(chw, B.bias[:, 32:40], aps["l0_fbias"], writes=("bias",))
    g1, g2 = aps["l0_g1"], aps["l0_g2"]

    for var in range(2):
        for d in range(8):
            P.dma(chw, k1[d * 16:(d + 1) * 16, :].rearrange("g (c e) -> g c e", e=16),
                  g2[var].rearrange("c g e -> g c e"), writes=k1k)
        for pt in range(4):
            for d in range(8):
                P.dma(chw, a_r[d * 16:(d + 1) * 16, :].rearrange("g (c b) -> g c b", b=32),
                      g1[var][:, pt * 8 + d, :].unsqueeze(0).broadcast_to([16, 4, 32]),
                      writes=ak)
            for b_ in range(32):
                for c_ in range(4):
                    if c_ == 0:
                        P.op("dve", lambda v, b2=b_, c2=c_: v.tensor_scalar(
                            facc[:, b2 * 16:(b2 + 1) * 16], k1[:, c2 * 16:(c2 + 1) * 16],
                            a_r[:, c2 * 32 + b2:c2 * 32 + b2 + 1], None, ALU.mult),
                            reads=ak + k1k, writes=fk)
                    else:
                        P.op("dve", lambda v, b2=b_, c2=c_: v.scalar_tensor_tensor(
                            facc[:, b2 * 16:(b2 + 1) * 16], k1[:, c2 * 16:(c2 + 1) * 16],
                            a_r[:, c2 * 32 + b2:c2 * 32 + b2 + 1],
                            facc[:, b2 * 16:(b2 + 1) * 16], ALU.mult, ALU.add),
                            reads=ak + k1k + fk, writes=fk)
            if var == 0:
                P.op("dve", lambda v, p2=pt: v.tensor_copy(
                    m[:, p2 * 1024:p2 * 1024 + 512], facc), reads=fk, writes=mk(p2 if False else pt))
                P.op("dve", lambda v, p2=pt: v.tensor_copy(
                    m[:, (4 + p2) * 1024 + 512:(4 + p2) * 1024 + 1024], facc),
                    reads=fk, writes=mk(4 + pt))
            else:
                P.op("dve", lambda v, p2=pt: v.tensor_copy(
                    m[:, p2 * 1024 + 512:p2 * 1024 + 1024], facc),
                    reads=fk, writes=mk(pt))
                P.op("dve", lambda v, p2=pt: v.tensor_scalar_mul(
                    m[:, (4 + p2) * 1024:(4 + p2) * 1024 + 512], facc, -1.0),
                    reads=fk, writes=mk(4 + pt))
    for tc in range(8):
        for oc in range(8):
            pb, pk = B.bank()
            for ct in range(8):
                P.op("pe", lambda t, c=ct, o=oc, q=tc, b=pb: t.matmul(
                    b[:, :], m[:, c * 1024 + o * 128:c * 1024 + o * 128 + 128],
                    ssl(B, c, q * 512, 512), start=(c == 0), stop=(c == 7)),
                    reads=(f"st{ct}",) + mkall, writes=(pk,), inc=(ct == 7))
            P.op("dve", lambda v, o=oc, q=tc, b=pb: v.scalar_tensor_tensor(
                zst[:, o * 512:(o + 1) * 512], b[:, :], B.bias[:, 32 + o:33 + o],
                ssl(B, o, q * 512, 512), ALU.add, ALU.add),
                reads=(pk, "bias", f"st{oc}"), writes=zk)
        for oc in range(8):
            P.op("dve", lambda v, o=oc, q=tc: v.tensor_copy(
                ssl(B, o, q * 512, 512), zst[:, o * 512:(o + 1) * 512]),
                reads=zk, writes=(f"st{oc}",))


def emit_l1_attn(P, B, aps):
    ATTOUT, QT, KT, VT, G_, WH, WO = 0, 16384, 18432, 20480, 22528, 22592, 25664
    atk = lambda h: B.keys(ATTOUT + h * 2048, 2048)
    qt = B.arb(QT, 4096)
    qtk = B.keys(QT, 2048)
    kt = B.arb(KT, 4096)
    ktk = B.keys(KT, 2048)
    vt = B.arb(VT, 4096)
    vtk = B.keys(VT, 2048)
    gb = B.arb(G_, 128)
    gk = B.keys(G_, 64)
    wh = [B.arb(WH + i * 1536, 3072) for i in range(2)]
    whk = [B.keys(WH + i * 1536, 1536) for i in range(2)]
    wo = B.arb(WO, 8192)
    wok = B.keys(WO, 4096)
    BR = WO + 4096
    brow = [B.arb(BR + i * 128, 256, 0, 1) for i in range(2)]
    browk = [B.keys(BR + i * 128, 128) for i in range(2)]

    chm = P.chan("l1a_misc", "pool")
    P.dma(chm, B.bias[:, 0:24], aps["l1_bqkv"], writes=("bias",))
    P.dma(chm, B.bias[:, 24:32], aps["l1_bo"], writes=("bias",))
    P.dma(chm, wo.rearrange("p (ct o) -> p ct o", o=1024),
          aps["l1_wo"].rearrange("(ct p) o -> p ct o", p=128), writes=wok)
    wqkv = aps["l1_wqkv"]

    def load_head_w(h):
        dst = wh[h % 2]
        for comp in range(3):
            src = bass.AP(wqkv.tensor, comp * 1024 + h * 128,
                          [[3072, 128], [3072 * 128, 8], [1, 128]])
            P.dma(P.chan(f"l1a_w{h % 2}", "pool"),
                  dst.rearrange("p (ct c) -> p ct c", c=384)[:, :, comp * 128:(comp + 1) * 128],
                  src, writes=whk[h % 2])
        P.dma(P.chan(f"l1a_w{h % 2}", "pool"), B.arb(BR + (h % 2) * 128, 128, 0, 1),
              bass.AP(aps["l1_bqkv_row"].tensor, 1024 + h * 128, [[1, 1], [1, 128]]),
              writes=browk[h % 2])
        P.dma(P.chan(f"l1a_w{h % 2}", "pool"), B.arb(BR + (h % 2) * 128 + 64, 128, 0, 1),
              bass.AP(aps["l1_bqkv_row"].tensor, 2048 + h * 128, [[1, 1], [1, 128]]),
              writes=browk[h % 2])

    load_head_w(0)
    for h in range(8):
        if h + 1 < 8:
            load_head_w(h + 1)
        w = wh[h % 2]
        wk = whk[h % 2]
        bq = B.bias[:, h:h + 1]
        bk = B.bias[:, 8 + h:8 + h + 1]
        bv = B.bias[:, 16 + h:16 + h + 1]
        for tc in range(8):
            pb, pk = B.bank()
            for ct in range(8):
                P.op("pe", lambda t, c=ct, q=tc, b=pb: t.matmul(
                    b[:, :], w.rearrange("p (ct c) -> p ct c", c=384)[:, c, 0:128],
                    ssl(B, c, q * 512, 512), start=(c == 0), stop=(c == 7)),
                    reads=(f"st{ct}",) + wk, writes=(pk,), inc=(ct == 7))
            P.op("act", lambda s, q=tc, b=pb: s.activation(
                qt[:, q * 512:q * 512 + 512], b[:, :], AF.Identity, bias=bq),
                reads=(pk, "bias"), writes=qtk)
        for tt in range(32):
            pb, pk = B.bank()
            for ct in range(8):
                P.op("pe", lambda t, c=ct, q=tt, b=pb: t.matmul(
                    b[:, 0:256], ssl(B, c, q * 128, 128),
                    w.rearrange("p (ct c) -> p ct c", c=384)[:, c, 128:384],
                    start=(c == 0), stop=False),
                    reads=(f"st{ct}",) + wk, writes=(pk,), inc=False)
            P.op("pe", lambda t, b=pb, hh=h: t.matmul(
                b[:, 0:256], B.onesbrow[0:1, :], brow[hh % 2], start=False, stop=True),
                reads=("onesbrow",) + browk[h % 2], writes=(pk,), inc=True)
            P.op("act", lambda s, q=tt, b=pb: s.activation(
                kt[:, q * 128:q * 128 + 128], b[:, 0:128], AF.Identity),
                reads=(pk,), writes=ktk)
            P.op("act", lambda s, q=tt, b=pb: s.activation(
                vt[:, q * 128:q * 128 + 128], b[:, 128:256], AF.Identity),
                reads=(pk,), writes=vtk)
        pb, pkk = B.bank()
        for tt in range(32):
            P.op("pe", lambda t, q=tt, b=pb: t.matmul(
                b[:, 0:128], kt[:, q * 128:q * 128 + 128], vt[:, q * 128:q * 128 + 128],
                start=(q == 0), stop=(q == 31)),
                reads=ktk + vtk, writes=(pkk,), inc=(tt == 31))
        P.op("dve", lambda v, b=pb: v.tensor_copy(gb, b[:, 0:128]), reads=(pkk,), writes=gk)
        for tc in range(8):
            pb2, pk2 = B.bank()
            P.op("pe", lambda t, q=tc, b=pb2: t.matmul(
                b[:, :], gb, qt[:, q * 512:q * 512 + 512], start=True, stop=True),
                reads=gk + qtk, writes=(pk2,))
            P.op("dve", lambda v, q=tc, b=pb2, hh=h: v.tensor_copy(
                B.arb(ATTOUT + hh * 2048 + q * 256, 512), b[:, :]),
                reads=(pk2,), writes=atk(h))
    for tc in range(8):
        for oc in range(8):
            pb, pk = B.bank()
            for ct in range(8):
                P.op("pe", lambda t, c=ct, o=oc, q=tc, b=pb: t.matmul(
                    b[:, :], wo[:, c * 1024 + o * 128:c * 1024 + o * 128 + 128],
                    B.arb(ATTOUT + c * 2048 + q * 256, 512), start=(c == 0), stop=(c == 7)),
                    reads=atk(ct) + wok, writes=(pk,), inc=(ct == 7))
            P.op("dve", lambda v, o=oc, q=tc, b=pb: v.scalar_tensor_tensor(
                ssl(B, o, q * 512, 512), b[:, :], B.bias[:, 24 + o:25 + o],
                ssl(B, o, q * 512, 512), ALU.add, ALU.add),
                reads=(pk, "bias", f"st{oc}"), writes=(f"st{oc}",))


def emit_moe_top2(P, B, logit_lhsT_fn, logit_reads, CWSTG, TOPS, LT, scale9=False):
    lt = B.arb(LT, 4096, 0, 8)
    ltk = B.keys(LT, 2048)
    for tc in range(8):
        pb, pk = B.bank()
        for ct in range(NDT):
            P.op("pe", lambda t, c=ct, q=tc, b=pb: t.matmul(
                b[0:8, :], logit_lhsT_fn(c), ssl(B, c, q * 512, 512),
                start=(c == 0), stop=(c == NDT - 1)),
                reads=(f"st{ct}",) + logit_reads, writes=(pk,), inc=(ct == NDT - 1))
        P.op("act", lambda s, q=tc, b=pb: s.activation(
            lt[:, q * 512:(q + 1) * 512], b[0:8, :], AF.Identity),
            reads=(pk,), writes=ltk)
    return emit_top2_from_lt(P, B, lt, ltk, CWSTG, TOPS)


def emit_top2_from_lt(P, B, lt, ltk, CWSTG, TOPS):
    cwstg = B.arb(CWSTG, 4096, 0, 8)
    cwk = B.keys(CWSTG, 2048)
    tf = B.ar(TOPS, 128)
    tfk = B.keys(TOPS, 128)
    ltok = B.arb(TOPS + 100, 8)
    cwb16 = B.arb(TOPS + 108, 8)
    for tt in range(32):
        j = tt % 2
        pst = B.pst[j]
        P.op("pe", lambda t, q=tt, p=pst: t.transpose(
            p[:, 0:8], lt[:, q * 128:(q + 1) * 128], B.idn[0:8, 0:8]),
            reads=ltk + ("idn",), writes=(f"pstb{j}",))
        P.op("dve", lambda v, p=pst: v.tensor_copy(ltok, p[:, 0:8]),
             reads=(f"pstb{j}",), writes=tfk)
        m1 = tf[:, 8:9]; m2 = tf[:, 9:10]; e2 = tf[:, 10:11]
        den = tf[:, 11:12]; w2 = tf[:, 12:13]
        l2 = tf[:, 0:8]; mask1 = tf[:, 16:24]; cwf = tf[:, 24:32]
        P.op("dve", lambda v: v.reduce_max(m1, ltok, axis=AX.X), reads=tfk, writes=tfk)
        P.op("dve", lambda v: v.tensor_tensor(mask1, ltok, m1.to_broadcast([128, 8]), ALU.is_equal),
             reads=tfk, writes=tfk)
        P.op("dve", lambda v: v.scalar_tensor_tensor(l2, mask1, -1e30, ltok, ALU.mult, ALU.add),
             reads=tfk, writes=tfk)
        P.op("dve", lambda v: v.reduce_max(m2, l2, axis=AX.X), reads=tfk, writes=tfk)
        P.op("dve", lambda v: v.tensor_sub(e2, m2, m1), reads=tfk, writes=tfk)
        P.op("act", lambda s: s.activation(e2, e2, AF.Exp), reads=tfk, writes=tfk)
        P.op("dve", lambda v: v.tensor_scalar_add(den, e2, 1.0), reads=tfk, writes=tfk)
        P.op("dve", lambda v: v.reciprocal(den, den), reads=tfk, writes=tfk)
        P.op("dve", lambda v: v.tensor_mul(w2, e2, den), reads=tfk, writes=tfk)
        P.op("dve", lambda v: v.tensor_scalar(cwf, mask1, den, None, ALU.mult),
             reads=tfk, writes=tfk)
        P.op("dve", lambda v: v.tensor_tensor(l2, l2, m2.to_broadcast([128, 8]), ALU.is_equal),
             reads=tfk, writes=tfk)
        P.op("dve", lambda v: v.tensor_scalar(l2, l2, w2, None, ALU.mult),
             reads=tfk, writes=tfk)
        P.op("dve", lambda v: v.tensor_add(cwf, cwf, l2), reads=tfk, writes=tfk)
        P.op("dve", lambda v: v.tensor_copy(cwb16, cwf), reads=tfk, writes=tfk)
        pst2 = B.pst[(tt + 1) % 2]
        P.op("pe", lambda t, p=pst2: t.transpose(p[0:8, 128:256], cwb16, B.idn[:]),
             reads=tfk + ("idn",), writes=(f"pstb{(tt + 1) % 2}",))
        P.op("dve", lambda v, q=tt, p=pst2: v.tensor_copy(
            cwstg[:, q * 128:(q + 1) * 128], p[0:8, 128:256]),
            reads=(f"pstb{(tt + 1) % 2}",), writes=cwk)
    return cwstg, cwk


def emit_l1_ffn(P, B, aps):
    Y, H_, W12, W3F, CWSTG, CWB, BAS, TOPS = (0, 8192, 16384, 20480, 28672,
                                              30720, 31232, 31744)
    yk = lambda oc: B.keys(Y + oc * 1024, 1024)
    ykall = B.keys(Y, 8192)
    hk = B.keys(H_, 8192)
    w12 = [(B.arb(W12 + i * 2048, 2048), B.arb(W12 + i * 2048 + 1024, 2048)) for i in range(2)]
    w12k = [B.keys(W12 + i * 2048, 2048) for i in range(2)]
    w3 = B.arb(W3F, 16384)
    w3k = B.keys(W3F, 8192)
    cwb = B.arb(CWB, 1024)
    cwbk = B.keys(CWB, 512)
    bas = B.arb(BAS, 1024, 0, 8)
    bask = B.keys(BAS, 512)

    chm = P.chan("l1f_misc", "pool")
    P.dma(chm, bas, aps["bas8"][0:8, :], writes=bask)
    wg = aps["l1_wg"]

    wgs = B.arb(TOPS + 128, 64)
    wgk = B.keys(TOPS + 128, 32)
    P.dma(chm, wgs.rearrange("p (ct c) -> p ct c", c=8),
          wg.rearrange("(ct p) c -> p ct c", p=128), writes=wgk)
    cwstg, cwk = emit_moe_top2(
        P, B, lambda ct: wgs[:, ct * 8:(ct + 1) * 8], wgk, CWSTG, TOPS, Y)

    emit_moe_mlp(P, B, "l1f", aps["l1_w1"], aps["l1_w2"], aps["l1_w3"], 8,
                 cwstg, cwk, (Y, H_, W12, W3F, CWB), bas, bask)


def emit_moe_mlp(P, B, tag, W1, W2, W3, nE, cwstg, cwk, offs, bas, bask):
    Y, H_, W12, W3F, CWB = offs
    yk = lambda oc: B.keys(Y + oc * 1024, 1024)
    ykall = B.keys(Y, 8192)
    hk = B.keys(H_, 8192)
    w12 = [(B.arb(W12 + i * 2048, 2048), B.arb(W12 + i * 2048 + 1024, 2048)) for i in range(2)]
    w12k = [B.keys(W12 + i * 2048, 2048) for i in range(2)]
    w3 = B.arb(W3F, 16384)
    w3k = B.keys(W3F, 8192)
    cwb = B.arb(CWB, 1024)
    cwbk = B.keys(CWB, 512)

    def load_w12(e, hc):
        par = hc % 2
        dsts = w12[par]
        for wi, W in ((0, W1), (1, W2)):
            src = bass.AP(W.tensor, e * 1024 * 2048 + hc * 256,
                          [[2048, 128], [2048 * 128, 8], [1, 256]])
            P.dma(P.chan(f"{tag}_w{par}", "pool"),
                  dsts[wi].rearrange("p (ct c) -> p ct c", c=256), src,
                  writes=w12k[par])

    def load_w3(e):
        src = bass.AP(W3.tensor, e * 2048 * 1024,
                      [[1024, 128], [1024 * 128, 16], [1, 1024]])
        P.dma(P.chan(f"{tag}_w3", "pool"),
              w3.rearrange("p (ht c) -> p ht c", c=1024), src, writes=w3k)

    for q4 in range(4):
        t0 = q4 * 1024
        for e in range(nE):
            shared = (e == nE - 1 and nE == 9)
            if not shared:
                for tc in range(2):
                    pb, pk = B.bank()
                    P.op("pe", lambda t, q=tc, b=pb, ee=e: t.matmul(
                        b[:, :], bas[:, ee * 128:(ee + 1) * 128],
                        cwstg[:, t0 + q * 512:t0 + q * 512 + 512], start=True, stop=True),
                        reads=bask + cwk, writes=(pk,))
                    P.op("act", lambda s, q=tc, b=pb: s.activation(
                        cwb[:, q * 512:(q + 1) * 512], b[:, :], AF.Identity),
                        reads=(pk,), writes=cwbk)
            load_w3(e)
            for hc in range(8):
                load_w12(e, hc)
                par = hc % 2
                for ot in range(2):
                    ht = hc * 2 + ot
                    for tc in range(2):
                        pbu, pku = B.bank()
                        for ct in range(8):
                            P.op("pe", lambda t, c=ct, o=ot, q=tc, b=pbu: t.matmul(
                                b[:, :], w12[par][0].rearrange("p (ct c) -> p ct c", c=256)[:, c, o * 128:(o + 1) * 128],
                                ssl(B, c, t0 + q * 512, 512), start=(c == 0), stop=(c == 7)),
                                reads=(f"st{ct}",) + w12k[par], writes=(pku,), inc=(ct == 7))
                        P.op("act", lambda s, b=pbu, h2=ht, q=tc: s.activation(
                            B.arb(H_ + h2 * 512 + q * 256, 512), b[:, :], AF.Silu),
                            reads=(pku,), writes=hk)
                        pbv, pkv = B.bank()
                        for ct in range(8):
                            P.op("pe", lambda t, c=ct, o=ot, q=tc, b=pbv: t.matmul(
                                b[:, :], w12[par][1].rearrange("p (ct c) -> p ct c", c=256)[:, c, o * 128:(o + 1) * 128],
                                ssl(B, c, t0 + q * 512, 512), start=(c == 0), stop=(c == 7)),
                                reads=(f"st{ct}",) + w12k[par], writes=(pkv,), inc=(ct == 7))
                        P.op("dve", lambda v, b=pbv, h2=ht, q=tc: v.tensor_mul(
                            B.arb(H_ + h2 * 512 + q * 256, 512),
                            B.arb(H_ + h2 * 512 + q * 256, 512), b[:, :]),
                            reads=(pkv,) + hk, writes=hk)
                        if not shared:
                            P.op("dve", lambda v, h2=ht, q=tc: v.tensor_mul(
                                B.arb(H_ + h2 * 512 + q * 256, 512),
                                B.arb(H_ + h2 * 512 + q * 256, 512),
                                cwb[:, q * 512:(q + 1) * 512]),
                                reads=hk + cwbk, writes=hk)
            for tch in range(4):
                for oc in range(8):
                    pb = B.ps[oc // 2]
                    pk = f"psb{oc // 2}"
                    sub = (oc % 2) * 256
                    for ht in range(16):
                        P.op("pe", lambda t, h2=ht, o=oc, q=tch, b=pb, sb=sub: t.matmul(
                            b[:, sb:sb + 256],
                            w3.rearrange("p (ht c) -> p ht c", c=1024)[:, h2, o * 128:(o + 1) * 128],
                            B.arb(H_ + h2 * 512 + q * 128, 256), start=(h2 == 0), stop=(h2 == 15)),
                            reads=hk + w3k, writes=(pk,), inc=(ht == 15))
                    if e == 0:
                        P.op("dve", lambda v, o=oc, q=tch, b=pb, sb=sub: v.tensor_copy(
                            B.ar(Y + o * 1024 + q * 256, 256), b[:, sb:sb + 256]),
                            reads=(pk,), writes=yk(oc))
                    else:
                        P.op("dve", lambda v, o=oc, q=tch, b=pb, sb=sub: v.tensor_add(
                            B.ar(Y + o * 1024 + q * 256, 256),
                            B.ar(Y + o * 1024 + q * 256, 256), b[:, sb:sb + 256]),
                            reads=(pk,) + yk(oc), writes=yk(oc))
        for oc in range(8):
            for tc in range(2):
                P.op("dve", lambda v, o=oc, q=tc: v.tensor_add(
                    ssl(B, o, t0 + q * 512, 512), ssl(B, o, t0 + q * 512, 512),
                    B.ar(Y + o * 1024 + q * 512, 512)),
                    reads=ykall + (f"st{oc}",), writes=(f"st{oc}",))


def emit_l2_attn(P, B, aps, spills):
    XN, WT, QT, KT = 0, 16384, 17408, 19456
    TAB1, TAB2 = 21504, 23552
    UBUF, TOKS = 25600, 27648
    RB, MB = 29696, 31744
    BSC2 = 33792
    BT, OWT, KERNT = 0, 8064, 21504
    V0, V1, UTOK, ZR, ZT, BIDX = 23552, 24576, 25600, 26624, 27648, 28928

    xn = B.arb(XN, 32768)
    xnk = lambda dt: B.keys(XN + dt * 2048, 2048)
    xnka = B.keys(XN, 16384)
    wt = [B.arb(WT + i * 512, 1024) for i in range(2)]
    wtk = [B.keys(WT + i * 512, 512) for i in range(2)]
    qt = B.arb(QT, 4096)
    qtk = B.keys(QT, 2048)
    kt = B.arb(KT, 4096)
    ktk = B.keys(KT, 2048)
    tab1 = B.arb(TAB1, 4096)
    t1k = B.keys(TAB1, 2048)
    tab2 = B.arb(TAB2, 4096)
    t2k = B.keys(TAB2, 2048)
    ubuf = B.arb(UBUF, 4096)
    ubk = B.keys(UBUF, 2048)
    toks = B.arb(TOKS, 4096)
    tok = B.keys(TOKS, 2048)

    chm = P.chan("l2a_misc", "pool")
    P.dma(chm, B.bias[:, 0:8], aps["l2_lnw"], writes=("bias",))
    P.dma(chm, B.bias[:, 8:16], aps["l2_lnb"], writes=("bias",))
    P.dma(chm, B.bias[:, 16:20], aps["l2_gb"], writes=("bias",))
    P.dma(chm, B.bias[:, 20:28], aps["l2_ob"], writes=("bias",))
    P.dma(chm, B.bias[:, 28:61], aps["l2_uvb"], writes=("bias",))
    P.dma(chm, tab1, aps["l2_tab1"], writes=t1k)
    P.dma(chm, tab2, aps["l2_tab2s"], writes=t2k)

    rb, rbk, mb, mbk = emit_ln_stats(P, B, 0, RB, MB)

    nsc = B.arb(BSC2, 512)
    nsck = B.keys(BSC2, 256)
    for dt in range(NDT):
        for tc in range(8):
            P.op("dve", lambda v, d=dt, q=tc: v.tensor_sub(
                nsc, ssl(B, d, q * 512, 512), mb[:, q * 512:(q + 1) * 512]),
                reads=(f"st{dt}",) + mbk, writes=nsck)
            P.op("dve", lambda v, d=dt, q=tc: v.tensor_mul(
                nsc, nsc, rb[:, q * 512:(q + 1) * 512]),
                reads=nsck + rbk, writes=nsck)
            P.op("dve", lambda v, d=dt, q=tc: v.tensor_scalar(
                xn[:, d * 4096 + q * 512:d * 4096 + q * 512 + 512], nsc,
                B.bias[:, d:d + 1], B.bias[:, 8 + d:9 + d], ALU.mult, ALU.add),
                reads=nsck + ("bias",), writes=xnk(dt)),
    uvw = aps["l2_uvw"]

    def load_wtile(oc, par):
        src = bass.AP(uvw.tensor, oc * 128,
                      [[4224, 128], [4224 * 128, 8], [1, 128]])
        P.dma(P.chan(f"l2a_w{par}", "pool"),
              wt[par].rearrange("p (ct c) -> p ct c", c=128), src, writes=wtk[par])

    uspill, vspill = spills
    load_wtile(0, 0)
    for ui, spill in ((0, uspill), (1, vspill)):
        for oc in range(16):
            gl = ui * 16 + oc
            par = gl % 2
            nxt = gl + 1
            if nxt < 32:
                load_wtile(nxt, nxt % 2)
            bias_col = B.bias[:, 28 + ui * 16 + oc:29 + ui * 16 + oc]
            for tc in range(8):
                pb, pk = B.bank()
                for ct in range(8):
                    P.op("pe", lambda t, c=ct, q=tc, b=pb, pr=par: t.matmul(
                        b[:, :], wt[pr].rearrange("p (ct c) -> p ct c", c=128)[:, c, :],
                        xn[:, c * 4096 + q * 512:c * 4096 + q * 512 + 512],
                        start=(c == 0), stop=(c == 7)),
                        reads=xnk(ct) + wtk[par], writes=(pk,), inc=(ct == 7))
                P.op("act", lambda s, q=tc, b=pb, bc=bias_col: s.activation(
                    ubuf[:, q * 512:(q + 1) * 512], b[:, :], AF.Silu, bias=bc),
                    reads=(pk, "bias"), writes=ubk)
            for tt in range(32):
                j = tt % 2
                pst = B.pst[j]
                P.op("pe", lambda t, q=tt, p=pst: t.transpose(
                    p[:, 0:128], ubuf[:, q * 128:(q + 1) * 128], B.idn[:]),
                    reads=ubk + ("idn",), writes=(f"pstb{j}",))
                P.op("dve", lambda v, q=tt, p=pst: v.tensor_copy(
                    toks[:, q * 128:(q + 1) * 128], p[:, 0:128]),
                    reads=(f"pstb{j}",), writes=tok)
            P.dma(P.chan(f"l2a_sp{par}", "pool"),
                  bass.AP(spill.tensor, oc * 128, [[2048, 128], [2048 * 128, 32], [1, 128]]),
                  toks.rearrange("p (tt c) -> p tt c", c=128), reads=tok)
    load_wtile(32, 0)
    bch = B.arb(BSC2, 512)
    bchk = B.keys(BSC2, 256)
    pre = B.arb(BSC2 + 256, 512)
    prek = B.keys(BSC2 + 256, 256)
    sw = B.arb(BSC2 + 512, 512)
    swk = B.keys(BSC2 + 512, 256)
    ta = B.arb(BSC2 + 768, 512)
    tak = B.keys(BSC2 + 768, 256)
    for tc in range(8):
        pb, pk = B.bank()
        for ct in range(8):
            P.op("pe", lambda t, c=ct, q=tc, b=pb: t.matmul(
                b[:, :], wt[0].rearrange("p (ct c) -> p ct c", c=128)[:, c, :],
                xn[:, c * 4096 + q * 512:c * 4096 + q * 512 + 512],
                start=(c == 0), stop=(c == 7)),
                reads=xnk(ct) + wtk[0], writes=(pk,), inc=(ct == 7))
        P.op("act", lambda s, b=pb: s.activation(
            bch, b[:, :], AF.Silu, bias=B.bias[:, 60:61]),
            reads=(pk, "bias"), writes=bchk)
        for br, dst, dk in ((0, qt, qtk), (1, kt, ktk)):
            gcol = B.bias[:, 16 + 2 * br:17 + 2 * br]
            bcol = B.bias[:, 17 + 2 * br:18 + 2 * br]
            P.op("dve", lambda v, g2=gcol, b2=bcol: v.tensor_scalar(
                pre, bch, g2, b2, ALU.mult, ALU.add),
                reads=bchk + ("bias",), writes=prek)
            P.op("dve", lambda v: v.tensor_copy(sw[0:64, :], pre[64:128, :]),
                 reads=prek, writes=swk)
            P.op("dve", lambda v: v.tensor_copy(sw[64:128, :], pre[0:64, :]),
                 reads=prek, writes=swk)
            P.op("dve", lambda v, q=tc: v.tensor_mul(
                ta, pre, tab1[:, q * 512:(q + 1) * 512]),
                reads=prek + t1k, writes=tak)
            P.op("dve", lambda v, q=tc: v.tensor_mul(
                sw, sw, tab2[:, q * 512:(q + 1) * 512]),
                reads=swk + t2k, writes=swk)
            P.op("dve", lambda v, d2=dst, q=tc: v.tensor_add(
                d2[:, q * 512:(q + 1) * 512], ta, sw),
                reads=tak + swk, writes=dk)
    bidx = B.idx32[:, 0:63]
    bxk = ("idx32",)
    P.dma(chm, bidx, aps["l2_bidx"], writes=bxk)
    btk = B.keys(BT, 8064)
    wxr = aps["l2_wxr"].rearrange("(a b) -> a b", b=1)
    for dd in range(63):
        P.idma(P.chan("l2a_bt", "pool"), B.ar(BT + dd * 128, 128), None,
               wxr, bass.IndirectOffsetOnAxis(ap=bidx[:, dd:dd + 1], axis=0),
               reads=bxk, writes=btk)
    owt = B.arb(OWT, 16384)
    owk = B.keys(OWT, 8192)
    P.dma(chm, owt.rearrange("p (et o) -> p et o", o=1024),
          aps["l2_ow"].rearrange("(et p) o -> p et o", p=128), writes=owk)
    kern = B.arb(KERNT, 4096)
    kkk = B.keys(KERNT, 2048)
    vb = [B.arb(V0, 2048), B.arb(V1, 2048)]
    vbk = [B.keys(V0, 1024), B.keys(V1, 1024)]
    ut = B.arb(UTOK, 2048)
    utk = B.keys(UTOK, 1024)
    zr = B.arb(ZR, 2048)
    zrk = B.keys(ZR, 1024)
    zt = B.arb(ZT, 2048)
    ztk = B.keys(ZT, 1024)
    ksc = B.ar(ZR, 128)
    ksck = zrk
    SC = float(1.0 / np.sqrt(128.0))
    for it in range(32):
        P.dma(P.chan("l2a_ut", "pool"), ut, uspill[it * 128:(it + 1) * 128, :],
              reads=(), writes=utk)
        for jt in range(32):
            dd = jt - it + 31
            pb, pk = B.bank()
            P.op("pe", lambda t, j=jt, i2=it, b=pb: t.matmul(
                b[:, 0:128], kt[:, j * 128:(j + 1) * 128], qt[:, i2 * 128:(i2 + 1) * 128],
                start=True, stop=True),
                reads=qtk + ktk, writes=(pk,))
            P.op("dve", lambda v, b=pb, d2=dd: v.scalar_tensor_tensor(
                ksc, b[:, 0:128], SC, B.ar(BT + d2 * 128, 128), ALU.mult, ALU.add),
                reads=(pk,) + btk, writes=ksck)
            P.op("dve", lambda v: v.tensor_scalar_max(ksc, ksc, 0.0),
                 reads=ksck, writes=ksck)
            P.op("dve", lambda v, j=jt: v.tensor_mul(
                kern[:, j * 128:(j + 1) * 128], ksc, ksc),
                reads=ksck, writes=kkk)
        for jt in range(32):
            par = jt % 2
            P.dma(P.chan(f"l2a_v{par}", "pool"), vb[par],
                  vspill[jt * 128:(jt + 1) * 128, :], writes=vbk[par])
            for ec in range(4):
                pbank = B.ps[ec]
                P.op("pe", lambda t, j=jt, e=ec, b=pbank, pr=par: t.matmul(
                    b[:, :], kern[:, j * 128:(j + 1) * 128],
                    vb[pr][:, e * 512:(e + 1) * 512],
                    start=(j == 0), stop=(j == 31)),
                    reads=kkk + vbk[par], writes=(f"psb{ec}",), inc=(ec == 3))
        for ec in range(4):
            P.op("dve", lambda v, e=ec: v.tensor_mul(
                zr[:, e * 512:(e + 1) * 512], ut[:, e * 512:(e + 1) * 512], B.ps[e][:, :]),
                reads=(f"psb{ec}", ) + utk, writes=zrk)
        for et in range(16):
            j = et % 2
            pst = B.pst[j]
            P.op("pe", lambda t, e=et, p=pst: t.transpose(
                p[:, 0:128], zr[:, e * 128:(e + 1) * 128], B.idn[:]),
                reads=zrk + ("idn",), writes=(f"pstb{j}",))
            P.op("dve", lambda v, e=et, p=pst: v.tensor_copy(
                zt[:, e * 128:(e + 1) * 128], p[:, 0:128]),
                reads=(f"pstb{j}",), writes=ztk)
        for oc in range(8):
            pb, pk = B.bank()
            for et in range(16):
                P.op("pe", lambda t, e=et, o=oc, b=pb: t.matmul(
                    b[:, 0:128], owt.rearrange("p (et o) -> p et o", o=1024)[:, e, o * 128:(o + 1) * 128],
                    zt[:, e * 128:(e + 1) * 128], start=(e == 0), stop=(e == 15)),
                    reads=ztk + owk, writes=(pk,), inc=(et == 15))
            P.op("dve", lambda v, o=oc, i2=it, b=pb: v.scalar_tensor_tensor(
                ksc, ssl(B, o, i2 * 128, 128), 2.0, b[:, 0:128],
                ALU.mult, ALU.add),
                reads=(pk, f"st{oc}") + ksck, writes=ksck)
            P.op("dve", lambda v, o=oc, i2=it: v.tensor_scalar(
                ssl(B, o, i2 * 128, 128), ksc, B.bias[:, 20 + o:21 + o], None,
                ALU.add),
                reads=ksck + ("bias",), writes=(f"st{oc}",))


def emit_l2_ffn(P, B, aps, xwspill):
    import concourse.mybir as _mb
    HS, WHH, XWB = 0, 16384, 28672
    HW, RZ, SCR = 30208, 30240, 30256
    H0A, H0B, XM, XMB = 30264, 30268, 30272, 30280
    XWROW, WT = 30720, 32768
    CTXW = 0
    hs = B.arb(HS, 32768)
    hsk = B.keys(HS, 16384)
    whh = B.arb(WHH, 24576)
    whhk = B.keys(WHH, 12288)
    xwb = B.arb(XWB, 3072)
    hw = B.ar(HW, 24)
    rz = B.ar(RZ, 16)
    scr = B.ar(SCR, 8)
    h0a = B.arb(H0A, 8)
    h0b = B.arb(H0B, 8)
    xm = B.ar(XM, 8)
    xmb = B.arb(XMB, 8)
    smallk = B.keys(HW, 100)
    xwrow = B.arb(XWROW, 4096)
    xwrk = B.keys(XWROW, 2048)
    wt = [B.arb(WT + i * 512, 1024) for i in range(2)]
    wtk = [B.keys(WT + i * 512, 512) for i in range(2)]

    chm = P.chan("l2f_misc", "pool")
    P.dma(chm, B.bias[:, 0:8], aps["l2_ctxb"], writes=("bias",))
    for dt in range(NDT):
        P.op("dve", lambda v, d=dt: v.reduce_sum(xm[:, d:d + 1], ssl(B, d, 0, T), axis=AX.X),
             reads=(f"st{dt}",), writes=smallk)
    P.op("dve", lambda v: v.tensor_scalar_mul(xmb, xm, 1.0 / T), reads=smallk, writes=smallk)
    ctxw = B.arb(CTXW, 8192)
    ctxwk = B.keys(CTXW, 4096)
    P.dma(chm, ctxw.rearrange("p (ct o) -> p ct o", o=1024),
          aps["l2_ctxw"].rearrange("(ct p) o -> p ct o", p=128), writes=ctxwk)
    pb0, pk0 = B.bank()
    for ot in range(8):
        for ct in range(8):
            P.op("pe", lambda t, o=ot, c=ct, b=pb0: t.matmul(
                b[:, o:o + 1], ctxw[:, c * 1024 + o * 128:c * 1024 + o * 128 + 128],
                xmb[:, c:c + 1], start=(c == 0), stop=(c == 7)),
                reads=ctxwk + smallk, writes=(pk0,), inc=(ot == 7 and ct == 7))
    P.op("dve", lambda v, b=pb0: v.tensor_tensor(h0a, b[:, 0:8], B.bias[:, 0:8], ALU.add),
         reads=(pk0, "bias"), writes=smallk)
    def load_gih(ot, par):
        src = bass.AP(aps["l2_gih"].tensor, ot * 128,
                      [[3072, 128], [3072 * 128, 8], [1, 128]])
        P.dma(P.chan(f"l2f_g{par}", "pool"),
              wt[par].rearrange("p (ct c) -> p ct c", c=128), src, writes=wtk[par])
    load_gih(0, 0)
    for ot in range(24):
        par = ot % 2
        if ot + 1 < 24:
            load_gih(ot + 1, (ot + 1) % 2)
        for tc in range(8):
            pb, pk = B.bank()
            for ct in range(8):
                P.op("pe", lambda t, c=ct, q=tc, b=pb, pr=par: t.matmul(
                    b[:, :], wt[pr].rearrange("p (ct c) -> p ct c", c=128)[:, c, :],
                    ssl(B, c, q * 512, 512), start=(c == 0), stop=(c == 7)),
                    reads=(f"st{ct}",) + wtk[par], writes=(pk,), inc=(ct == 7))
            P.op("act", lambda s, q=tc, b=pb: s.activation(
                xwrow[:, q * 512:(q + 1) * 512], b[:, :], AF.Identity),
                reads=(pk,), writes=xwrk)
        P.dma(P.chan(f"l2f_xw{par}", "sp"),
              bass.AP(xwspill.tensor, ot * 128 * T, [[T, 128], [1, T]]),
              xwrow, reads=xwrk, writes=("xwsp",))
    P.dma(chm, whh.rearrange("p (it o) -> p it o", o=3072),
          aps["l2_ghh"].rearrange("(it p) o -> p it o", p=128), writes=whhk)
    P.flush("pe", reads=whhk + smallk, writes=("psb5",))
    P.flush("dve", reads=smallk, writes=B.keys(HW, 100))
    P.flush("act", reads=B.keys(HW, 100))
    P.flush("pool", reads=("xwsp",))
    nc = P.nc
    gp = P.ctx.enter_context(nc.semaphore("gru_pe"))
    gd = P.ctx.enter_context(nc.semaphore("gru_dv"))
    ga = P.ctx.enter_context(nc.semaphore("gru_ac"))
    chx = P.ctx.enter_context(nc.semaphore("gru_chx"))
    t_e, v_e, s_e, p_e = nc.tensor, nc.vector, nc.scalar, nc.gpsimd
    ET = _mb.EngineType
    psb = B.ps[5]
    hsv = hs.rearrange("p (ot t) -> p ot t", t=T)
    xwv = xwb.rearrange("p (ot ts) -> p ot ts", ts=128)
    hbufs = [h0a, h0b]
    with nc.Fori(0, 32, engines=(ET.PE, ET.DVE, ET.Activation, ET.Pool)) as io:
        p_e.wait_ge(gd, io * 1024)
        p_e.dma_start(
            xwb.rearrange("p (ot ts) -> p ot ts", ts=128),
            bass.AP(xwspill.tensor, io * 128,
                    [[T, 128], [128 * T, 24], [1, 128]])).then_inc(chx, 16)
        t_e.wait_ge(chx, (io + 1) * 16)
        with nc.Fori(0, 64, engines=(ET.PE, ET.DVE, ET.Activation)) as ii:
            for sj in range(2):
                hin = hbufs[sj % 2]
                hout = hbufs[1 - sj % 2]
                t_e.wait_ge(gd, io * 1024 + ii * 16 + sj * 8)
                for ot in range(24):
                    for it in range(8):
                        mm = t_e.matmul(psb[:, ot:ot + 1],
                                        whh[:, it * 3072 + ot * 128:it * 3072 + ot * 128 + 128],
                                        hin[:, it:it + 1],
                                        start=(it == 0), stop=(it == 7))
                mm.then_inc(gp, 1)
                v_e.wait_ge(gp, io * 128 + ii * 2 + sj + 1)
                v_e.tensor_copy(hw, psb[:, 0:24]).then_inc(gd, 1)
                v_e.wait_ge(gd, io * 1024 + ii * 16 + sj * 8 + 1)
                v_e.tensor_tensor(rz, hw[:, 0:16], xwv[:, 0:16, ii * 2 + sj], ALU.add).then_inc(gd, 1)
                s_e.wait_ge(gd, io * 1024 + ii * 16 + sj * 8 + 2)
                s_e.activation(rz, rz, AF.Sigmoid).then_inc(ga, 1)
                v_e.wait_ge(ga, io * 256 + ii * 4 + sj * 2 + 1)
                v_e.tensor_tensor(scr, rz[:, 0:8], hw[:, 16:24], ALU.mult).then_inc(gd, 1)
                v_e.wait_ge(gd, io * 1024 + ii * 16 + sj * 8 + 3)
                v_e.tensor_tensor(scr, scr, xwv[:, 16:24, ii * 2 + sj], ALU.add).then_inc(gd, 1)
                s_e.wait_ge(gd, io * 1024 + ii * 16 + sj * 8 + 4)
                s_e.activation(scr, scr, AF.Tanh).then_inc(ga, 1)
                v_e.wait_ge(ga, io * 256 + ii * 4 + sj * 2 + 2)
                v_e.tensor_tensor(hw[:, 0:8], hin, scr, ALU.subtract).then_inc(gd, 1)
                v_e.wait_ge(gd, io * 1024 + ii * 16 + sj * 8 + 5)
                v_e.tensor_tensor(hw[:, 0:8], rz[:, 8:16], hw[:, 0:8], ALU.mult).then_inc(gd, 1)
                v_e.wait_ge(gd, io * 1024 + ii * 16 + sj * 8 + 6)
                v_e.tensor_tensor(hout, scr, hw[:, 0:8], ALU.add).then_inc(gd, 1)
                v_e.wait_ge(gd, io * 1024 + ii * 16 + sj * 8 + 7)
                v_e.tensor_copy(hsv[:, :, io * 128 + ii * 2 + sj], hout).then_inc(gd, 1)
    P.op("dve", lambda v: v.nop(), writes=hsk + B.keys(H0A, 20) + ("psb5",) + whhk)
    import os as _os
    if _os.environ.get("L2FFN_STOP") == "gru":
        return
    EXPW, LT, SQS, SSs, NUMs, EEs, CWSTG, TOPS = (16384, 20480, 22528, 22784,
                                                  23296, 23808, 28672, 26624)
    expw = B.arb(EXPW, 8192)
    expwk = B.keys(EXPW, 4096)
    P.dma(chm, expw.rearrange("p (ct o) -> p ct o", o=1024),
          aps["l2_exprw"].rearrange("(ct p) o -> p ct o", p=128), writes=expwk)
    eye = B.ar(TOPS + 128, 64)
    eyek = B.keys(TOPS + 128, 64)
    P.dma(chm, eye, aps["l2_eye"], writes=eyek)
    lt = B.arb(LT, 4096, 0, 8)
    ltk = B.keys(LT, 2048)
    sqs = B.ar(SQS, 512)
    sqk2 = B.keys(SQS, 512)
    SSr = B.ar(SSs, 512, 0, 8)
    NUr = B.ar(NUMs, 512, 0, 8)
    EEr = B.ar(EEs, 512, 0, 8)
    rowk = B.keys(SSs, 1536)
    exps = B.arb(27136, 1024)
    expsk = B.keys(27136, 512)
    for tc in range(8):
        bss, kss = B.ps[0], "psb0"
        bnu, knu = B.ps[1], "psb1"
        bee, kee = B.ps[2], "psb2"
        for e in range(8):
            P.op("dve", lambda v, e2=e, q=tc: v.tensor_mul(
                sqs, hs[:, e2 * T + q * 512:e2 * T + q * 512 + 512],
                hs[:, e2 * T + q * 512:e2 * T + q * 512 + 512]),
                reads=hsk, writes=sqk2)
            P.op("pe", lambda t, e2=e, b=bss: t.matmul(
                b[0:8, :], eye[:, e2 * 8:(e2 + 1) * 8], sqs, start=(e2 == 0), stop=(e2 == 7)),
                reads=sqk2 + eyek, writes=(kss,), inc=True)
        for e in range(8):
            pbx, pkx = B.ps[3 + (e % 2)], f"psb{3 + (e % 2)}"
            for ct in range(8):
                P.op("pe", lambda t, c=ct, e2=e, q=tc, b=pbx: t.matmul(
                    b[:, :], expw[:, c * 1024 + e2 * 128:c * 1024 + e2 * 128 + 128],
                    ssl(B, c, q * 512, 512), start=(c == 0), stop=(c == 7)),
                    reads=(f"st{ct}",) + expwk, writes=(pkx,), inc=(ct == 7))
            P.op("act", lambda s2, b=pbx: s2.activation(exps[:, 0:512], b[:, :], AF.Identity),
                 reads=(pkx,), writes=expsk)
            P.op("dve", lambda v, e2=e, q=tc: v.tensor_mul(
                sqs, exps[:, 0:512], hs[:, e2 * T + q * 512:e2 * T + q * 512 + 512]),
                reads=expsk + hsk, writes=sqk2)
            P.op("pe", lambda t, e2=e, b=bnu: t.matmul(
                b[0:8, :], eye[:, e2 * 8:(e2 + 1) * 8], sqs, start=(e2 == 0), stop=(e2 == 7)),
                reads=sqk2 + eyek, writes=(knu,), inc=True)
            P.op("dve", lambda v: v.tensor_mul(sqs, exps[:, 0:512], exps[:, 0:512]),
                 reads=expsk, writes=sqk2)
            P.op("pe", lambda t, e2=e, b=bee: t.matmul(
                b[0:8, :], eye[:, e2 * 8:(e2 + 1) * 8], sqs, start=(e2 == 0), stop=(e2 == 7)),
                reads=sqk2 + eyek, writes=(kee,), inc=True)
        P.op("dve", lambda v, b=bss: v.tensor_copy(SSr, b[0:8, :]), reads=(kss,), writes=rowk)
        P.op("dve", lambda v, b=bnu: v.tensor_copy(NUr, b[0:8, :]), reads=(knu,), writes=rowk)
        P.op("dve", lambda v, b=bee: v.tensor_copy(EEr, b[0:8, :]), reads=(kee,), writes=rowk)
        P.op("act", lambda s2: s2.activation(SSr, SSr, AF.Sqrt), reads=rowk, writes=rowk)
        P.op("dve", lambda v: v.tensor_scalar_max(SSr, SSr, 1e-12), reads=rowk, writes=rowk)
        P.op("act", lambda s2: s2.activation(EEr, EEr, AF.Sqrt), reads=rowk, writes=rowk)
        P.op("dve", lambda v: v.tensor_scalar_max(EEr, EEr, 1e-8), reads=rowk, writes=rowk)
        P.op("dve", lambda v: v.tensor_mul(SSr, SSr, EEr), reads=rowk, writes=rowk)
        P.op("dve", lambda v: v.reciprocal(SSr, SSr), reads=rowk, writes=rowk)
        P.op("dve", lambda v: v.tensor_mul(NUr, NUr, SSr), reads=rowk, writes=rowk)
        P.op("dve", lambda v, q=tc: v.tensor_scalar(
            lt[:, q * 512:(q + 1) * 512], NUr, -9.0, 9.0, ALU.mult, ALU.add),
            reads=rowk, writes=ltk)
    if _os.environ.get("L2FFN_STOP") == "scores":
        return
    cwstg, cwk = emit_top2_from_lt(P, B, lt, ltk, CWSTG, TOPS)
    if _os.environ.get("L2FFN_STOP") == "top2":
        return
    BAS2 = 30720
    bas = B.arb(BAS2, 1024, 0, 8)
    bask = B.keys(BAS2, 512)
    P.dma(chm, bas, aps["bas8"][0:8, :], writes=bask)
    emit_moe_mlp(P, B, "l2f", aps["l2_gate"], aps["l2_up"], aps["l2_down"], 9,
                 cwstg, cwk, (0, 8192, 16384, 20480, 31232), bas, bask)


def emit_l3_attn(P, B, aps):
    MT, WA, WB, WC_, HT = 0, 16384, 20480, 24576, 28672
    QT, KT_, VT, VTOK = 28672, 29184, 29952, 30720
    ET_, ETT, MSK, DEN = 31776, 32800, 33824, 34112
    mt = B.arb(MT, 32768)
    mtk = lambda ct: B.keys(MT + ct * 2048, 2048)
    mtka = B.keys(MT, 16384)
    ht = B.arb(HT, 4096)
    htk = B.keys(HT, 2048)
    wa = B.arb(WA, 8192)
    wak = B.keys(WA, 4096)
    wb = B.arb(WB, 8192)
    wbk = B.keys(WB, 4096)
    qt = B.arb(QT, 1024)
    qtk = B.keys(QT, 512)
    kt = B.arb(KT_, 1536)
    ktk = B.keys(KT_, 768)
    vt = B.arb(VT, 1536)
    vtk = B.keys(VT, 768)
    vtok = B.arb(VTOK, 2112)
    vtokk = B.keys(VTOK, 1056)
    et = B.arb(ET_, 2048)
    etk = B.keys(ET_, 1024)
    ett = B.arb(ETT, 2048)
    ettk = B.keys(ETT, 1024)
    msk = B.arb(MSK, 576)
    mskk = B.keys(MSK, 288)
    den = B.ar(DEN, 512)
    denk = B.keys(DEN, 512)
    chm = P.chan("l3a_misc", "pool")
    P.dma(chm, B.bias[:, 0:8], aps["l3_nm1b"], writes=("bias",))
    P.dma(chm, B.bias[:, 8:16], aps["l3_nm2b"], writes=("bias",))
    P.dma(chm, B.bias[:, 16:24], aps["l3_qb"], writes=("bias",))
    P.dma(chm, B.bias[:, 24:32], aps["l3_kb"], writes=("bias",))
    P.dma(chm, B.bias[:, 32:40], aps["l3_vb"], writes=("bias",))
    P.dma(chm, msk.rearrange("p (v c) -> p v c", c=192),
          aps["l3_masks"].rearrange("(v p) c -> p v c", p=128), writes=mskk)
    P.dma(chm, wa.rearrange("p (ct o) -> p ct o", o=1024),
          aps["l3_nm1"].rearrange("(ct p) o -> p ct o", p=128), writes=wak)
    P.dma(chm, wb.rearrange("p (ct o) -> p ct o", o=1024),
          aps["l3_nm2"].rearrange("(ct p) o -> p ct o", p=128), writes=wbk)
    for tc in range(8):
        for oc in range(8):
            pb, pk = B.bank()
            for ct in range(8):
                P.op("pe", lambda t, c=ct, o=oc, q=tc, b=pb: t.matmul(
                    b[:, :], wa[:, c * 1024 + o * 128:c * 1024 + o * 128 + 128],
                    ssl(B, c, q * 512, 512), start=(c == 0), stop=(c == 7)),
                    reads=(f"st{ct}",) + wak, writes=(pk,), inc=(ct == 7))
            P.op("act", lambda s2, o=oc, b=pb: s2.activation(
                ht[:, o * 512:(o + 1) * 512], b[:, :], AF.Relu, bias=B.bias[:, o:o + 1]),
                reads=(pk, "bias"), writes=htk)
        for oc in range(8):
            pb, pk = B.bank()
            for ct in range(8):
                P.op("pe", lambda t, c=ct, o=oc, b=pb: t.matmul(
                    b[:, :], wb[:, c * 1024 + o * 128:c * 1024 + o * 128 + 128],
                    ht[:, c * 512:(c + 1) * 512], start=(c == 0), stop=(c == 7)),
                    reads=htk + wbk, writes=(pk,), inc=(ct == 7))
            P.op("act", lambda s2, o=oc, q=tc, b=pb: s2.activation(
                mt[:, o * 4096 + q * 512:o * 4096 + q * 512 + 512], b[:, :],
                AF.Identity, bias=B.bias[:, 8 + o:9 + o]),
                reads=(pk, "bias"), writes=mtk(oc)),
    P.dma(chm, wa.rearrange("p (ct o) -> p ct o", o=1024),
          aps["l3_qw"].rearrange("(ct p) o -> p ct o", p=128), writes=wak)
    P.dma(chm, wb.rearrange("p (ct o) -> p ct o", o=1024),
          aps["l3_kw"].rearrange("(ct p) o -> p ct o", p=128), writes=wbk)
    wc = B.arb(WC_, 8192)
    wck = B.keys(WC_, 4096)
    P.dma(chm, wc.rearrange("p (ct o) -> p ct o", o=1024),
          aps["l3_vw"].rearrange("(ct p) o -> p ct o", p=128), writes=wck)
    SC = float(1.0 / np.sqrt(1024.0))
    for it in range(32):
        i0 = it * 128
        s0 = max(0, min(i0 - 32, T - 192))
        var = 0 if it == 0 else (2 if it == 31 else 1)
        for oc in range(8):
            pb, pk = B.bank()
            for ct in range(8):
                P.op("pe", lambda t, c=ct, o=oc, b=pb: t.matmul(
                    b[:, 0:128], wa[:, c * 1024 + o * 128:c * 1024 + o * 128 + 128],
                    mt[:, c * 4096 + i0:c * 4096 + i0 + 128], start=(c == 0), stop=(c == 7)),
                    reads=mtka + wak, writes=(pk,), inc=(ct == 7))
            P.op("act", lambda s2, o=oc, b=pb: s2.activation(
                qt[:, o * 128:(o + 1) * 128], b[:, 0:128], AF.Identity,
                bias=B.bias[:, 16 + o:17 + o]),
                reads=(pk, "bias"), writes=qtk)
        for wsel, dst, dk2, bcol in ((1, kt, ktk, 24), (2, vt, vtk, 32)):
            ww = wb if wsel == 1 else wc
            wwk = wbk if wsel == 1 else wck
            for oc in range(8):
                pb, pk = B.bank()
                for ct in range(8):
                    P.op("pe", lambda t, c=ct, o=oc, b=pb, w2=ww: t.matmul(
                        b[:, 0:192], w2[:, c * 1024 + o * 128:c * 1024 + o * 128 + 128],
                        mt[:, c * 4096 + s0:c * 4096 + s0 + 192], start=(c == 0), stop=(c == 7)),
                        reads=mtka + wwk, writes=(pk,), inc=(ct == 7))
                P.op("act", lambda s2, o=oc, b=pb, d2=dst, bc=bcol: s2.activation(
                    d2[:, o * 192:o * 192 + 192], b[:, 0:192], AF.Identity,
                    bias=B.bias[:, bc + o:bc + o + 1]),
                    reads=(pk, "bias"), writes=dk2)
        for oc in range(8):
            for sub in range(2):
                w_ = 128 if sub == 0 else 64
                j = (oc * 2 + sub) % 2
                pst = B.pst[j]
                P.op("pe", lambda t, o=oc, sb=sub, p=pst, wz=w_: t.transpose(
                    p[0:wz, 0:128],
                    vt[:, o * 192 + sb * 128:o * 192 + sb * 128 + (128 if sb == 0 else 64)],
                    B.idn[:]),
                    reads=vtk + ("idn",), writes=(f"pstb{j}",))
                P.op("dve", lambda v, o=oc, sb=sub, p=pst, wz=w_: v.tensor_copy(
                    B.arb(VTOK + sb * 528 + o * 64, 128, 0, wz), p[0:wz, 0:128]),
                    reads=(f"pstb{j}",), writes=vtokk)
        pb, pk = B.bank()
        for ct in range(8):
            P.op("pe", lambda t, c=ct, b=pb: t.matmul(
                b[:, 0:192], qt[:, c * 128:(c + 1) * 128], kt[:, c * 192:c * 192 + 192],
                start=(c == 0), stop=(c == 7)),
                reads=qtk + ktk, writes=(pk,), inc=(ct == 7))
        P.op("act", lambda s2, b=pb: s2.activation(et[:, 0:192], b[:, 0:192], AF.Exp, scale=SC),
             reads=(pk,), writes=etk)
        P.op("dve", lambda v, vv=var: v.tensor_mul(
            et[:, 0:192], et[:, 0:192], msk.rearrange("p (v c) -> p v c", c=192)[:, vv, :]),
            reads=etk + mskk, writes=etk)
        for sub in range(2):
            w_ = 128 if sub == 0 else 64
            j = sub % 2
            pst = B.pst[j]
            P.op("pe", lambda t, sb=sub, p=pst, wz=w_: t.transpose(
                p[0:wz, 0:128], et[:, sb * 128:sb * 128 + wz], B.idn[:]),
                reads=etk + ("idn",), writes=(f"pstb{j}",))
            P.op("dve", lambda v, sb=sub, p=pst, wz=w_: v.tensor_copy(
                B.arb(ETT + sb * 512, 128, 0, wz), p[0:wz, 0:128]),
                reads=(f"pstb{j}",), writes=ettk)
        pbd, pkd = B.bank()
        P.op("pe", lambda t, b=pbd: t.matmul(
            b[0:1, 0:128], B.onesb[:, 0:1], B.arb(ETT, 128), start=True, stop=False),
            reads=ettk + ("onesb",), writes=(pkd,), inc=False)
        P.op("pe", lambda t, b=pbd: t.matmul(
            b[0:1, 0:128], B.onesb[0:64, 0:1], B.arb(ETT + 512, 128, 0, 64),
            start=False, stop=True),
            reads=ettk + ("onesb",), writes=(pkd,), inc=True)
        P.op("dve", lambda v, b=pbd: v.reciprocal(den[0:1, 0:128], b[0:1, 0:128]),
             reads=(pkd,), writes=denk)
        pbr, pkr = B.bank()
        P.op("pe", lambda t, b=pbr: t.matmul(
            b[:, 0:128], B.onesrow[0:1, :], den[0:1, 0:128], start=True, stop=True),
            reads=denk + ("onesrow",), writes=(pkr,))
        P.op("act", lambda s2, b=pbr: s2.activation(B.arb(DEN + 128, 128), b[:, 0:128], AF.Identity),
             reads=(pkr,), writes=denk)
        for oc in range(8):
            pb2, pk2 = B.bank()
            P.op("pe", lambda t, o=oc, b=pb2: t.matmul(
                b[:, 0:128], B.arb(VTOK + o * 64, 128), B.arb(ETT, 128),
                start=True, stop=False),
                reads=vtokk + ettk, writes=(pk2,), inc=False)
            P.op("pe", lambda t, o=oc, b=pb2: t.matmul(
                b[:, 0:128], B.arb(VTOK + 528 + o * 64, 128, 0, 64),
                B.arb(ETT + 512, 128, 0, 64), start=False, stop=True),
                reads=vtokk + ettk, writes=(pk2,), inc=True)
            P.op("dve", lambda v, o=oc, b=pb2: v.tensor_mul(
                et[:, 0:128], b[:, 0:128], B.arb(DEN + 128, 128)),
                reads=(pk2,) + denk + etk, writes=etk)
            P.op("dve", lambda v, o=oc, i2=it: v.tensor_add(
                ssl(B, o, i2 * 128, 128), ssl(B, o, i2 * 128, 128), et[:, 0:128]),
                reads=etk + (f"st{oc}",), writes=(f"st{oc}",))


def emit_l3_ffn(P, B, aps, mspill):
    T1o, T2o, TRIo, KMTo, RTRo, GROWo, LOo = 0, 2048, 4096, 5120, 5376, 5408, 7456
    SCT, CMPF, KMo = 7464, 11560, 11560
    ATT, MSB, DWOB = 7464, 15656, 17704
    GATo = 17704
    WH, QK, SW, KR = 19752, 21288, 23336, 25384
    VT2, VTK2, EB, RROW, RB = 27432, 29480, 30536, 31048, 31560
    MWOB = 31816
    t1 = B.arb(T1o, 4096); t1k = B.keys(T1o, 2048)
    t2 = B.arb(T2o, 4096); t2k = B.keys(T2o, 2048)
    tri = B.arb(TRIo, 2048); trik = B.keys(TRIo, 1024)
    kmt = B.ar(KMTo, 256); kmtk = B.keys(KMTo, 256)
    rtr = B.arb(RTRo, 64); rtrk = B.keys(RTRo, 32)
    grow = B.arb(GROWo, 4096, 0, 1); growk = B.keys(GROWo, 2048)
    lo = B.ar(LOo, 8, 0, 8); lok = B.keys(LOo, 8)
    sct = B.ar(SCT, 4096, 0, 8); sctk = B.keys(SCT, 4096)
    cmpf = B.ar(CMPF, 4096, 0, 8); cmpk = B.keys(CMPF, 4096)
    km = B.arb(KMo, 4096, 0, 8); kmk = B.keys(KMo, 2048)
    gat = B.arb(GATo, 4096, 0, 8); gatk = B.keys(GATo, 2048)
    attk = lambda pr: B.keys(ATT + pr * 2048, 2048)
    attka = B.keys(ATT, 8192)
    msb = [B.arb(MSB + i * 1024, 2048) for i in range(2)]
    msbk = [B.keys(MSB + i * 1024, 1024) for i in range(2)]
    dwo = B.arb(DWOB, 4096); dwok = B.keys(DWOB, 2048)
    wh = [B.arb(WH + i * 768, 1536) for i in range(2)]
    whk = [B.keys(WH + i * 768, 768) for i in range(2)]
    qk = B.arb(QK, 4096); qkk = B.keys(QK, 2048)
    swk = B.keys(SW, 2048)
    kr = B.arb(KR, 4096, 0, 64); krk = B.keys(KR, 2048)
    vt2 = B.arb(VT2, 4096, 0, 64); vt2k = B.keys(VT2, 2048)
    ebuf = [B.arb(EB + i * 256, 512) for i in range(2)]
    ekc = [("f3e0",), ("f3e1",)]
    vtk = ("f3vt",)
    rrow = B.ar(RROW, 512, 0, 1); rrk = B.keys(RROW, 512)
    rbuf = B.arb(RB, 512, 0, 64); rbk = B.keys(RB, 256)
    mwo = B.arb(MWOB, 4096); mwok = B.keys(MWOB, 2048)

    chm = P.chan("l3f_misc", "pool")
    P.dma(chm, t1, aps["l3_t1"], writes=t1k)
    P.dma(chm, t2, aps["l3_t2"], writes=t2k)
    P.dma(chm, tri, aps["l3_tri"], writes=trik)
    P.dma(chm, rtr.rearrange("p (ct c) -> p ct c", c=8),
          aps["l3_router"].rearrange("(ct p) c -> p ct c", p=128), writes=rtrk)
    P.dma(chm, mwo.rearrange("p (ct o) -> p ct o", o=1024),
          aps["l3_mwo"].rearrange("(ct p) o -> p ct o", p=128), writes=mwok)

    def fresh_bank(avoid=()):
        b, k = B.bank()
        while k in avoid:
            b, k = B.bank()
        return b, k

    for tc in range(8):
        pb, pk = fresh_bank()
        for ct in range(NDT):
            P.op("pe", lambda t, c=ct, q=tc, b=pb: t.matmul(
                b[0:8, :], rtr.rearrange("p (ct c) -> p ct c", c=8)[:, c, :],
                ssl(B, c, q * 512, 512), start=(c == 0), stop=(c == NDT - 1)),
                reads=(f"st{ct}",) + rtrk, writes=(pk,), inc=(ct == NDT - 1))
        P.op("dve", lambda v, q=tc, b=pb: v.tensor_copy(
            sct[:, q * 512:(q + 1) * 512], b[0:8, :]), reads=(pk,), writes=sctk)
    P.op("dve", lambda v: v.tensor_reduce(lo[:, 0:1], sct, axis=AX.X, op=ALU.min),
         reads=sctk, writes=lok)
    P.op("dve", lambda v: v.reduce_max(lo[:, 1:2], sct, axis=AX.X), reads=sctk, writes=lok)
    P.op("dve", lambda v: v.tensor_scalar_add(lo[:, 0:1], lo[:, 0:1], -1.0), reads=lok, writes=lok)
    P.op("dve", lambda v: v.tensor_scalar_add(lo[:, 1:2], lo[:, 1:2], 1.0), reads=lok, writes=lok)
    for _ in range(42):
        P.op("dve", lambda v: v.tensor_add(lo[:, 2:3], lo[:, 0:1], lo[:, 1:2]), reads=lok, writes=lok)
        P.op("dve", lambda v: v.tensor_scalar_mul(lo[:, 2:3], lo[:, 2:3], 0.5), reads=lok, writes=lok)
        P.op("dve", lambda v: v.tensor_scalar(cmpf, sct, lo[:, 2:3], None, ALU.is_ge),
             reads=sctk + lok, writes=cmpk)
        P.op("dve", lambda v: v.reduce_sum(lo[:, 3:4], cmpf, axis=AX.X), reads=cmpk, writes=lok)
        P.op("dve", lambda v: v.tensor_scalar(lo[:, 4:5], lo[:, 3:4], 512.0, None, ALU.is_ge),
             reads=lok, writes=lok)
        P.op("dve", lambda v: v.tensor_sub(lo[:, 5:6], lo[:, 2:3], lo[:, 0:1]), reads=lok, writes=lok)
        P.op("dve", lambda v: v.tensor_mul(lo[:, 5:6], lo[:, 5:6], lo[:, 4:5]), reads=lok, writes=lok)
        P.op("dve", lambda v: v.tensor_add(lo[:, 0:1], lo[:, 0:1], lo[:, 5:6]), reads=lok, writes=lok)
        P.op("dve", lambda v: v.tensor_scalar(lo[:, 6:7], lo[:, 4:5], -1.0, 1.0, ALU.mult, ALU.add),
             reads=lok, writes=lok)
        P.op("dve", lambda v: v.tensor_sub(lo[:, 5:6], lo[:, 2:3], lo[:, 1:2]), reads=lok, writes=lok)
        P.op("dve", lambda v: v.tensor_mul(lo[:, 5:6], lo[:, 5:6], lo[:, 6:7]), reads=lok, writes=lok)
        P.op("dve", lambda v: v.tensor_add(lo[:, 1:2], lo[:, 1:2], lo[:, 5:6]), reads=lok, writes=lok)
    P.op("dve", lambda v: v.tensor_scalar(km, sct, lo[:, 0:1], None, ALU.is_ge),
         reads=sctk + lok, writes=kmk)
    P.op("act", lambda s2: s2.activation(gat, sct, AF.Sigmoid), reads=sctk, writes=gatk)
    P.op("dve", lambda v: v.tensor_mul(gat, gat, km), reads=gatk + kmk, writes=gatk)
    for kt in range(32):
        j = kt % 2
        pst = B.pst[j]
        P.op("pe", lambda t, q=kt, p=pst: t.transpose(
            p[0:128, 0:8], km[:, q * 128:(q + 1) * 128], B.idn[0:8, 0:8]),
            reads=kmk + ("idn",), writes=(f"pstb{j}",))
        P.op("dve", lambda v, q=kt, p=pst: v.tensor_copy(
            kmt[:, q * 8:(q + 1) * 8], p[0:128, 0:8]),
            reads=(f"pstb{j}",), writes=kmtk)

    SCH = 0.125
    chsw = P.chan("l3f_sw", "sp")

    def run_branch(wq_ap, mosa):
        for h in range(8):
            par = h % 2
            whr = wh[par].rearrange("p (ct c) -> p ct c", c=192)
            src = bass.AP(wq_ap.tensor, h * 1024 * 192,
                          [[192, 128], [192 * 128, 8], [1, 192]])
            P.dma(P.chan(f"l3f_w{par}", "pool"), whr, src, writes=whk[par])
            for tc in range(8):
                pb, pk = fresh_bank()
                for ct in range(8):
                    P.op("pe", lambda t, c=ct, q=tc, b=pb, w2=whr: t.matmul(
                        b[:, :], w2[:, c, 0:128], ssl(B, c, q * 512, 512),
                        start=(c == 0), stop=(c == 7)),
                        reads=(f"st{ct}",) + whk[par], writes=(pk,), inc=(ct == 7))
                P.op("act", lambda s2, q=tc, b=pb: s2.activation(
                    qk[:, q * 512:(q + 1) * 512], b[:, :], AF.Copy),
                    reads=(pk,), writes=qkk)
                pb2, pk2 = fresh_bank()
                for ct in range(8):
                    P.op("pe", lambda t, c=ct, q=tc, b=pb2, w2=whr: t.matmul(
                        b[0:64, :], w2[:, c, 128:192], ssl(B, c, q * 512, 512),
                        start=(c == 0), stop=(c == 7)),
                        reads=(f"st{ct}",) + whk[par], writes=(pk2,), inc=(ct == 7))
                P.op("act", lambda s2, q=tc, b=pb2: s2.activation(
                    vt2[:, q * 512:(q + 1) * 512], b[0:64, :], AF.Copy),
                    reads=(pk2,), writes=vt2k)
            for (d0, s0_) in ((0, 16), (16, 0), (64, 80), (80, 64)):
                P.dma(chsw, B.arb(SW, 4096, d0, 16), B.arb(QK, 4096, s0_, 16),
                      reads=qkk, writes=swk)
            P.op("dve", lambda v: v.tensor_mul(
                B.arb(SW, 4096, 0, 32), B.arb(SW, 4096, 0, 32), B.arb(T2o, 4096, 0, 32)),
                reads=swk + t2k, writes=swk)
            P.op("dve", lambda v: v.tensor_mul(
                B.arb(SW, 4096, 64, 32), B.arb(SW, 4096, 64, 32), B.arb(T2o, 4096, 64, 32)),
                reads=swk + t2k, writes=swk)
            P.op("dve", lambda v: v.tensor_mul(qk, qk, t1), reads=qkk + t1k, writes=qkk)
            P.op("dve", lambda v: v.tensor_add(
                B.arb(QK, 4096, 0, 32), B.arb(QK, 4096, 0, 32), B.arb(SW, 4096, 0, 32)),
                reads=qkk + swk, writes=qkk)
            P.op("dve", lambda v: v.tensor_add(
                B.arb(QK, 4096, 64, 32), B.arb(QK, 4096, 64, 32), B.arb(SW, 4096, 64, 32)),
                reads=qkk + swk, writes=qkk)
            P.dma(chsw, kr, B.arb(QK, 4096, 64, 64), reads=qkk, writes=krk)
            for kt in range(32):
                j = kt % 2
                pst = B.pst[j]
                P.op("pe", lambda t, q=kt, p=pst: t.transpose(
                    p[:, 0:64], vt2[:, q * 128:(q + 1) * 128], B.idn[0:64, 0:64]),
                    reads=vt2k + ("idn",), writes=(f"pstb{j}",))
                if mosa:
                    P.op("dve", lambda v, q=kt, p=pst: v.tensor_scalar(
                        B.arb(VTK2 + q * 33, 64), p[:, 0:64],
                        kmt[:, q * 8 + h:q * 8 + h + 1], None, ALU.mult),
                        reads=(f"pstb{j}",) + kmtk, writes=vtk)
                    P.op("dve", lambda v, q=kt: v.tensor_copy(
                        B.arb(VTK2 + q * 33 + 32, 2)[:, 0:1],
                        kmt[:, q * 8 + h:q * 8 + h + 1]),
                        reads=kmtk, writes=vtk)
                else:
                    P.op("dve", lambda v, q=kt, p=pst: v.tensor_copy(
                        B.arb(VTK2 + q * 33, 64), p[:, 0:64]),
                        reads=(f"pstb{j}",), writes=vtk)
                    if h == 0:
                        P.op("dve", lambda v, q=kt: v.memset(
                            B.arb(VTK2 + q * 33 + 32, 2)[:, 0:1], 1.0), writes=vtk)
            if mosa:
                P.dma(chsw, grow, gat[h:h + 1, :], reads=gatk, writes=growk)
            pair, prow = h // 2, (h % 2) * 64
            for qc in range(8):
                nb, nk = fresh_bank()
                nkt = qc * 4 + 4
                for kt in range(nkt):
                    sb, sk = fresh_bank((nk,))
                    P.op("pe", lambda t, k_=kt, q=qc, b=sb: t.matmul(
                        b[:, :], kr[:, k_ * 128:(k_ + 1) * 128],
                        B.arb(QK, 4096, 0, 64)[:, q * 512:(q + 1) * 512],
                        start=True, stop=True),
                        reads=krk + qkk, writes=(sk,))
                    P.op("act", lambda s2, k_=kt, b=sb: s2.activation(
                        ebuf[k_ % 2], b[:, :], AF.Exp, scale=SCH),
                        reads=(sk,), writes=ekc[kt % 2])
                    if kt >= qc * 4:
                        o = kt - qc * 4
                        P.op("dve", lambda v, k_=kt, o2=o: v.tensor_mul(
                            ebuf[k_ % 2], ebuf[k_ % 2], tri[:, o2 * 512:(o2 + 1) * 512]),
                            reads=ekc[kt % 2] + trik, writes=ekc[kt % 2])
                    P.op("pe", lambda t, k_=kt, b=nb, n=nkt: t.matmul(
                        b[0:65, :], B.arb(VTK2 + k_ * 33, 65), ebuf[k_ % 2],
                        start=(k_ == 0), stop=(k_ == n - 1)),
                        reads=ekc[kt % 2] + vtk, writes=(nk,), inc=(kt == nkt - 1))
                if mosa:
                    P.op("dve", lambda v, b=nb: v.tensor_scalar_max(
                        rrow, b[64:65, :], 1e-30), reads=(nk,), writes=rrk)
                    P.op("dve", lambda v: v.reciprocal(rrow, rrow), reads=rrk, writes=rrk)
                    P.op("dve", lambda v, q=qc: v.tensor_mul(
                        rrow, rrow, grow[0:1, q * 512:(q + 1) * 512]),
                        reads=rrk + growk, writes=rrk)
                else:
                    P.op("dve", lambda v, b=nb: v.reciprocal(rrow, b[64:65, :]),
                         reads=(nk,), writes=rrk)
                bb, bbk2 = fresh_bank((nk,))
                P.op("pe", lambda t, b=bb: t.matmul(
                    b[0:64, :], B.onesrow[0:1, 0:64], rrow, start=True, stop=True),
                    reads=rrk + ("onesrow",), writes=(bbk2,))
                P.op("act", lambda s2, b=bb: s2.activation(rbuf, b[0:64, :], AF.Copy),
                     reads=(bbk2,), writes=rbk)
                P.op("dve", lambda v, b=nb, q=qc, pr=pair, pw=prow: v.tensor_mul(
                    B.arb(ATT + pr * 2048 + q * 256, 512, pw, 64), b[0:64, :], rbuf),
                    reads=(nk,) + rbk, writes=attk(pair))
        if mosa:
            for pr in range(4):
                P.dma(P.chan("l3f_ms", "sp"),
                      bass.AP(mspill.tensor, pr * 128 * T, [[T, 128], [1, T]]),
                      B.arb(ATT + pr * 2048, 4096), reads=attk(pr), writes=("msp",))

    run_branch(aps["l3_mqkv"], True)
    P.dma(P.chan("l3f_wo2", "pool"), dwo.rearrange("p (ct o) -> p ct o", o=1024),
          aps["l3_dwo"].rearrange("(ct p) o -> p ct o", p=128),
          reads=("msp",), writes=dwok)
    run_branch(aps["l3_dqkv"], False)
    for tc in range(8):
        par = tc % 2
        mr = msb[par].rearrange("p (ct c) -> p ct c", c=512)
        P.dma(P.chan(f"l3f_mr{par}", "pool"), mr,
              bass.AP(mspill.tensor, tc * 512, [[T, 128], [128 * T, 4], [1, 512]]),
              reads=("msp",), writes=msbk[par])
        for oc in range(8):
            pb, pk = fresh_bank()
            for ct in range(4):
                P.op("pe", lambda t, c=ct, o=oc, b=pb, m2=mr: t.matmul(
                    b[:, :], mwo[:, c * 1024 + o * 128:c * 1024 + o * 128 + 128],
                    m2[:, c, :], start=(c == 0), stop=False),
                    reads=msbk[par] + mwok, writes=(pk,), inc=False)
            for ct in range(4):
                P.op("pe", lambda t, c=ct, o=oc, q=tc, b=pb: t.matmul(
                    b[:, :], dwo[:, c * 1024 + o * 128:c * 1024 + o * 128 + 128],
                    B.arb(ATT + c * 2048 + q * 256, 512), start=False, stop=(c == 3)),
                    reads=attka + dwok, writes=(pk,), inc=(ct == 3))
            P.op("dve", lambda v, o=oc, q=tc, b=pb: v.tensor_add(
                ssl(B, o, q * 512, 512), ssl(B, o, q * 512, 512), b[:, :]),
                reads=(pk, f"st{oc}"), writes=(f"st{oc}",))
    P.op("dve", lambda v: v.tensor_copy(lo[:, 2:3], lo[:, 0:1]),
         reads=("f3e0", "f3e1", "f3vt") + lok,
         writes=B.keys(VTK2, 1056) + B.keys(EB, 512) + lok)


SLOT_EMIT = {
    "l0_attn": emit_l0_attn,
    "l0_ffn": emit_l0_ffn,
    "l1_attn": emit_l1_attn,
    "l1_ffn": emit_l1_ffn,
    "l2_attn": lambda P, B, aps: emit_l2_attn(P, B, aps, B._l2_spills),
    "l2_ffn": lambda P, B, aps: emit_l2_ffn(P, B, aps, B._l2_xwspill),
    "l3_attn": emit_l3_attn,
    "l3_ffn": lambda P, B, aps: emit_l3_ffn(P, B, aps, B._l3_mspill),
}


DBG = {}


def build_program(slots, shapes):
    nc = bass.Bass("TRN2", target_bir_lowering=False)
    x_hbm = nc.dram_tensor("x", [T, D], F32, kind="ExternalInput").ap()
    y_hbm = nc.dram_tensor("y", [T, D], F32, kind="ExternalOutput").ap()
    need = set(SLOT_INPUTS["prologue"])
    for s in slots:
        if s != "ln":
            need.update(SLOT_INPUTS[s])
    aps = {}
    for name in sorted(need):
        sh, npdt = shapes[name]
        dt = {"float32": F32, "int32": I32}[str(npdt)]
        aps[name] = nc.dram_tensor(name, list(sh), dt, kind="ExternalInput").ap()
    with ExitStack() as ctx:
        P = Prog(nc)
        ctx.enter_context(P.ctx)
        B = Bufs(nc, ctx)
        if "l2_attn" in slots:
            B._l2_spills = (nc.dram_tensor("l2_uspill", [T, 2048], BF16).ap(),
                            nc.dram_tensor("l2_vspill", [T, 2048], BF16).ap())
        if "l2_ffn" in slots:
            B._l2_xwspill = nc.dram_tensor("l2_xwspill", [24 * 128, T], BF16).ap()
        if "l3_ffn" in slots:
            B._l3_mspill = nc.dram_tensor("l3_mspill", [4 * 128, T], BF16).ap()
        emit_prologue(P, B, aps, x_hbm)
        for s in slots:
            if s == "ln":
                emit_ln(P, B)
            else:
                SLOT_EMIT[s](P, B, aps)
        emit_epilogue(P, B, y_hbm)
        P.finalize()
        DBG["P"] = P
        DBG["semmap"] = {k: str(v) for k, v in P.sem.items()}
    return nc, sorted(need)


_PLAN = [["l0_attn", "ln", "l0_ffn", "ln"],
         ["l1_attn", "ln", "l1_ffn", "ln"],
         ["l2_attn", "ln", "l2_ffn", "ln"],
         ["l3_attn", "ln", "l3_ffn", "ln"]]


def kernel(**inputs):
    g = host_prep(inputs)
    shapes = {k: (v.shape, v.dtype) for k, v in g.items()}
    x = np.asarray(inputs["L0_attn_x"], dtype=np.float32)
    cur = [np.ascontiguousarray(x[c]) for c in range(NCORES)]
    for slots in _PLAN:
        nc, names = build_program(slots, shapes)
        in_maps = [dict({n: g[n] for n in names}, x=cur[c]) for c in range(NCORES)]
        res = run_bass_kernel_spmd(nc, in_maps, core_ids=list(range(NCORES)))
        cur = [np.asarray(res.results[c]["y"]) for c in range(NCORES)]
    return np.stack(cur).astype(np.float32)


# version 05
# speedup vs baseline: 1.1298x; 1.1298x over previous
import numpy as np
import ml_dtypes
from contextlib import ExitStack
import concourse.bass as bass
import concourse.mybir as mybir
from concourse.bass_utils import run_bass_kernel_spmd

F32 = mybir.dt.float32
BF16 = mybir.dt.bfloat16
I32 = mybir.dt.int32
AF = mybir.ActivationFunctionType
ALU = mybir.AluOpType
AX = mybir.AxisListType
BF16NP = ml_dtypes.bfloat16

T = 4096
D = 1024
NDT = 8
NCORES = 8
ARENA_F32 = 34816
LN_OFF = 20480

ENGS = ("pe", "dve", "act", "pool", "sp")


class Prog:
    def __init__(self, nc):
        self.nc = nc
        self.ctx = ExitStack()
        self.eng = {"pe": nc.tensor, "dve": nc.vector, "act": nc.scalar,
                    "pool": nc.gpsimd, "sp": nc.sync}
        self.sem = {}
        self.cnt = {}
        for e in ENGS:
            self.sem[e] = self.ctx.enter_context(nc.semaphore(f"sem_{e}"))
            self.cnt[e] = 0
        self.chan_eng = {}
        self.ch_waited = {}
        self.ch_version = {}
        self.watermark = {e: {} for e in ENGS}
        self.last_write = {}
        self.readers = {}
        self.pending = {e: False for e in ENGS}

    def chan(self, name, eng="pool"):
        key = "ch_" + name
        if key not in self.sem:
            self.sem[key] = self.ctx.enter_context(self.nc.semaphore(key))
            self.cnt[key] = 0
            self.chan_eng[key] = eng
        return key

    def _wait(self, eng, token):
        src, val = token
        if src.startswith("ch_"):
            self.ch_waited[src] = True
        if src == eng:
            if eng == "pe":
                return
            if val > self.cnt[eng]:
                raise RuntimeError(
                    f"same-engine dep on pending op ({eng}); producer must inc")
        if src.startswith("ch_"):
            val = self.cnt[src]
        wm = self.watermark[eng]
        if wm.get(src, 0) >= val:
            return
        self.eng[eng].wait_ge(self.sem[src], val)
        wm[src] = val

    def _deps(self, eng, reads, writes):
        for b in reads:
            t = self.last_write.get(b)
            if t is not None:
                self._wait(eng, t)
        for b in writes:
            t = self.last_write.get(b)
            if t is not None:
                self._wait(eng, t)
            for t in self.readers.get(b, ()):
                self._wait(eng, t)

    def _record(self, token, reads, writes):
        for b in writes:
            self.last_write[b] = token
            self.readers[b] = []
        for b in reads:
            self.readers.setdefault(b, []).append(token)

    def op(self, eng, emit, reads=(), writes=(), inc=True):
        self._deps(eng, reads, writes)
        inst = emit(self.eng[eng])
        if inc:
            inst.then_inc(self.sem[eng], 1)
            self.cnt[eng] += 1
            self.pending[eng] = False
            token = (eng, self.cnt[eng])
        else:
            self.pending[eng] = True
            token = (eng, self.cnt[eng] + 1)
        self._record(token, reads, writes)
        return token

    def _fresh_chan(self, ch):
        if "misc" not in ch:
            return ch
        if not self.ch_waited.get(ch):
            return ch
        base = ch.split("#")[0]
        v = self.ch_version.get(base, 0) + 1
        self.ch_version[base] = v
        eng = self.chan_eng[ch]
        newk = f"{base}#{v}"
        if newk not in self.sem:
            self.sem[newk] = self.ctx.enter_context(self.nc.semaphore(newk))
            self.cnt[newk] = 0
            self.chan_eng[newk] = eng
        return newk

    def dma(self, ch, out_ap, in_ap, reads=(), writes=(), **kw):
        ch = self._fresh_chan(ch)
        eng = self.chan_eng[ch]
        self._deps(eng, reads, writes)
        self.eng[eng].dma_start(out_ap, in_ap, **kw).then_inc(self.sem[ch], 16)
        self.cnt[ch] += 16
        token = (ch, self.cnt[ch])
        self._record(token, reads, writes)
        return token

    def idma(self, ch, out_ap, off_out, in_ap, off_in, reads=(), writes=()):
        ch = self._fresh_chan(ch)
        eng = self.chan_eng[ch]
        self._deps(eng, reads, writes)
        self.eng[eng].indirect_dma_start(out_ap, off_out, in_ap, off_in).then_inc(self.sem[ch], 16)
        self.cnt[ch] += 16
        token = (ch, self.cnt[ch])
        self._record(token, reads, writes)
        return token

    def flush(self, eng, reads=(), writes=()):
        self._deps(eng, reads, writes)

    def finalize(self):
        for e in ENGS:
            if self.pending[e]:
                self.op(e, lambda x: x.nop(), inc=True)
        for key, v in self.cnt.items():
            if key.startswith("ch_") and v > 0:
                self.eng["sp"].wait_ge(self.sem[key], v)
        for e in ENGS:
            if e != "sp" and self.cnt[e] > 0:
                self.eng["sp"].wait_ge(self.sem[e], self.cnt[e])


class Bufs:
    def __init__(self, nc, ctx):
        self.stream = ctx.enter_context(nc.sbuf_tensor("stream", [128, NDT * T], BF16))
        self.arena = ctx.enter_context(nc.sbuf_tensor("arena", [128, ARENA_F32], F32))
        self.arena_b = self.arena[:, :].bitcast(BF16)
        self.idn = ctx.enter_context(nc.sbuf_tensor("idn_sb", [128, 128], BF16))
        self.onesb = ctx.enter_context(nc.sbuf_tensor("onesb", [128, 2], BF16))
        self.onesrow = ctx.enter_context(nc.sbuf_tensor("onesrow", [1, 128], F32))
        self.onesbrow = ctx.enter_context(nc.sbuf_tensor("onesbrow", [1, 128], BF16))
        self.epsc = ctx.enter_context(nc.sbuf_tensor("epsc", [1, 2], F32))
        self.bias = ctx.enter_context(nc.sbuf_tensor("biasarena", [128, 96], F32))
        self.idx32 = ctx.enter_context(nc.sbuf_tensor("idx32", [128, 64], I32))
        self.ps = [ctx.enter_context(nc.psum_tensor(f"psum{i}", [128, 512], F32)) for i in range(6)]
        self.pst = [ctx.enter_context(nc.psum_tensor(f"psumt{i}", [128, 1024], BF16)) for i in range(2)]
        self._bank = 0

    def bank(self):
        i = self._bank
        self._bank = (self._bank + 1) % 6
        return self.ps[i], f"psb{i}"

    def ar(self, off, cols, p0=0, pn=128):
        assert 0 <= off and off + cols <= ARENA_F32, (off, cols)
        return self.arena[p0:p0 + pn, off:off + cols]

    def arb(self, offf32, cols_bf16, p0=0, pn=128):
        off = offf32 * 2
        assert 0 <= off and off + cols_bf16 <= ARENA_F32 * 2, (offf32, cols_bf16)
        return self.arena_b[p0:p0 + pn, off:off + cols_bf16]

    @staticmethod
    def keys(off, cols_f32):
        b0 = off // 2048
        b1 = (off + max(cols_f32, 1) - 1) // 2048
        return tuple(f"blk{i}" for i in range(b0, b1 + 1))


def ssl(B, dt, t0, n):
    return B.stream[:, dt * T + t0:dt * T + t0 + n]


def _pf(v):
    v = np.ascontiguousarray(np.asarray(v), dtype=np.float32)
    n = v.shape[0] // 128
    return np.ascontiguousarray(v.reshape(n, 128).T)


def host_prep(inputs):
    g = {}
    f32 = np.float32
    c = lambda x: np.ascontiguousarray(np.asarray(x), dtype=f32)

    g["l0_wqkv"] = c(inputs["L0_attn_w_qkv"])
    g["l0_bqkv"] = _pf(inputs["L0_attn_b_qkv"])
    g["l0_wproj"] = c(inputs["L0_attn_w_proj"])
    g["l0_bproj"] = _pf(inputs["L0_attn_b_proj"])
    g["l0_g1"] = np.stack([
        c(np.transpose(np.asarray(inputs["L0_ffn_G1_R"])[0], (1, 2, 0))),
        c(np.transpose(np.asarray(inputs["L0_ffn_G1_I"])[0], (1, 2, 0)))])
    g["l0_g2"] = np.stack([
        c(np.transpose(np.asarray(inputs["L0_ffn_G2_R"])[:, :, 0, :], (0, 2, 1))),
        c(np.transpose(np.asarray(inputs["L0_ffn_G2_I"])[:, :, 0, :], (0, 2, 1)))])
    g["l0_fbias"] = _pf(np.concatenate([c(inputs["L0_ffn_bias_real"]),
                                        c(inputs["L0_ffn_bias_imag"])]))
    w = np.asarray(inputs["L1_attn_w_qkv"], dtype=f32).reshape(3, 128, 8, 1024)
    g["l1_wqkv"] = c(np.transpose(w, (3, 0, 2, 1)).reshape(1024, 3072))
    b = np.asarray(inputs["L1_attn_b_qkv"], dtype=f32).reshape(3, 128, 8)
    g["l1_bqkv"] = c(np.transpose(b, (0, 2, 1)).reshape(24, 128).T)
    g["l1_bqkv_row"] = c(np.transpose(b, (0, 2, 1)).reshape(1, 3072))
    g["l1_wo"] = c(np.asarray(inputs["L1_attn_w_o"], dtype=f32).T)
    g["l1_bo"] = _pf(inputs["L1_attn_b_o"])
    g["l1_wg"] = c(inputs["L1_ffn_Wg"])
    g["l1_w1"] = c(inputs["L1_ffn_W1"])
    g["l1_w2"] = c(inputs["L1_ffn_W2"])
    g["l1_w3"] = c(inputs["L1_ffn_W3"])
    g["l2_uvw"] = c(np.asarray(inputs["L2_attn_uv_w"], dtype=f32).T)
    g["l2_uvb"] = _pf(inputs["L2_attn_uv_b"])
    g["l2_lnw"] = _pf(inputs["L2_attn_ln_w"])
    g["l2_lnb"] = _pf(inputs["L2_attn_ln_b"])
    gam = np.asarray(inputs["L2_attn_gamma"], dtype=f32)
    bet = np.asarray(inputs["L2_attn_beta"], dtype=f32)
    g["l2_gb"] = c(np.stack([gam[0], bet[0], gam[1], bet[1]], axis=1))
    g["l2_ow"] = c(np.asarray(inputs["L2_attn_o_w"], dtype=f32).T)
    g["l2_ob"] = _pf(inputs["L2_attn_o_b"])
    w_rel = np.asarray(inputs["L2_attn_w"], dtype=f32)
    wx = w_rel[np.clip(np.arange(-4224, 4736), 0, 1022)]
    g["l2_wxr"] = c(wx[::-1])
    dj = np.arange(128)
    idx = np.zeros((128, 63), np.int32)
    for dd in range(-31, 32):
        idx[:, dd + 31] = 4224 - 128 * dd - dj
    g["l2_bidx"] = idx
    half = 64
    inv = (np.float32(10000.0) ** (np.arange(half, dtype=f32) / np.float32(half))).astype(f32)
    ang = (np.arange(T, dtype=f32)[:, None] * inv[None, :]).astype(f32)
    cosv = np.cos(ang.astype(np.float64)).astype(f32).T
    sinv = np.sin(ang.astype(np.float64)).astype(f32).T
    g["l2_tab1"] = c(np.concatenate([cosv, cosv], axis=0))
    g["l2_tab2s"] = c(np.concatenate([-sinv, sinv], axis=0))
    g["l2_ctxw"] = c(np.asarray(inputs["L2_ffn_ctx_w"], dtype=f32).T)
    g["l2_ctxb"] = _pf(inputs["L2_ffn_ctx_b"])
    g["l2_gih"] = c(np.asarray(inputs["L2_ffn_gru_w_ih"], dtype=f32).T)
    g["l2_ghh"] = c(np.asarray(inputs["L2_ffn_gru_w_hh"], dtype=f32).T)
    g["l2_exprw"] = c(np.asarray(inputs["L2_ffn_expr_w"], dtype=f32).T)
    gw = np.asarray(inputs["L2_ffn_gate_w"], dtype=f32)
    uw = np.asarray(inputs["L2_ffn_up_w"], dtype=f32)
    dw = np.asarray(inputs["L2_ffn_down_w"], dtype=f32)
    g["l2_gate"] = c(np.concatenate([np.transpose(gw, (0, 2, 1)),
                                     np.asarray(inputs["L2_ffn_shared_gate_w"], dtype=f32).T[None]], axis=0))
    g["l2_up"] = c(np.concatenate([np.transpose(uw, (0, 2, 1)),
                                   np.asarray(inputs["L2_ffn_shared_up_w"], dtype=f32).T[None]], axis=0))
    g["l2_down"] = c(np.concatenate([np.transpose(dw, (0, 2, 1)),
                                     np.asarray(inputs["L2_ffn_shared_down_w"], dtype=f32).T[None]], axis=0))
    g["l3_nm1"] = c(inputs["L3_attn_nm_w1"]); g["l3_nm1b"] = _pf(inputs["L3_attn_nm_b1"])
    g["l3_nm2"] = c(inputs["L3_attn_nm_w2"]); g["l3_nm2b"] = _pf(inputs["L3_attn_nm_b2"])
    g["l3_qw"] = c(inputs["L3_attn_q_w"]); g["l3_qb"] = _pf(inputs["L3_attn_q_b"])
    g["l3_kw"] = c(inputs["L3_attn_k_w"]); g["l3_kb"] = _pf(inputs["L3_attn_k_b"])
    g["l3_vw"] = c(inputs["L3_attn_v_w"]); g["l3_vb"] = _pf(inputs["L3_attn_v_b"])
    masks = np.zeros((3, 128, 192), dtype=f32)
    for var, i0 in ((0, 0), (1, 128), (2, T - 128)):
        s0 = max(0, min(i0 - 32, T - 192))
        for di in range(128):
            i = i0 + di
            for w in range(-32, 33):
                j = i + w
                if 0 <= j < T and 0 <= j - s0 < 192:
                    masks[var, di, j - s0] = 1.0
    g["l3_masks"] = c(masks.reshape(3 * 128, 192))
    mq = np.asarray(inputs["L3_ffn_mosa_wqkv"], dtype=f32)
    dq = np.asarray(inputs["L3_ffn_dense_wqkv"], dtype=f32)
    g["l3_mqkv"] = c(np.transpose(mq, (1, 0, 2)))
    g["l3_dqkv"] = c(np.transpose(dq, (1, 0, 2)))
    g["l3_mwo"] = c(inputs["L3_ffn_mosa_wo"])
    g["l3_dwo"] = c(inputs["L3_ffn_dense_wo"])
    g["l3_router"] = c(inputs["L3_ffn_mosa_router_w"])
    halfr = 16
    invr = (np.float32(10000.0) ** (-(np.arange(halfr, dtype=f32)) / np.float32(halfr))).astype(f32)
    angr = (np.arange(T, dtype=f32)[:, None] * invr[None, :]).astype(f32)
    cr = np.cos(angr.astype(np.float64)).astype(f32).T
    sr = np.sin(angr.astype(np.float64)).astype(f32).T
    one = np.ones((32, T), dtype=f32); zer = np.zeros((32, T), dtype=f32)
    g["l3_t1"] = c(np.concatenate([cr, cr, one, cr, cr, one], axis=0))
    g["l3_t2"] = c(np.concatenate([-sr, sr, zer, -sr, sr, zer], axis=0))
    tri = np.zeros((128, 4, 512), dtype=f32)
    dk = np.arange(128)[:, None]; dq = np.arange(512)[None, :]
    for o in range(4):
        tri[:, o, :] = (dk <= dq - o * 128) * 1.0
    g["l3_tri"] = c(tri.reshape(128, 4 * 512))
    eye = np.zeros((128, 64), dtype=f32)
    for e in range(8):
        eye[:, e * 8 + e] = 1.0
    g["l2_eye"] = eye
    bas = np.zeros((8, 1024), dtype=f32)
    for e in range(8):
        bas[e, e * 128:(e + 1) * 128] = 1.0
    g["bas8"] = bas
    g["idn"] = np.eye(128, dtype=f32)
    return g


SLOT_INPUTS = {
    "prologue": ["idn"],
    "l0_attn": ["l0_wqkv", "l0_bqkv", "l0_wproj", "l0_bproj"],
    "l0_ffn": ["l0_g1", "l0_g2", "l0_fbias"],
    "l1_attn": ["l1_wqkv", "l1_bqkv", "l1_bqkv_row", "l1_wo", "l1_bo"],
    "l1_ffn": ["l1_wg", "l1_w1", "l1_w2", "l1_w3", "bas8"],
    "l2_attn": ["l2_uvw", "l2_uvb", "l2_lnw", "l2_lnb", "l2_gb", "l2_ow", "l2_ob",
                "l2_wxr", "l2_bidx", "l2_tab1", "l2_tab2s"],
    "l2_ffn": ["l2_ctxw", "l2_ctxb", "l2_gih", "l2_ghh", "l2_exprw", "l2_gate",
               "l2_up", "l2_down", "l2_eye", "bas8"],
    "l3_attn": ["l3_nm1", "l3_nm1b", "l3_nm2", "l3_nm2b", "l3_qw", "l3_qb",
                "l3_kw", "l3_kb", "l3_vw", "l3_vb", "l3_masks"],
    "l3_ffn": ["l3_mqkv", "l3_dqkv", "l3_mwo", "l3_dwo", "l3_router",
               "l3_t1", "l3_t2", "l3_tri"],
}


def emit_prologue(P, B, gap, x_hbm):
    P.dma(P.chan("pro_misc", "pool"), B.idn[:], gap["idn"], writes=("idn",))
    P.op("dve", lambda v: v.memset(B.onesb[:], 1.0), writes=("onesb",))
    P.op("dve", lambda v: v.memset(B.onesrow[:], 1.0), writes=("onesrow",))
    P.op("dve", lambda v: v.memset(B.onesbrow[:], 1.0), writes=("onesbrow",))
    P.op("dve", lambda v: v.memset(B.epsc[:], 1e-5), writes=("epsc",))
    for tt in range(T // 128):
        i = tt % 2
        stg = B.arb(i * 2048, 1024)
        skey = f"blk{i}"
        P.dma(P.chan(f"xstg{i}", "pool"), stg, x_hbm[tt * 128:(tt + 1) * 128, :], writes=(skey,))
        for dt in range(NDT):
            pk = f"pstb{dt % 2}"
            pst = B.pst[dt % 2]
            P.op("pe", lambda t, s=stg, d=dt, p=pst: t.transpose(
                p[:, 0:128], s[:, d * 128:(d + 1) * 128], B.idn[:]),
                reads=(skey, "idn"), writes=(pk,))
            P.op("dve", lambda v, d=dt, q=tt, p=pst: v.tensor_copy(
                ssl(B, d, q * 128, 128), p[:, 0:128]),
                reads=(pk,), writes=(f"st{dt}",))


def emit_epilogue(P, B, y_hbm):
    for tt in range(T // 128):
        i = tt % 2
        stg = B.ar(i * 2048, 1024)
        skey = f"blk{i}"
        for dt in range(NDT):
            pk = f"pstb{dt % 2}"
            pst = B.pst[dt % 2]
            P.op("pe", lambda t, d=dt, q=tt, p=pst: t.transpose(
                p[:, 0:128], ssl(B, d, q * 128, 128), B.idn[:]),
                reads=(f"st{dt}", "idn"), writes=(pk,))
            P.op("dve", lambda v, d=dt, p=pst, s=stg: v.tensor_copy(
                s[:, d * 128:(d + 1) * 128], p[:, 0:128]),
                reads=(pk,), writes=(skey,))
        P.dma(P.chan(f"ystg{i}", "sp"), y_hbm[tt * 128:(tt + 1) * 128, :], stg, reads=(skey,))


def emit_ln_stats(P, B, off, rboff, mboff, eps=1e-5):
    sq = B.arb(off, 4096)
    sqk = B.keys(off, 2048)
    r1 = off + 2048
    r1k = B.keys(r1, 4096)
    S1 = B.ar(r1, 4096, 0, 1)
    S2 = B.ar(r1, 4096, 32, 1)
    mu = B.ar(r1, 4096, 64, 1)
    r2 = r1 + 4096
    r2k = B.keys(r2, 4096)
    var = B.ar(r2, 4096, 0, 1)
    mu2 = B.ar(r2, 4096, 32, 1)
    bsc = B.ar(r2 + 4096, 512, 0, 1)
    bsck = B.keys(r2 + 4096, 512)
    rb = B.arb(rboff, 4096)
    rbk = B.keys(rboff, 2048)
    mb = B.arb(mboff, 4096)
    mbk = B.keys(mboff, 2048)

    for tc in range(8):
        b1, k1 = B.bank()
        for dt in range(NDT):
            P.op("pe", lambda t, d=dt, q=tc, b=b1: t.matmul(
                b[0:1, :], B.onesb[:, 0:1], ssl(B, d, q * 512, 512),
                start=(d == 0), stop=(d == NDT - 1)),
                reads=(f"st{dt}", "onesb"), writes=(k1,), inc=(dt == NDT - 1))
        P.op("dve", lambda v, q=tc, b=b1: v.tensor_copy(S1[:, q * 512:(q + 1) * 512], b[0:1, :]),
             reads=(k1,), writes=r1k)
        for dt in range(NDT):
            P.op("dve", lambda v, d=dt, q=tc: v.tensor_mul(
                sq[:, d * 512:(d + 1) * 512], ssl(B, d, q * 512, 512), ssl(B, d, q * 512, 512)),
                reads=(f"st{dt}",), writes=sqk)
        b2, k2 = B.bank()
        for dt in range(NDT):
            P.op("pe", lambda t, d=dt, b=b2: t.matmul(
                b[0:1, :], B.onesb[:, 0:1], sq[:, d * 512:(d + 1) * 512],
                start=(d == 0), stop=(d == NDT - 1)),
                reads=sqk + ("onesb",), writes=(k2,), inc=(dt == NDT - 1))
        P.op("dve", lambda v, q=tc, b=b2: v.tensor_copy(S2[:, q * 512:(q + 1) * 512], b[0:1, :]),
             reads=(k2,), writes=r1k)

    inv = 1.0 / D
    P.op("dve", lambda v: v.tensor_scalar_mul(mu, S1, inv), reads=r1k, writes=r1k)
    P.op("dve", lambda v: v.tensor_mul(mu2, mu, mu), reads=r1k, writes=r2k)
    P.op("dve", lambda v: v.scalar_tensor_tensor(var, S2, inv, mu2, ALU.mult, ALU.subtract),
         reads=r1k + r2k, writes=r2k)
    P.op("act", lambda s: s.activation(var, var, AF.Ln, bias=B.epsc[0:1, 0:1]),
         reads=r2k + ("epsc",), writes=r2k)
    P.op("act", lambda s: s.activation(var, var, AF.Exp, scale=-0.5), reads=r2k, writes=r2k)

    for tc in range(8):
        br, krr = B.bank()
        bm, kmm = B.bank()
        P.op("pe", lambda t, q=tc, b=br: t.matmul(
            b[:, :], B.onesrow[0:1, :], var[:, q * 512:(q + 1) * 512], start=True, stop=True),
            reads=r2k + ("onesrow",), writes=(krr,))
        P.op("dve", lambda v, q=tc: v.tensor_copy(bsc, mu[:, q * 512:(q + 1) * 512]),
             reads=r1k, writes=bsck)
        P.op("pe", lambda t, b=bm: t.matmul(
            b[:, :], B.onesrow[0:1, :], bsc, start=True, stop=True),
            reads=bsck + ("onesrow",), writes=(kmm,))
        P.op("act", lambda s, q=tc, b=br: s.activation(rb[:, q * 512:(q + 1) * 512], b[:, :], AF.Identity),
             reads=(krr,), writes=rbk)
        P.op("act", lambda s, q=tc, b=bm: s.activation(mb[:, q * 512:(q + 1) * 512], b[:, :], AF.Identity),
             reads=(kmm,), writes=mbk)
    return (rb, rbk, mb, mbk)


def emit_ln(P, B, eps=1e-5):
    off = LN_OFF
    rboff = off + 10752
    mboff = off
    rb, rbk, mb, mbk = emit_ln_stats(P, B, off, rboff, mboff, eps)
    nsc = B.arb(off + 12800, 1024)
    nsck = B.keys(off + 12800, 512)
    for dt in range(NDT):
        for tc in range(8):
            j = (dt * 8 + tc) % 2
            P.op("dve", lambda v, d=dt, q=tc, jj=j: v.tensor_sub(
                nsc[:, jj * 512:jj * 512 + 512], ssl(B, d, q * 512, 512),
                mb[:, q * 512:(q + 1) * 512]),
                reads=(f"st{dt}",) + mbk, writes=nsck)
            P.op("dve", lambda v, d=dt, q=tc, jj=j: v.tensor_mul(
                ssl(B, d, q * 512, 512), nsc[:, jj * 512:jj * 512 + 512],
                rb[:, q * 512:(q + 1) * 512]),
                reads=nsck + rbk, writes=(f"st{dt}",))


def emit_l0_attn(P, B, aps):
    ATTOUT, QK, VT, V, E, RROW, RB, WH, WP = (0, 16384, 20480, 22528, 23584,
                                              24096, 24608, 24864, 26400)
    attout = B.arb(ATTOUT, 32768)
    atk = lambda pair: B.keys(ATTOUT + pair * 2048, 2048)
    qk = B.arb(QK, 8192, 0, 64)
    qkk = B.keys(QK, 4096)
    vt = B.arb(VT, 4096, 0, 64)
    vtk = B.keys(VT, 2048)
    vre = B.arb(V, 2112)
    vk = B.keys(V, 1056)
    EDEPTH = 6
    ebuf = [B.arb(30496 + i * 256, 512) for i in range(EDEPTH)]
    ek = [(f"l0ae{i}",) for i in range(EDEPTH)]
    rrow = B.ar(RROW, 512, 0, 1)
    rrk = B.keys(RROW, 512)
    rbuf = B.arb(RB, 512, 0, 64)
    rbk = B.keys(RB, 256)
    wh = [B.arb(WH + i * 768, 1536) for i in range(2)]
    whk = [B.keys(WH + i * 768, 768) for i in range(2)]
    wp = B.arb(WP, 8192)
    wpk = B.keys(WP, 4096)

    chm = P.chan("l0a_misc", "pool")
    P.dma(chm, B.bias[:, 0:24], aps["l0_bqkv"], writes=("bias",))
    P.dma(chm, B.bias[:, 24:32], aps["l0_bproj"], writes=("bias",))
    P.dma(chm, wp.rearrange("p (ct o) -> p ct o", o=1024),
          aps["l0_wproj"].rearrange("(ct p) o -> p ct o", p=128), writes=wpk)
    wqkv = aps["l0_wqkv"]

    def load_head_w(h):
        dst = wh[h % 2]
        for comp in range(3):
            src = bass.AP(wqkv.tensor, comp * 1024 + h * 64,
                          [[3072, 128], [3072 * 128, 8], [1, 64]])
            P.dma(P.chan(f"l0a_w{h % 2}", "pool"),
                  dst.rearrange("p (ct c) -> p ct c", c=192)[:, :, comp * 64:(comp + 1) * 64],
                  src, writes=whk[h % 2])

    for tt in range(32):
        col = tt * 66 + 64
        P.op("dve", lambda v, c=col: v.memset(B.arb(V + c // 2, 1), 1.0),
             writes=vk, inc=True)

    load_head_w(0)
    for h in range(16):
        if h + 1 < 16:
            load_head_w(h + 1)
        w = wh[h % 2]
        wk = whk[h % 2]
        bq = B.bias[(h % 2) * 64:(h % 2) * 64 + 64, h // 2:h // 2 + 1]
        bk = B.bias[(h % 2) * 64:(h % 2) * 64 + 64, 8 + h // 2:8 + h // 2 + 1]
        bv = B.bias[(h % 2) * 64:(h % 2) * 64 + 64, 16 + h // 2:16 + h // 2 + 1]
        for comp, dstc, bias_ap in ((0, 0, bq), (1, 4096, bk), (2, None, bv)):
            for tc in range(8):
                pb, pk = B.bank()
                for ct in range(8):
                    P.op("pe", lambda t, c=ct, q=tc, b=pb, cp=comp: t.matmul(
                        b[0:64, :], w.rearrange("p (ct c) -> p ct c", c=192)[:, c, cp * 64:(cp + 1) * 64],
                        ssl(B, c, q * 512, 512), start=(c == 0), stop=(c == 7)),
                        reads=(f"st{ct}",) + wk, writes=(pk,), inc=(ct == 7))
                if comp < 2:
                    P.op("act", lambda s, q=tc, b=pb, dc=dstc, ba=bias_ap: s.activation(
                        qk[:, dc + q * 512:dc + q * 512 + 512], b[0:64, :], AF.Identity, bias=ba),
                        reads=(pk, "bias"), writes=qkk)
                else:
                    P.op("act", lambda s, q=tc, b=pb, ba=bias_ap: s.activation(
                        vt[:, q * 512:q * 512 + 512], b[0:64, :], AF.Identity, bias=ba),
                        reads=(pk, "bias"), writes=vtk)
        for tt in range(32):
            pj = tt % 2
            pst = B.pst[pj]
            P.op("pe", lambda t, q=tt, p=pst: t.transpose(
                p[:, 0:64], vt[:, q * 128:(q + 1) * 128], B.idn[0:64, 0:64]),
                reads=vtk + ("idn",), writes=(f"pstb{pj}",))
            P.op("dve", lambda v, q=tt, p=pst: v.tensor_copy(
                B.arb(V + q * 33, 64), p[:, 0:64]),
                reads=(f"pstb{pj}",), writes=vk)
        pair = h // 2
        prow = (h % 2) * 64
        for tc in range(8):
            nb, nk = B.bank()
            DEPTH = 4
            for kt in range(32 + DEPTH):
                if kt < 32:
                    sb, sk = B.bank()
                    if sk == nk:
                        sb, sk = B.bank()
                    P.op("pe", lambda t, k_=kt, q=tc, b=sb: t.matmul(
                        b[:, :], qk[:, 4096 + k_ * 128:4096 + k_ * 128 + 128],
                        qk[:, q * 512:q * 512 + 512],
                        start=True, stop=True),
                        reads=qkk, writes=(sk,))
                    P.op("act", lambda s, k_=kt, b=sb: s.activation(
                        ebuf[k_ % EDEPTH], b[:, :], AF.Exp, scale=0.125),
                        reads=(sk,), writes=ek[kt % EDEPTH])
                if kt >= DEPTH:
                    k0 = kt - DEPTH
                    P.op("pe", lambda t, k_=k0, b=nb: t.matmul(
                        b[0:65, :], B.arb(V + k_ * 33, 65), ebuf[k_ % EDEPTH],
                        start=(k_ == 0), stop=(k_ == 31)),
                        reads=ek[k0 % EDEPTH] + vk, writes=(nk,), inc=(k0 == 31))
            P.op("dve", lambda v, b=nb: v.reciprocal(rrow, b[64:65, :]), reads=(nk,), writes=rrk)
            bb, bbk = B.bank()
            P.op("pe", lambda t, b=bb: t.matmul(
                b[0:64, :], B.onesrow[0:1, 0:64], rrow, start=True, stop=True),
                reads=rrk + ("onesrow",), writes=(bbk,))
            P.op("act", lambda s, b=bb: s.activation(rbuf, b[0:64, :], AF.Copy),
                 reads=(bbk,), writes=rbk)
            P.op("dve", lambda v, q=tc, b=nb: v.tensor_mul(
                B.arb(ATTOUT + pair * 2048 + q * 256, 512, prow, 64),
                b[0:64, :], rbuf),
                reads=(nk,) + rbk, writes=atk(pair))
    P.op("dve", lambda v: v.tensor_copy(rrow[0:1, 0:1], rrow[0:1, 0:1]),
         reads=tuple(x for t2 in ek for x in t2) + rrk,
         writes=B.keys(30496, 1536) + rrk)
    for tc in range(8):
        for oc in range(8):
            pb, pk = B.bank()
            for ct in range(8):
                P.op("pe", lambda t, c=ct, o=oc, q=tc, b=pb: t.matmul(
                    b[:, :], wp[:, c * 1024 + o * 128:c * 1024 + o * 128 + 128],
                    B.arb(ATTOUT + c * 2048 + q * 256, 512), start=(c == 0), stop=(c == 7)),
                    reads=atk(ct) + wpk, writes=(pk,), inc=(ct == 7))
            P.op("dve", lambda v, o=oc, q=tc, b=pb: v.scalar_tensor_tensor(
                ssl(B, o, q * 512, 512), b[:, :], B.bias[:, 24 + o:25 + o],
                ssl(B, o, q * 512, 512), ALU.add, ALU.add),
                reads=(pk, "bias", f"st{oc}"), writes=(f"st{oc}",))


def emit_l0_ffn(P, B, aps):
    A_, K1, FACC, M, ZST = 0, 128, 160, 672, 4768
    a_r = B.ar(A_, 128)
    ak = B.keys(A_, 128)
    k1 = B.arb(K1, 64)
    k1k = B.keys(K1, 32)
    facc = B.ar(FACC, 512)
    fk = B.keys(FACC, 512)
    m = B.arb(M, 8192)
    mk = lambda ct: B.keys(M + ct * 512, 512)
    mkall = B.keys(M, 4096)
    zst = B.arb(ZST, 4096)
    zk = B.keys(ZST, 2048)

    chw = P.chan("l0f_misc", "pool")
    P.dma(chw, B.bias[:, 32:40], aps["l0_fbias"], writes=("bias",))
    g1, g2 = aps["l0_g1"], aps["l0_g2"]
    chw = P.chan("l0f_rep", "pool")

    for var in range(2):
        for d in range(8):
            P.dma(chw, k1[d * 16:(d + 1) * 16, :].rearrange("g (c e) -> g c e", e=16),
                  g2[var].rearrange("c g e -> g c e"), writes=k1k)
        for pt in range(4):
            for d in range(8):
                P.dma(chw, a_r[d * 16:(d + 1) * 16, :].rearrange("g (c b) -> g c b", b=32),
                      g1[var][:, pt * 8 + d, :].unsqueeze(0).broadcast_to([16, 4, 32]),
                      writes=ak)
            for b_ in range(32):
                for c_ in range(4):
                    if c_ == 0:
                        P.op("dve", lambda v, b2=b_, c2=c_: v.tensor_scalar(
                            facc[:, b2 * 16:(b2 + 1) * 16], k1[:, c2 * 16:(c2 + 1) * 16],
                            a_r[:, c2 * 32 + b2:c2 * 32 + b2 + 1], None, ALU.mult),
                            reads=ak + k1k, writes=fk)
                    else:
                        P.op("dve", lambda v, b2=b_, c2=c_: v.scalar_tensor_tensor(
                            facc[:, b2 * 16:(b2 + 1) * 16], k1[:, c2 * 16:(c2 + 1) * 16],
                            a_r[:, c2 * 32 + b2:c2 * 32 + b2 + 1],
                            facc[:, b2 * 16:(b2 + 1) * 16], ALU.mult, ALU.add),
                            reads=ak + k1k + fk, writes=fk)
            if var == 0:
                P.op("dve", lambda v, p2=pt: v.tensor_copy(
                    m[:, p2 * 1024:p2 * 1024 + 512], facc), reads=fk, writes=mk(p2 if False else pt))
                P.op("dve", lambda v, p2=pt: v.tensor_copy(
                    m[:, (4 + p2) * 1024 + 512:(4 + p2) * 1024 + 1024], facc),
                    reads=fk, writes=mk(4 + pt))
            else:
                P.op("dve", lambda v, p2=pt: v.tensor_copy(
                    m[:, p2 * 1024 + 512:p2 * 1024 + 1024], facc),
                    reads=fk, writes=mk(pt))
                P.op("dve", lambda v, p2=pt: v.tensor_scalar_mul(
                    m[:, (4 + p2) * 1024:(4 + p2) * 1024 + 512], facc, -1.0),
                    reads=fk, writes=mk(4 + pt))
    for tc in range(8):
        for oc in range(8):
            pb, pk = B.bank()
            for ct in range(8):
                P.op("pe", lambda t, c=ct, o=oc, q=tc, b=pb: t.matmul(
                    b[:, :], m[:, c * 1024 + o * 128:c * 1024 + o * 128 + 128],
                    ssl(B, c, q * 512, 512), start=(c == 0), stop=(c == 7)),
                    reads=(f"st{ct}",) + mkall, writes=(pk,), inc=(ct == 7))
            P.op("dve", lambda v, o=oc, q=tc, b=pb: v.scalar_tensor_tensor(
                zst[:, o * 512:(o + 1) * 512], b[:, :], B.bias[:, 32 + o:33 + o],
                ssl(B, o, q * 512, 512), ALU.add, ALU.add),
                reads=(pk, "bias", f"st{oc}"), writes=zk)
        for oc in range(8):
            P.op("dve", lambda v, o=oc, q=tc: v.tensor_copy(
                ssl(B, o, q * 512, 512), zst[:, o * 512:(o + 1) * 512]),
                reads=zk, writes=(f"st{oc}",))


def emit_l1_attn(P, B, aps):
    ATTOUT, QT, KT, VT, G_, WH, WO = 0, 16384, 18432, 20480, 22528, 22592, 25664
    atk = lambda h: B.keys(ATTOUT + h * 2048, 2048)
    qt = B.arb(QT, 4096)
    qtk = B.keys(QT, 2048)
    kt = B.arb(KT, 4096)
    ktk = B.keys(KT, 2048)
    vt = B.arb(VT, 4096)
    vtk = B.keys(VT, 2048)
    gb = B.arb(G_, 128)
    gk = B.keys(G_, 64)
    wh = [B.arb(WH + i * 1536, 3072) for i in range(2)]
    whk = [B.keys(WH + i * 1536, 1536) for i in range(2)]
    wo = B.arb(WO, 8192)
    wok = B.keys(WO, 4096)
    BR = WO + 4096
    brow = [B.arb(BR + i * 128, 256, 0, 1) for i in range(2)]
    browk = [B.keys(BR + i * 128, 128) for i in range(2)]

    chm = P.chan("l1a_misc", "pool")
    P.dma(chm, B.bias[:, 0:24], aps["l1_bqkv"], writes=("bias",))
    P.dma(chm, B.bias[:, 24:32], aps["l1_bo"], writes=("bias",))
    P.dma(chm, wo.rearrange("p (ct o) -> p ct o", o=1024),
          aps["l1_wo"].rearrange("(ct p) o -> p ct o", p=128), writes=wok)
    wqkv = aps["l1_wqkv"]

    def load_head_w(h):
        dst = wh[h % 2]
        for comp in range(3):
            src = bass.AP(wqkv.tensor, comp * 1024 + h * 128,
                          [[3072, 128], [3072 * 128, 8], [1, 128]])
            P.dma(P.chan(f"l1a_w{h % 2}", "pool"),
                  dst.rearrange("p (ct c) -> p ct c", c=384)[:, :, comp * 128:(comp + 1) * 128],
                  src, writes=whk[h % 2])
        P.dma(P.chan(f"l1a_w{h % 2}", "pool"), B.arb(BR + (h % 2) * 128, 128, 0, 1),
              bass.AP(aps["l1_bqkv_row"].tensor, 1024 + h * 128, [[1, 1], [1, 128]]),
              writes=browk[h % 2])
        P.dma(P.chan(f"l1a_w{h % 2}", "pool"), B.arb(BR + (h % 2) * 128 + 64, 128, 0, 1),
              bass.AP(aps["l1_bqkv_row"].tensor, 2048 + h * 128, [[1, 1], [1, 128]]),
              writes=browk[h % 2])

    load_head_w(0)
    for h in range(8):
        if h + 1 < 8:
            load_head_w(h + 1)
        w = wh[h % 2]
        wk = whk[h % 2]
        bq = B.bias[:, h:h + 1]
        bk = B.bias[:, 8 + h:8 + h + 1]
        bv = B.bias[:, 16 + h:16 + h + 1]
        for tc in range(8):
            pb, pk = B.bank()
            for ct in range(8):
                P.op("pe", lambda t, c=ct, q=tc, b=pb: t.matmul(
                    b[:, :], w.rearrange("p (ct c) -> p ct c", c=384)[:, c, 0:128],
                    ssl(B, c, q * 512, 512), start=(c == 0), stop=(c == 7)),
                    reads=(f"st{ct}",) + wk, writes=(pk,), inc=(ct == 7))
            P.op("act", lambda s, q=tc, b=pb: s.activation(
                qt[:, q * 512:q * 512 + 512], b[:, :], AF.Identity, bias=bq),
                reads=(pk, "bias"), writes=qtk)
        for tt in range(32):
            pb, pk = B.bank()
            for ct in range(8):
                P.op("pe", lambda t, c=ct, q=tt, b=pb: t.matmul(
                    b[:, 0:256], ssl(B, c, q * 128, 128),
                    w.rearrange("p (ct c) -> p ct c", c=384)[:, c, 128:384],
                    start=(c == 0), stop=False),
                    reads=(f"st{ct}",) + wk, writes=(pk,), inc=False)
            P.op("pe", lambda t, b=pb, hh=h: t.matmul(
                b[:, 0:256], B.onesbrow[0:1, :], brow[hh % 2], start=False, stop=True),
                reads=("onesbrow",) + browk[h % 2], writes=(pk,), inc=True)
            P.op("act", lambda s, q=tt, b=pb: s.activation(
                kt[:, q * 128:q * 128 + 128], b[:, 0:128], AF.Identity),
                reads=(pk,), writes=ktk)
            P.op("act", lambda s, q=tt, b=pb: s.activation(
                vt[:, q * 128:q * 128 + 128], b[:, 128:256], AF.Identity),
                reads=(pk,), writes=vtk)
        pb, pkk = B.bank()
        for tt in range(32):
            P.op("pe", lambda t, q=tt, b=pb: t.matmul(
                b[:, 0:128], kt[:, q * 128:q * 128 + 128], vt[:, q * 128:q * 128 + 128],
                start=(q == 0), stop=(q == 31)),
                reads=ktk + vtk, writes=(pkk,), inc=(tt == 31))
        P.op("dve", lambda v, b=pb: v.tensor_copy(gb, b[:, 0:128]), reads=(pkk,), writes=gk)
        for tc in range(8):
            pb2, pk2 = B.bank()
            P.op("pe", lambda t, q=tc, b=pb2: t.matmul(
                b[:, :], gb, qt[:, q * 512:q * 512 + 512], start=True, stop=True),
                reads=gk + qtk, writes=(pk2,))
            P.op("dve", lambda v, q=tc, b=pb2, hh=h: v.tensor_copy(
                B.arb(ATTOUT + hh * 2048 + q * 256, 512), b[:, :]),
                reads=(pk2,), writes=atk(h))
    for tc in range(8):
        for oc in range(8):
            pb, pk = B.bank()
            for ct in range(8):
                P.op("pe", lambda t, c=ct, o=oc, q=tc, b=pb: t.matmul(
                    b[:, :], wo[:, c * 1024 + o * 128:c * 1024 + o * 128 + 128],
                    B.arb(ATTOUT + c * 2048 + q * 256, 512), start=(c == 0), stop=(c == 7)),
                    reads=atk(ct) + wok, writes=(pk,), inc=(ct == 7))
            P.op("dve", lambda v, o=oc, q=tc, b=pb: v.scalar_tensor_tensor(
                ssl(B, o, q * 512, 512), b[:, :], B.bias[:, 24 + o:25 + o],
                ssl(B, o, q * 512, 512), ALU.add, ALU.add),
                reads=(pk, "bias", f"st{oc}"), writes=(f"st{oc}",))


def emit_moe_top2(P, B, logit_lhsT_fn, logit_reads, CWSTG, TOPS, LT, scale9=False):
    lt = B.arb(LT, 4096, 0, 8)
    ltk = B.keys(LT, 2048)
    for tc in range(8):
        pb, pk = B.bank()
        for ct in range(NDT):
            P.op("pe", lambda t, c=ct, q=tc, b=pb: t.matmul(
                b[0:8, :], logit_lhsT_fn(c), ssl(B, c, q * 512, 512),
                start=(c == 0), stop=(c == NDT - 1)),
                reads=(f"st{ct}",) + logit_reads, writes=(pk,), inc=(ct == NDT - 1))
        P.op("act", lambda s, q=tc, b=pb: s.activation(
            lt[:, q * 512:(q + 1) * 512], b[0:8, :], AF.Identity),
            reads=(pk,), writes=ltk)
    return emit_top2_from_lt(P, B, lt, ltk, CWSTG, TOPS)


def emit_top2_from_lt(P, B, lt, ltk, CWSTG, TOPS):
    cwstg = B.arb(CWSTG, 4096, 0, 8)
    cwk = B.keys(CWSTG, 2048)
    tf = B.ar(TOPS, 128)
    tfk = B.keys(TOPS, 128)
    ltok = B.arb(TOPS + 100, 8)
    cwb16 = B.arb(TOPS + 108, 8)
    for tt in range(32):
        j = tt % 2
        pst = B.pst[j]
        P.op("pe", lambda t, q=tt, p=pst: t.transpose(
            p[:, 0:8], lt[:, q * 128:(q + 1) * 128], B.idn[0:8, 0:8]),
            reads=ltk + ("idn",), writes=(f"pstb{j}",))
        P.op("dve", lambda v, p=pst: v.tensor_copy(ltok, p[:, 0:8]),
             reads=(f"pstb{j}",), writes=tfk)
        m1 = tf[:, 8:9]; m2 = tf[:, 9:10]; e2 = tf[:, 10:11]
        den = tf[:, 11:12]; w2 = tf[:, 12:13]
        l2 = tf[:, 0:8]; mask1 = tf[:, 16:24]; cwf = tf[:, 24:32]
        P.op("dve", lambda v: v.reduce_max(m1, ltok, axis=AX.X), reads=tfk, writes=tfk)
        P.op("dve", lambda v: v.tensor_tensor(mask1, ltok, m1.to_broadcast([128, 8]), ALU.is_equal),
             reads=tfk, writes=tfk)
        P.op("dve", lambda v: v.scalar_tensor_tensor(l2, mask1, -1e30, ltok, ALU.mult, ALU.add),
             reads=tfk, writes=tfk)
        P.op("dve", lambda v: v.reduce_max(m2, l2, axis=AX.X), reads=tfk, writes=tfk)
        P.op("dve", lambda v: v.tensor_sub(e2, m2, m1), reads=tfk, writes=tfk)
        P.op("act", lambda s: s.activation(e2, e2, AF.Exp), reads=tfk, writes=tfk)
        P.op("dve", lambda v: v.tensor_scalar_add(den, e2, 1.0), reads=tfk, writes=tfk)
        P.op("dve", lambda v: v.reciprocal(den, den), reads=tfk, writes=tfk)
        P.op("dve", lambda v: v.tensor_mul(w2, e2, den), reads=tfk, writes=tfk)
        P.op("dve", lambda v: v.tensor_scalar(cwf, mask1, den, None, ALU.mult),
             reads=tfk, writes=tfk)
        P.op("dve", lambda v: v.tensor_tensor(l2, l2, m2.to_broadcast([128, 8]), ALU.is_equal),
             reads=tfk, writes=tfk)
        P.op("dve", lambda v: v.tensor_scalar(l2, l2, w2, None, ALU.mult),
             reads=tfk, writes=tfk)
        P.op("dve", lambda v: v.tensor_add(cwf, cwf, l2), reads=tfk, writes=tfk)
        P.op("dve", lambda v: v.tensor_copy(cwb16, cwf), reads=tfk, writes=tfk)
        pst2 = B.pst[(tt + 1) % 2]
        P.op("pe", lambda t, p=pst2: t.transpose(p[0:8, 128:256], cwb16, B.idn[:]),
             reads=tfk + ("idn",), writes=(f"pstb{(tt + 1) % 2}",))
        P.op("dve", lambda v, q=tt, p=pst2: v.tensor_copy(
            cwstg[:, q * 128:(q + 1) * 128], p[0:8, 128:256]),
            reads=(f"pstb{(tt + 1) % 2}",), writes=cwk)
    return cwstg, cwk


def emit_l1_ffn(P, B, aps):
    Y, H_, W12, W3F, CWSTG, CWB, BAS, TOPS = (0, 8192, 16384, 20480, 28672,
                                              30720, 31232, 31744)
    yk = lambda oc: B.keys(Y + oc * 1024, 1024)
    ykall = B.keys(Y, 8192)
    hk = B.keys(H_, 8192)
    w12 = [(B.arb(W12 + i * 2048, 2048), B.arb(W12 + i * 2048 + 1024, 2048)) for i in range(2)]
    w12k = [B.keys(W12 + i * 2048, 2048) for i in range(2)]
    w3 = B.arb(W3F, 16384)
    w3k = B.keys(W3F, 8192)
    cwb = B.arb(CWB, 1024)
    cwbk = B.keys(CWB, 512)
    bas = B.arb(BAS, 1024, 0, 8)
    bask = B.keys(BAS, 512)

    chm = P.chan("l1f_misc", "pool")
    P.dma(chm, bas, aps["bas8"][0:8, :], writes=bask)
    wg = aps["l1_wg"]

    wgs = B.arb(TOPS + 128, 64)
    wgk = B.keys(TOPS + 128, 32)
    P.dma(chm, wgs.rearrange("p (ct c) -> p ct c", c=8),
          wg.rearrange("(ct p) c -> p ct c", p=128), writes=wgk)
    cwstg, cwk = emit_moe_top2(
        P, B, lambda ct: wgs[:, ct * 8:(ct + 1) * 8], wgk, CWSTG, TOPS, Y)

    emit_moe_mlp(P, B, "l1f", aps["l1_w1"], aps["l1_w2"], aps["l1_w3"], 8,
                 cwstg, cwk, (Y, H_, W12, W3F, CWB), bas, bask)


def emit_moe_mlp(P, B, tag, W1, W2, W3, nE, cwstg, cwk, offs, bas, bask):
    Y, H_, W12, W3F, CWB = offs
    yk = lambda oc: B.keys(Y + oc * 1024, 1024)
    ykall = B.keys(Y, 8192)
    hk = B.keys(H_, 8192)
    w12 = [(B.arb(W12 + i * 2048, 2048), B.arb(W12 + i * 2048 + 1024, 2048)) for i in range(2)]
    w12k = [B.keys(W12 + i * 2048, 2048) for i in range(2)]
    w3 = B.arb(W3F, 16384)
    w3k = B.keys(W3F, 8192)
    cwb = B.arb(CWB, 1024)
    cwbk = B.keys(CWB, 512)

    def load_w12(e, hc):
        par = hc % 2
        dsts = w12[par]
        for wi, W in ((0, W1), (1, W2)):
            src = bass.AP(W.tensor, e * 1024 * 2048 + hc * 256,
                          [[2048, 128], [2048 * 128, 8], [1, 256]])
            P.dma(P.chan(f"{tag}_w{par}", "pool"),
                  dsts[wi].rearrange("p (ct c) -> p ct c", c=256), src,
                  writes=w12k[par])

    def load_w3(e):
        src = bass.AP(W3.tensor, e * 2048 * 1024,
                      [[1024, 128], [1024 * 128, 16], [1, 1024]])
        P.dma(P.chan(f"{tag}_w3", "pool"),
              w3.rearrange("p (ht c) -> p ht c", c=1024), src, writes=w3k)

    for q4 in range(4):
        t0 = q4 * 1024
        for e in range(nE):
            shared = (e == nE - 1 and nE == 9)
            if not shared:
                for tc in range(2):
                    pb, pk = B.bank()
                    P.op("pe", lambda t, q=tc, b=pb, ee=e: t.matmul(
                        b[:, :], bas[:, ee * 128:(ee + 1) * 128],
                        cwstg[:, t0 + q * 512:t0 + q * 512 + 512], start=True, stop=True),
                        reads=bask + cwk, writes=(pk,))
                    P.op("act", lambda s, q=tc, b=pb: s.activation(
                        cwb[:, q * 512:(q + 1) * 512], b[:, :], AF.Identity),
                        reads=(pk,), writes=cwbk)
            load_w3(e)
            for hc in range(8):
                load_w12(e, hc)
                par = hc % 2
                for ot in range(2):
                    ht = hc * 2 + ot
                    for tc in range(2):
                        pbu, pku = B.bank()
                        for ct in range(8):
                            P.op("pe", lambda t, c=ct, o=ot, q=tc, b=pbu: t.matmul(
                                b[:, :], w12[par][0].rearrange("p (ct c) -> p ct c", c=256)[:, c, o * 128:(o + 1) * 128],
                                ssl(B, c, t0 + q * 512, 512), start=(c == 0), stop=(c == 7)),
                                reads=(f"st{ct}",) + w12k[par], writes=(pku,), inc=(ct == 7))
                        P.op("act", lambda s, b=pbu, h2=ht, q=tc: s.activation(
                            B.arb(H_ + h2 * 512 + q * 256, 512), b[:, :], AF.Silu),
                            reads=(pku,), writes=hk)
                        pbv, pkv = B.bank()
                        for ct in range(8):
                            P.op("pe", lambda t, c=ct, o=ot, q=tc, b=pbv: t.matmul(
                                b[:, :], w12[par][1].rearrange("p (ct c) -> p ct c", c=256)[:, c, o * 128:(o + 1) * 128],
                                ssl(B, c, t0 + q * 512, 512), start=(c == 0), stop=(c == 7)),
                                reads=(f"st{ct}",) + w12k[par], writes=(pkv,), inc=(ct == 7))
                        P.op("dve", lambda v, b=pbv, h2=ht, q=tc: v.tensor_mul(
                            B.arb(H_ + h2 * 512 + q * 256, 512),
                            B.arb(H_ + h2 * 512 + q * 256, 512), b[:, :]),
                            reads=(pkv,) + hk, writes=hk)
                        if not shared:
                            P.op("dve", lambda v, h2=ht, q=tc: v.tensor_mul(
                                B.arb(H_ + h2 * 512 + q * 256, 512),
                                B.arb(H_ + h2 * 512 + q * 256, 512),
                                cwb[:, q * 512:(q + 1) * 512]),
                                reads=hk + cwbk, writes=hk)
            for tch in range(4):
                for oc in range(8):
                    pb = B.ps[oc // 2]
                    pk = f"psb{oc // 2}"
                    sub = (oc % 2) * 256
                    for ht in range(16):
                        P.op("pe", lambda t, h2=ht, o=oc, q=tch, b=pb, sb=sub: t.matmul(
                            b[:, sb:sb + 256],
                            w3.rearrange("p (ht c) -> p ht c", c=1024)[:, h2, o * 128:(o + 1) * 128],
                            B.arb(H_ + h2 * 512 + q * 128, 256), start=(h2 == 0), stop=(h2 == 15)),
                            reads=hk + w3k, writes=(pk,), inc=(ht == 15))
                    if e == 0:
                        P.op("dve", lambda v, o=oc, q=tch, b=pb, sb=sub: v.tensor_copy(
                            B.ar(Y + o * 1024 + q * 256, 256), b[:, sb:sb + 256]),
                            reads=(pk,), writes=yk(oc))
                    else:
                        P.op("dve", lambda v, o=oc, q=tch, b=pb, sb=sub: v.tensor_add(
                            B.ar(Y + o * 1024 + q * 256, 256),
                            B.ar(Y + o * 1024 + q * 256, 256), b[:, sb:sb + 256]),
                            reads=(pk,) + yk(oc), writes=yk(oc))
        for oc in range(8):
            for tc in range(2):
                P.op("dve", lambda v, o=oc, q=tc: v.tensor_add(
                    ssl(B, o, t0 + q * 512, 512), ssl(B, o, t0 + q * 512, 512),
                    B.ar(Y + o * 1024 + q * 512, 512)),
                    reads=ykall + (f"st{oc}",), writes=(f"st{oc}",))


def emit_l2_attn(P, B, aps, spills):
    XN, WT, QT, KT = 0, 16384, 17408, 19456
    TAB1, TAB2 = 21504, 23552
    UBUF, TOKS = 25600, 27648
    RB, MB = 29696, 31744
    BSC2 = 33792
    BT, OWT, KERNT = 0, 8064, 21504
    V0, V1, UTOK, ZR, ZT, BIDX = 23552, 24576, 25600, 26624, 27648, 28928

    xn = B.arb(XN, 32768)
    xnk = lambda dt: B.keys(XN + dt * 2048, 2048)
    xnka = B.keys(XN, 16384)
    wt = [B.arb(WT + i * 512, 1024) for i in range(2)]
    wtk = [B.keys(WT + i * 512, 512) for i in range(2)]
    qt = B.arb(QT, 4096)
    qtk = B.keys(QT, 2048)
    kt = B.arb(KT, 4096)
    ktk = B.keys(KT, 2048)
    tab1 = B.arb(TAB1, 4096)
    t1k = B.keys(TAB1, 2048)
    tab2 = B.arb(TAB2, 4096)
    t2k = B.keys(TAB2, 2048)
    ubuf = B.arb(UBUF, 4096)
    ubk = B.keys(UBUF, 2048)
    toks = B.arb(TOKS, 4096)
    tok = B.keys(TOKS, 2048)

    chm = P.chan("l2a_misc", "pool")
    P.dma(chm, B.bias[:, 0:8], aps["l2_lnw"], writes=("bias",))
    P.dma(chm, B.bias[:, 8:16], aps["l2_lnb"], writes=("bias",))
    P.dma(chm, B.bias[:, 16:20], aps["l2_gb"], writes=("bias",))
    P.dma(chm, B.bias[:, 20:28], aps["l2_ob"], writes=("bias",))
    P.dma(chm, B.bias[:, 28:61], aps["l2_uvb"], writes=("bias",))
    P.dma(chm, tab1, aps["l2_tab1"], writes=t1k)
    P.dma(chm, tab2, aps["l2_tab2s"], writes=t2k)

    rb, rbk, mb, mbk = emit_ln_stats(P, B, 0, RB, MB)

    nsc = B.arb(BSC2, 512)
    nsck = B.keys(BSC2, 256)
    for dt in range(NDT):
        for tc in range(8):
            P.op("dve", lambda v, d=dt, q=tc: v.tensor_sub(
                nsc, ssl(B, d, q * 512, 512), mb[:, q * 512:(q + 1) * 512]),
                reads=(f"st{dt}",) + mbk, writes=nsck)
            P.op("dve", lambda v, d=dt, q=tc: v.tensor_mul(
                nsc, nsc, rb[:, q * 512:(q + 1) * 512]),
                reads=nsck + rbk, writes=nsck)
            P.op("dve", lambda v, d=dt, q=tc: v.tensor_scalar(
                xn[:, d * 4096 + q * 512:d * 4096 + q * 512 + 512], nsc,
                B.bias[:, d:d + 1], B.bias[:, 8 + d:9 + d], ALU.mult, ALU.add),
                reads=nsck + ("bias",), writes=xnk(dt)),
    uvw = aps["l2_uvw"]

    def load_wtile(oc, par):
        src = bass.AP(uvw.tensor, oc * 128,
                      [[4224, 128], [4224 * 128, 8], [1, 128]])
        P.dma(P.chan(f"l2a_w{par}", "pool"),
              wt[par].rearrange("p (ct c) -> p ct c", c=128), src, writes=wtk[par])

    uspill, vspill = spills
    load_wtile(0, 0)
    for ui, spill in ((0, uspill), (1, vspill)):
        for oc in range(16):
            gl = ui * 16 + oc
            par = gl % 2
            nxt = gl + 1
            if nxt < 32:
                load_wtile(nxt, nxt % 2)
            bias_col = B.bias[:, 28 + ui * 16 + oc:29 + ui * 16 + oc]
            for tc in range(8):
                pb, pk = B.bank()
                for ct in range(8):
                    P.op("pe", lambda t, c=ct, q=tc, b=pb, pr=par: t.matmul(
                        b[:, :], wt[pr].rearrange("p (ct c) -> p ct c", c=128)[:, c, :],
                        xn[:, c * 4096 + q * 512:c * 4096 + q * 512 + 512],
                        start=(c == 0), stop=(c == 7)),
                        reads=xnk(ct) + wtk[par], writes=(pk,), inc=(ct == 7))
                P.op("act", lambda s, q=tc, b=pb, bc=bias_col: s.activation(
                    ubuf[:, q * 512:(q + 1) * 512], b[:, :], AF.Silu, bias=bc),
                    reads=(pk, "bias"), writes=ubk)
            for tt in range(32):
                j = tt % 2
                pst = B.pst[j]
                P.op("pe", lambda t, q=tt, p=pst: t.transpose(
                    p[:, 0:128], ubuf[:, q * 128:(q + 1) * 128], B.idn[:]),
                    reads=ubk + ("idn",), writes=(f"pstb{j}",))
                P.op("dve", lambda v, q=tt, p=pst: v.tensor_copy(
                    toks[:, q * 128:(q + 1) * 128], p[:, 0:128]),
                    reads=(f"pstb{j}",), writes=tok)
            P.dma(P.chan(f"l2a_sp{par}", "pool"),
                  bass.AP(spill.tensor, oc * 128, [[2048, 128], [2048 * 128, 32], [1, 128]]),
                  toks.rearrange("p (tt c) -> p tt c", c=128), reads=tok,
                  writes=("l2a_uvsp",))
    load_wtile(32, 0)
    bch = B.arb(BSC2, 512)
    bchk = B.keys(BSC2, 256)
    pre = B.arb(BSC2 + 256, 512)
    prek = B.keys(BSC2 + 256, 256)
    sw = B.arb(BSC2 + 512, 512)
    swk = B.keys(BSC2 + 512, 256)
    ta = B.arb(BSC2 + 768, 512)
    tak = B.keys(BSC2 + 768, 256)
    for tc in range(8):
        pb, pk = B.bank()
        for ct in range(8):
            P.op("pe", lambda t, c=ct, q=tc, b=pb: t.matmul(
                b[:, :], wt[0].rearrange("p (ct c) -> p ct c", c=128)[:, c, :],
                xn[:, c * 4096 + q * 512:c * 4096 + q * 512 + 512],
                start=(c == 0), stop=(c == 7)),
                reads=xnk(ct) + wtk[0], writes=(pk,), inc=(ct == 7))
        P.op("act", lambda s, b=pb: s.activation(
            bch, b[:, :], AF.Silu, bias=B.bias[:, 60:61]),
            reads=(pk, "bias"), writes=bchk)
        for br, dst, dk in ((0, qt, qtk), (1, kt, ktk)):
            gcol = B.bias[:, 16 + 2 * br:17 + 2 * br]
            bcol = B.bias[:, 17 + 2 * br:18 + 2 * br]
            P.op("dve", lambda v, g2=gcol, b2=bcol: v.tensor_scalar(
                pre, bch, g2, b2, ALU.mult, ALU.add),
                reads=bchk + ("bias",), writes=prek)
            P.op("dve", lambda v: v.tensor_copy(sw[0:64, :], pre[64:128, :]),
                 reads=prek, writes=swk)
            P.op("dve", lambda v: v.tensor_copy(sw[64:128, :], pre[0:64, :]),
                 reads=prek, writes=swk)
            P.op("dve", lambda v, q=tc: v.tensor_mul(
                ta, pre, tab1[:, q * 512:(q + 1) * 512]),
                reads=prek + t1k, writes=tak)
            P.op("dve", lambda v, q=tc: v.tensor_mul(
                sw, sw, tab2[:, q * 512:(q + 1) * 512]),
                reads=swk + t2k, writes=swk)
            P.op("dve", lambda v, d2=dst, q=tc: v.tensor_add(
                d2[:, q * 512:(q + 1) * 512], ta, sw),
                reads=tak + swk, writes=dk)
    bidx = B.idx32[:, 0:63]
    bxk = ("idx32",)
    P.dma(chm, bidx, aps["l2_bidx"], writes=bxk)
    btk = B.keys(BT, 8064)
    wxr = aps["l2_wxr"].rearrange("(a b) -> a b", b=1)
    for dd in range(63):
        P.idma(P.chan("l2a_bt", "pool"), B.ar(BT + dd * 128, 128), None,
               wxr, bass.IndirectOffsetOnAxis(ap=bidx[:, dd:dd + 1], axis=0),
               reads=bxk, writes=btk)
    owt = B.arb(OWT, 16384)
    owk = B.keys(OWT, 8192)
    P.dma(chm, owt.rearrange("p (et o) -> p et o", o=1024),
          aps["l2_ow"].rearrange("(et p) o -> p et o", p=128), writes=owk)
    kern = B.arb(KERNT, 4096)
    kkk = B.keys(KERNT, 2048)
    vb4 = [B.arb(V0 + i * 512, 1024) for i in range(4)]
    vb4k = [(f"l2av{i}",) for i in range(4)]
    ut = B.arb(UTOK, 2048)
    utk = B.keys(UTOK, 1024)
    zr = B.arb(ZR, 2048)
    zrk = B.keys(ZR, 1024)
    zt = B.arb(ZT, 2048)
    ztk = B.keys(ZT, 1024)
    ksc = B.ar(ZR, 128)
    ksck = zrk
    SC = float(1.0 / np.sqrt(128.0))
    kernb = [kern, B.arb(29696, 4096)]
    kkkb = [kkk, B.keys(29696, 2048)]
    ksc2 = B.ar(31744, 128)
    ksc2k = B.keys(31744, 128)
    utb = [ut, B.arb(31872, 2048)]
    utbk = [utk, B.keys(31872, 1024)]
    loc = [(B.ps[4], "psb4"), (B.ps[5], "psb5")]

    def scores_phase(it):
        kern_, kk_ = kernb[it % 2], kkkb[it % 2]
        for jt in range(32):
            dd = jt - it + 31
            pb, pk = loc[jt % 2]
            P.op("pe", lambda t, j=jt, i2=it, b=pb: t.matmul(
                b[:, 0:128], kt[:, j * 128:(j + 1) * 128], qt[:, i2 * 128:(i2 + 1) * 128],
                start=True, stop=True),
                reads=qtk + ktk, writes=(pk,))
            P.op("dve", lambda v, b=pb, d2=dd: v.scalar_tensor_tensor(
                ksc2, b[:, 0:128], SC, B.ar(BT + d2 * 128, 128), ALU.mult, ALU.add),
                reads=(pk,) + btk, writes=ksc2k)
            P.op("dve", lambda v: v.tensor_scalar_max(ksc2, ksc2, 0.0),
                 reads=ksc2k, writes=ksc2k)
            P.op("dve", lambda v, j=jt, kn=kern_: v.tensor_mul(
                kn[:, j * 128:(j + 1) * 128], ksc2, ksc2),
                reads=ksc2k, writes=kk_)

    def kernv_phase(it):
        kern_, kk_ = kernb[it % 2], kkkb[it % 2]
        P.dma(P.chan(f"l2a_ut{it % 2}", "pool"), utb[it % 2],
              uspill[it * 128:(it + 1) * 128, :],
              reads=("l2a_uvsp",), writes=utbk[it % 2])
        for jt in range(32):
            for half in range(2):
                sl4 = (jt * 2 + half) % 4
                P.dma(P.chan(f"l2a_v{sl4}", "pool"), vb4[sl4],
                      vspill[jt * 128:(jt + 1) * 128, half * 1024:(half + 1) * 1024],
                      reads=("l2a_uvsp",), writes=vb4k[sl4])
                for ec2 in range(2):
                    ec = half * 2 + ec2
                    pbank = B.ps[ec]
                    P.op("pe", lambda t, j=jt, e2=ec2, e=ec, b=pbank, s4=sl4, kn=kern_: t.matmul(
                        b[:, :], kn[:, j * 128:(j + 1) * 128],
                        vb4[s4][:, e2 * 512:(e2 + 1) * 512],
                        start=(j == 0), stop=(j == 31)),
                        reads=kk_ + vb4k[sl4], writes=(f"psb{ec}",),
                        inc=(ec2 == 1))

    def epilogue_phase(it):
        for ec in range(4):
            P.op("dve", lambda v, e=ec, i2=it: v.tensor_mul(
                zr[:, e * 512:(e + 1) * 512], utb[i2 % 2][:, e * 512:(e + 1) * 512],
                B.ps[e][:, :]),
                reads=(f"psb{ec}", ) + utbk[it % 2], writes=zrk)
        for et in range(16):
            j = et % 2
            pst = B.pst[j]
            P.op("pe", lambda t, e=et, p=pst: t.transpose(
                p[:, 0:128], zr[:, e * 128:(e + 1) * 128], B.idn[:]),
                reads=zrk + ("idn",), writes=(f"pstb{j}",))
            P.op("dve", lambda v, e=et, p=pst: v.tensor_copy(
                zt[:, e * 128:(e + 1) * 128], p[:, 0:128]),
                reads=(f"pstb{j}",), writes=ztk)
        for oc in range(8):
            pb, pk = loc[oc % 2]
            for et in range(16):
                P.op("pe", lambda t, e=et, o=oc, b=pb: t.matmul(
                    b[:, 0:128], owt.rearrange("p (et o) -> p et o", o=1024)[:, e, o * 128:(o + 1) * 128],
                    zt[:, e * 128:(e + 1) * 128], start=(e == 0), stop=(e == 15)),
                    reads=ztk + owk, writes=(pk,), inc=(et == 15))
            P.op("dve", lambda v, o=oc, i2=it, b=pb: v.scalar_tensor_tensor(
                ksc, ssl(B, o, i2 * 128, 128), 2.0, b[:, 0:128],
                ALU.mult, ALU.add),
                reads=(pk, f"st{oc}") + ksck, writes=ksck)
            P.op("dve", lambda v, o=oc, i2=it: v.tensor_scalar(
                ssl(B, o, i2 * 128, 128), ksc, B.bias[:, 20 + o:21 + o], None,
                ALU.add),
                reads=ksck + ("bias",), writes=(f"st{oc}",))

    scores_phase(0)
    for it in range(33):
        if it + 1 < 32:
            scores_phase(it + 1)
        if it >= 1:
            epilogue_phase(it - 1)
        if it < 32:
            kernv_phase(it)
    P.op("dve", lambda v: v.tensor_copy(ksc2[0:1, 0:1], ksc2[0:1, 0:1]),
         reads=tuple(x for t2 in vb4k for x in t2) + ksc2k,
         writes=B.keys(V0, 2048) + ksc2k)


def emit_l2_ffn(P, B, aps, xwspill):
    import concourse.mybir as _mb
    HS, WHH, XWB = 0, 16384, 28672
    HW, RZ, SCR = 30208, 30240, 30256
    H0A, H0B, XM, XMB = 30264, 30268, 30272, 30280
    XWROW, WT = 30720, 32768
    CTXW = 0
    hs = B.arb(HS, 32768)
    hsk = B.keys(HS, 16384)
    whh = B.arb(WHH, 24576)
    whhk = B.keys(WHH, 12288)
    xwb = B.arb(XWB, 3072)
    hw = B.ar(HW, 24)
    rz = B.ar(RZ, 16)
    scr = B.ar(SCR, 8)
    h0a = B.arb(H0A, 8)
    h0b = B.arb(H0B, 8)
    xm = B.ar(XM, 8)
    xmb = B.arb(XMB, 8)
    smallk = B.keys(HW, 100)
    xwrow = B.arb(XWROW, 4096)
    xwrk = B.keys(XWROW, 2048)
    wt = [B.arb(WT + i * 512, 1024) for i in range(2)]
    wtk = [B.keys(WT + i * 512, 512) for i in range(2)]

    chm = P.chan("l2f_misc", "pool")
    P.dma(chm, B.bias[:, 0:8], aps["l2_ctxb"], writes=("bias",))
    for dt in range(NDT):
        P.op("dve", lambda v, d=dt: v.reduce_sum(xm[:, d:d + 1], ssl(B, d, 0, T), axis=AX.X),
             reads=(f"st{dt}",), writes=smallk)
    P.op("dve", lambda v: v.tensor_scalar_mul(xmb, xm, 1.0 / T), reads=smallk, writes=smallk)
    ctxw = B.arb(CTXW, 8192)
    ctxwk = B.keys(CTXW, 4096)
    P.dma(chm, ctxw.rearrange("p (ct o) -> p ct o", o=1024),
          aps["l2_ctxw"].rearrange("(ct p) o -> p ct o", p=128), writes=ctxwk)
    pb0, pk0 = B.bank()
    for ot in range(8):
        for ct in range(8):
            P.op("pe", lambda t, o=ot, c=ct, b=pb0: t.matmul(
                b[:, o:o + 1], ctxw[:, c * 1024 + o * 128:c * 1024 + o * 128 + 128],
                xmb[:, c:c + 1], start=(c == 0), stop=(c == 7)),
                reads=ctxwk + smallk, writes=(pk0,), inc=(ot == 7 and ct == 7))
    P.op("dve", lambda v, b=pb0: v.tensor_tensor(h0a, b[:, 0:8], B.bias[:, 0:8], ALU.add),
         reads=(pk0, "bias"), writes=smallk)
    def load_gih(ot, par):
        src = bass.AP(aps["l2_gih"].tensor, ot * 128,
                      [[3072, 128], [3072 * 128, 8], [1, 128]])
        P.dma(P.chan(f"l2f_g{par}", "pool"),
              wt[par].rearrange("p (ct c) -> p ct c", c=128), src, writes=wtk[par])
    load_gih(0, 0)
    for ot in range(24):
        par = ot % 2
        if ot + 1 < 24:
            load_gih(ot + 1, (ot + 1) % 2)
        for tc in range(8):
            pb, pk = B.bank()
            for ct in range(8):
                P.op("pe", lambda t, c=ct, q=tc, b=pb, pr=par: t.matmul(
                    b[:, :], wt[pr].rearrange("p (ct c) -> p ct c", c=128)[:, c, :],
                    ssl(B, c, q * 512, 512), start=(c == 0), stop=(c == 7)),
                    reads=(f"st{ct}",) + wtk[par], writes=(pk,), inc=(ct == 7))
            P.op("act", lambda s, q=tc, b=pb: s.activation(
                xwrow[:, q * 512:(q + 1) * 512], b[:, :], AF.Identity),
                reads=(pk,), writes=xwrk)
        P.dma(P.chan(f"l2f_xw{par}", "sp"),
              bass.AP(xwspill.tensor, ot * 128 * T, [[T, 128], [1, T]]),
              xwrow, reads=xwrk, writes=("xwsp",))
    P.dma(chm, whh.rearrange("p (it o) -> p it o", o=3072),
          aps["l2_ghh"].rearrange("(it p) o -> p it o", p=128), writes=whhk)
    P.flush("pe", reads=whhk + smallk, writes=("psb5",))
    P.flush("dve", reads=smallk, writes=B.keys(HW, 100))
    P.flush("act", reads=B.keys(HW, 100))
    P.flush("pool", reads=("xwsp",))
    nc = P.nc
    gp = P.ctx.enter_context(nc.semaphore("gru_pe"))
    gd = P.ctx.enter_context(nc.semaphore("gru_dv"))
    ga = P.ctx.enter_context(nc.semaphore("gru_ac"))
    chx = P.ctx.enter_context(nc.semaphore("gru_chx"))
    t_e, v_e, s_e, p_e = nc.tensor, nc.vector, nc.scalar, nc.gpsimd
    ET = _mb.EngineType
    psb = B.ps[5]
    hsv = hs.rearrange("p (ot t) -> p ot t", t=T)
    xwv = xwb.rearrange("p (ot ts) -> p ot ts", ts=128)
    hbufs = [h0a, h0b]
    with nc.Fori(0, 32, engines=(ET.PE, ET.DVE, ET.Activation, ET.Pool)) as io:
        p_e.wait_ge(gd, io * 1024)
        p_e.dma_start(
            xwb.rearrange("p (ot ts) -> p ot ts", ts=128),
            bass.AP(xwspill.tensor, io * 128,
                    [[T, 128], [128 * T, 24], [1, 128]])).then_inc(chx, 16)
        t_e.wait_ge(chx, (io + 1) * 16)
        with nc.Fori(0, 64, engines=(ET.PE, ET.DVE, ET.Activation)) as ii:
            for sj in range(2):
                hin = hbufs[sj % 2]
                hout = hbufs[1 - sj % 2]
                t_e.wait_ge(gd, io * 1024 + ii * 16 + sj * 8)
                for ot in range(24):
                    for it in range(8):
                        mm = t_e.matmul(psb[:, ot:ot + 1],
                                        whh[:, it * 3072 + ot * 128:it * 3072 + ot * 128 + 128],
                                        hin[:, it:it + 1],
                                        start=(it == 0), stop=(it == 7))
                mm.then_inc(gp, 1)
                v_e.wait_ge(gp, io * 128 + ii * 2 + sj + 1)
                v_e.tensor_copy(hw, psb[:, 0:24]).then_inc(gd, 1)
                v_e.wait_ge(gd, io * 1024 + ii * 16 + sj * 8 + 1)
                v_e.tensor_tensor(rz, hw[:, 0:16], xwv[:, 0:16, ii * 2 + sj], ALU.add).then_inc(gd, 1)
                s_e.wait_ge(gd, io * 1024 + ii * 16 + sj * 8 + 2)
                s_e.activation(rz, rz, AF.Sigmoid).then_inc(ga, 1)
                v_e.wait_ge(ga, io * 256 + ii * 4 + sj * 2 + 1)
                v_e.tensor_tensor(scr, rz[:, 0:8], hw[:, 16:24], ALU.mult).then_inc(gd, 1)
                v_e.wait_ge(gd, io * 1024 + ii * 16 + sj * 8 + 3)
                v_e.tensor_tensor(scr, scr, xwv[:, 16:24, ii * 2 + sj], ALU.add).then_inc(gd, 1)
                s_e.wait_ge(gd, io * 1024 + ii * 16 + sj * 8 + 4)
                s_e.activation(scr, scr, AF.Tanh).then_inc(ga, 1)
                v_e.wait_ge(ga, io * 256 + ii * 4 + sj * 2 + 2)
                v_e.tensor_tensor(hw[:, 0:8], hin, scr, ALU.subtract).then_inc(gd, 1)
                v_e.wait_ge(gd, io * 1024 + ii * 16 + sj * 8 + 5)
                v_e.tensor_tensor(hw[:, 0:8], rz[:, 8:16], hw[:, 0:8], ALU.mult).then_inc(gd, 1)
                v_e.wait_ge(gd, io * 1024 + ii * 16 + sj * 8 + 6)
                v_e.tensor_tensor(hout, scr, hw[:, 0:8], ALU.add).then_inc(gd, 1)
                v_e.wait_ge(gd, io * 1024 + ii * 16 + sj * 8 + 7)
                v_e.tensor_copy(hsv[:, :, io * 128 + ii * 2 + sj], hout).then_inc(gd, 1)
    P.op("dve", lambda v: v.nop(), writes=hsk + B.keys(H0A, 20) + ("psb5",) + whhk)
    import os as _os
    if _os.environ.get("L2FFN_STOP") == "gru":
        return
    EXPW, LT, SQS, SSs, NUMs, EEs, CWSTG, TOPS = (16384, 20480, 22528, 22784,
                                                  23296, 23808, 28672, 26624)
    expw = B.arb(EXPW, 8192)
    expwk = B.keys(EXPW, 4096)
    P.dma(chm, expw.rearrange("p (ct o) -> p ct o", o=1024),
          aps["l2_exprw"].rearrange("(ct p) o -> p ct o", p=128), writes=expwk)
    eye = B.ar(TOPS + 128, 64)
    eyek = B.keys(TOPS + 128, 64)
    P.dma(chm, eye, aps["l2_eye"], writes=eyek)
    lt = B.arb(LT, 4096, 0, 8)
    ltk = B.keys(LT, 2048)
    sqs = B.ar(SQS, 512)
    sqk2 = B.keys(SQS, 512)
    SSr = B.ar(SSs, 512, 0, 8)
    NUr = B.ar(NUMs, 512, 0, 8)
    EEr = B.ar(EEs, 512, 0, 8)
    rowk = B.keys(SSs, 1536)
    exps = B.arb(27136, 1024)
    expsk = B.keys(27136, 512)
    for tc in range(8):
        bss, kss = B.ps[0], "psb0"
        bnu, knu = B.ps[1], "psb1"
        bee, kee = B.ps[2], "psb2"
        for e in range(8):
            P.op("dve", lambda v, e2=e, q=tc: v.tensor_mul(
                sqs, hs[:, e2 * T + q * 512:e2 * T + q * 512 + 512],
                hs[:, e2 * T + q * 512:e2 * T + q * 512 + 512]),
                reads=hsk, writes=sqk2)
            P.op("pe", lambda t, e2=e, b=bss: t.matmul(
                b[0:8, :], eye[:, e2 * 8:(e2 + 1) * 8], sqs, start=(e2 == 0), stop=(e2 == 7)),
                reads=sqk2 + eyek, writes=(kss,), inc=True)
        for e in range(8):
            pbx, pkx = B.ps[3 + (e % 2)], f"psb{3 + (e % 2)}"
            for ct in range(8):
                P.op("pe", lambda t, c=ct, e2=e, q=tc, b=pbx: t.matmul(
                    b[:, :], expw[:, c * 1024 + e2 * 128:c * 1024 + e2 * 128 + 128],
                    ssl(B, c, q * 512, 512), start=(c == 0), stop=(c == 7)),
                    reads=(f"st{ct}",) + expwk, writes=(pkx,), inc=(ct == 7))
            P.op("act", lambda s2, b=pbx: s2.activation(exps[:, 0:512], b[:, :], AF.Identity),
                 reads=(pkx,), writes=expsk)
            P.op("dve", lambda v, e2=e, q=tc: v.tensor_mul(
                sqs, exps[:, 0:512], hs[:, e2 * T + q * 512:e2 * T + q * 512 + 512]),
                reads=expsk + hsk, writes=sqk2)
            P.op("pe", lambda t, e2=e, b=bnu: t.matmul(
                b[0:8, :], eye[:, e2 * 8:(e2 + 1) * 8], sqs, start=(e2 == 0), stop=(e2 == 7)),
                reads=sqk2 + eyek, writes=(knu,), inc=True)
            P.op("dve", lambda v: v.tensor_mul(sqs, exps[:, 0:512], exps[:, 0:512]),
                 reads=expsk, writes=sqk2)
            P.op("pe", lambda t, e2=e, b=bee: t.matmul(
                b[0:8, :], eye[:, e2 * 8:(e2 + 1) * 8], sqs, start=(e2 == 0), stop=(e2 == 7)),
                reads=sqk2 + eyek, writes=(kee,), inc=True)
        P.op("dve", lambda v, b=bss: v.tensor_copy(SSr, b[0:8, :]), reads=(kss,), writes=rowk)
        P.op("dve", lambda v, b=bnu: v.tensor_copy(NUr, b[0:8, :]), reads=(knu,), writes=rowk)
        P.op("dve", lambda v, b=bee: v.tensor_copy(EEr, b[0:8, :]), reads=(kee,), writes=rowk)
        P.op("act", lambda s2: s2.activation(SSr, SSr, AF.Sqrt), reads=rowk, writes=rowk)
        P.op("dve", lambda v: v.tensor_scalar_max(SSr, SSr, 1e-12), reads=rowk, writes=rowk)
        P.op("act", lambda s2: s2.activation(EEr, EEr, AF.Sqrt), reads=rowk, writes=rowk)
        P.op("dve", lambda v: v.tensor_scalar_max(EEr, EEr, 1e-8), reads=rowk, writes=rowk)
        P.op("dve", lambda v: v.tensor_mul(SSr, SSr, EEr), reads=rowk, writes=rowk)
        P.op("dve", lambda v: v.reciprocal(SSr, SSr), reads=rowk, writes=rowk)
        P.op("dve", lambda v: v.tensor_mul(NUr, NUr, SSr), reads=rowk, writes=rowk)
        P.op("dve", lambda v, q=tc: v.tensor_scalar(
            lt[:, q * 512:(q + 1) * 512], NUr, -9.0, 9.0, ALU.mult, ALU.add),
            reads=rowk, writes=ltk)
    if _os.environ.get("L2FFN_STOP") == "scores":
        return
    cwstg, cwk = emit_top2_from_lt(P, B, lt, ltk, CWSTG, TOPS)
    if _os.environ.get("L2FFN_STOP") == "top2":
        return
    BAS2 = 30720
    bas = B.arb(BAS2, 1024, 0, 8)
    bask = B.keys(BAS2, 512)
    P.dma(chm, bas, aps["bas8"][0:8, :], writes=bask)
    emit_moe_mlp(P, B, "l2f", aps["l2_gate"], aps["l2_up"], aps["l2_down"], 9,
                 cwstg, cwk, (0, 8192, 16384, 20480, 31232), bas, bask)


def emit_l3_attn(P, B, aps):
    MT, WA, WB, WC_, HT = 0, 16384, 20480, 24576, 28672
    QT, KT_, VT, VTOK = 28672, 29184, 29952, 30720
    ET_, ETT, MSK, DEN = 31776, 32800, 33824, 34112
    mt = B.arb(MT, 32768)
    mtk = lambda ct: B.keys(MT + ct * 2048, 2048)
    mtka = B.keys(MT, 16384)
    ht = B.arb(HT, 4096)
    htk = B.keys(HT, 2048)
    wa = B.arb(WA, 8192)
    wak = B.keys(WA, 4096)
    wb = B.arb(WB, 8192)
    wbk = B.keys(WB, 4096)
    qt = B.arb(QT, 1024)
    qtk = B.keys(QT, 512)
    kt = B.arb(KT_, 1536)
    ktk = B.keys(KT_, 768)
    vt = B.arb(VT, 1536)
    vtk = B.keys(VT, 768)
    vtok = B.arb(VTOK, 2112)
    vtokk = B.keys(VTOK, 1056)
    et = B.arb(ET_, 2048)
    etk = B.keys(ET_, 1024)
    ett = B.arb(ETT, 2048)
    ettk = B.keys(ETT, 1024)
    msk = B.arb(MSK, 576)
    mskk = B.keys(MSK, 288)
    den = B.ar(DEN, 512)
    denk = B.keys(DEN, 512)
    chm = P.chan("l3a_misc", "pool")
    P.dma(chm, B.bias[:, 0:8], aps["l3_nm1b"], writes=("bias",))
    P.dma(chm, B.bias[:, 8:16], aps["l3_nm2b"], writes=("bias",))
    P.dma(chm, B.bias[:, 16:24], aps["l3_qb"], writes=("bias",))
    P.dma(chm, B.bias[:, 24:32], aps["l3_kb"], writes=("bias",))
    P.dma(chm, B.bias[:, 32:40], aps["l3_vb"], writes=("bias",))
    P.dma(chm, msk.rearrange("p (v c) -> p v c", c=192),
          aps["l3_masks"].rearrange("(v p) c -> p v c", p=128), writes=mskk)
    P.dma(chm, wa.rearrange("p (ct o) -> p ct o", o=1024),
          aps["l3_nm1"].rearrange("(ct p) o -> p ct o", p=128), writes=wak)
    P.dma(chm, wb.rearrange("p (ct o) -> p ct o", o=1024),
          aps["l3_nm2"].rearrange("(ct p) o -> p ct o", p=128), writes=wbk)
    for tc in range(8):
        for oc in range(8):
            pb, pk = B.bank()
            for ct in range(8):
                P.op("pe", lambda t, c=ct, o=oc, q=tc, b=pb: t.matmul(
                    b[:, :], wa[:, c * 1024 + o * 128:c * 1024 + o * 128 + 128],
                    ssl(B, c, q * 512, 512), start=(c == 0), stop=(c == 7)),
                    reads=(f"st{ct}",) + wak, writes=(pk,), inc=(ct == 7))
            P.op("act", lambda s2, o=oc, b=pb: s2.activation(
                ht[:, o * 512:(o + 1) * 512], b[:, :], AF.Relu, bias=B.bias[:, o:o + 1]),
                reads=(pk, "bias"), writes=htk)
        for oc in range(8):
            pb, pk = B.bank()
            for ct in range(8):
                P.op("pe", lambda t, c=ct, o=oc, b=pb: t.matmul(
                    b[:, :], wb[:, c * 1024 + o * 128:c * 1024 + o * 128 + 128],
                    ht[:, c * 512:(c + 1) * 512], start=(c == 0), stop=(c == 7)),
                    reads=htk + wbk, writes=(pk,), inc=(ct == 7))
            P.op("act", lambda s2, o=oc, q=tc, b=pb: s2.activation(
                mt[:, o * 4096 + q * 512:o * 4096 + q * 512 + 512], b[:, :],
                AF.Identity, bias=B.bias[:, 8 + o:9 + o]),
                reads=(pk, "bias"), writes=mtk(oc)),
    P.dma(chm, wa.rearrange("p (ct o) -> p ct o", o=1024),
          aps["l3_qw"].rearrange("(ct p) o -> p ct o", p=128), writes=wak)
    P.dma(chm, wb.rearrange("p (ct o) -> p ct o", o=1024),
          aps["l3_kw"].rearrange("(ct p) o -> p ct o", p=128), writes=wbk)
    wc = B.arb(WC_, 8192)
    wck = B.keys(WC_, 4096)
    P.dma(chm, wc.rearrange("p (ct o) -> p ct o", o=1024),
          aps["l3_vw"].rearrange("(ct p) o -> p ct o", p=128), writes=wck)
    SC = float(1.0 / np.sqrt(1024.0))
    for it in range(32):
        i0 = it * 128
        s0 = max(0, min(i0 - 32, T - 192))
        var = 0 if it == 0 else (2 if it == 31 else 1)
        for oc in range(8):
            pb, pk = B.bank()
            for ct in range(8):
                P.op("pe", lambda t, c=ct, o=oc, b=pb: t.matmul(
                    b[:, 0:128], wa[:, c * 1024 + o * 128:c * 1024 + o * 128 + 128],
                    mt[:, c * 4096 + i0:c * 4096 + i0 + 128], start=(c == 0), stop=(c == 7)),
                    reads=mtka + wak, writes=(pk,), inc=(ct == 7))
            P.op("act", lambda s2, o=oc, b=pb: s2.activation(
                qt[:, o * 128:(o + 1) * 128], b[:, 0:128], AF.Identity,
                bias=B.bias[:, 16 + o:17 + o]),
                reads=(pk, "bias"), writes=qtk)
        for wsel, dst, dk2, bcol in ((1, kt, ktk, 24), (2, vt, vtk, 32)):
            ww = wb if wsel == 1 else wc
            wwk = wbk if wsel == 1 else wck
            for oc in range(8):
                pb, pk = B.bank()
                for ct in range(8):
                    P.op("pe", lambda t, c=ct, o=oc, b=pb, w2=ww: t.matmul(
                        b[:, 0:192], w2[:, c * 1024 + o * 128:c * 1024 + o * 128 + 128],
                        mt[:, c * 4096 + s0:c * 4096 + s0 + 192], start=(c == 0), stop=(c == 7)),
                        reads=mtka + wwk, writes=(pk,), inc=(ct == 7))
                P.op("act", lambda s2, o=oc, b=pb, d2=dst, bc=bcol: s2.activation(
                    d2[:, o * 192:o * 192 + 192], b[:, 0:192], AF.Identity,
                    bias=B.bias[:, bc + o:bc + o + 1]),
                    reads=(pk, "bias"), writes=dk2)
        for oc in range(8):
            for sub in range(2):
                w_ = 128 if sub == 0 else 64
                j = (oc * 2 + sub) % 2
                pst = B.pst[j]
                P.op("pe", lambda t, o=oc, sb=sub, p=pst, wz=w_: t.transpose(
                    p[0:wz, 0:128],
                    vt[:, o * 192 + sb * 128:o * 192 + sb * 128 + (128 if sb == 0 else 64)],
                    B.idn[:]),
                    reads=vtk + ("idn",), writes=(f"pstb{j}",))
                P.op("dve", lambda v, o=oc, sb=sub, p=pst, wz=w_: v.tensor_copy(
                    B.arb(VTOK + sb * 528 + o * 64, 128, 0, wz), p[0:wz, 0:128]),
                    reads=(f"pstb{j}",), writes=vtokk)
        pb, pk = B.bank()
        for ct in range(8):
            P.op("pe", lambda t, c=ct, b=pb: t.matmul(
                b[:, 0:192], qt[:, c * 128:(c + 1) * 128], kt[:, c * 192:c * 192 + 192],
                start=(c == 0), stop=(c == 7)),
                reads=qtk + ktk, writes=(pk,), inc=(ct == 7))
        P.op("act", lambda s2, b=pb: s2.activation(et[:, 0:192], b[:, 0:192], AF.Exp, scale=SC),
             reads=(pk,), writes=etk)
        P.op("dve", lambda v, vv=var: v.tensor_mul(
            et[:, 0:192], et[:, 0:192], msk.rearrange("p (v c) -> p v c", c=192)[:, vv, :]),
            reads=etk + mskk, writes=etk)
        for sub in range(2):
            w_ = 128 if sub == 0 else 64
            j = sub % 2
            pst = B.pst[j]
            P.op("pe", lambda t, sb=sub, p=pst, wz=w_: t.transpose(
                p[0:wz, 0:128], et[:, sb * 128:sb * 128 + wz], B.idn[:]),
                reads=etk + ("idn",), writes=(f"pstb{j}",))
            P.op("dve", lambda v, sb=sub, p=pst, wz=w_: v.tensor_copy(
                B.arb(ETT + sb * 512, 128, 0, wz), p[0:wz, 0:128]),
                reads=(f"pstb{j}",), writes=ettk)
        pbd, pkd = B.bank()
        P.op("pe", lambda t, b=pbd: t.matmul(
            b[0:1, 0:128], B.onesb[:, 0:1], B.arb(ETT, 128), start=True, stop=False),
            reads=ettk + ("onesb",), writes=(pkd,), inc=False)
        P.op("pe", lambda t, b=pbd: t.matmul(
            b[0:1, 0:128], B.onesb[0:64, 0:1], B.arb(ETT + 512, 128, 0, 64),
            start=False, stop=True),
            reads=ettk + ("onesb",), writes=(pkd,), inc=True)
        P.op("dve", lambda v, b=pbd: v.reciprocal(den[0:1, 0:128], b[0:1, 0:128]),
             reads=(pkd,), writes=denk)
        pbr, pkr = B.bank()
        P.op("pe", lambda t, b=pbr: t.matmul(
            b[:, 0:128], B.onesrow[0:1, :], den[0:1, 0:128], start=True, stop=True),
            reads=denk + ("onesrow",), writes=(pkr,))
        P.op("act", lambda s2, b=pbr: s2.activation(B.arb(DEN + 128, 128), b[:, 0:128], AF.Identity),
             reads=(pkr,), writes=denk)
        for oc in range(8):
            pb2, pk2 = B.bank()
            P.op("pe", lambda t, o=oc, b=pb2: t.matmul(
                b[:, 0:128], B.arb(VTOK + o * 64, 128), B.arb(ETT, 128),
                start=True, stop=False),
                reads=vtokk + ettk, writes=(pk2,), inc=False)
            P.op("pe", lambda t, o=oc, b=pb2: t.matmul(
                b[:, 0:128], B.arb(VTOK + 528 + o * 64, 128, 0, 64),
                B.arb(ETT + 512, 128, 0, 64), start=False, stop=True),
                reads=vtokk + ettk, writes=(pk2,), inc=True)
            P.op("dve", lambda v, o=oc, b=pb2: v.tensor_mul(
                et[:, 0:128], b[:, 0:128], B.arb(DEN + 128, 128)),
                reads=(pk2,) + denk + etk, writes=etk)
            P.op("dve", lambda v, o=oc, i2=it: v.tensor_add(
                ssl(B, o, i2 * 128, 128), ssl(B, o, i2 * 128, 128), et[:, 0:128]),
                reads=etk + (f"st{oc}",), writes=(f"st{oc}",))


def emit_l3_ffn(P, B, aps, mspill):
    T1o, T2o, TRIo, KMTo, RTRo, GROWo, LOo = 0, 2048, 4096, 5120, 5376, 5408, 7456
    SCT, CMPF, KMo = 7464, 11560, 11560
    ATT, MSB, DWOB = 7464, 15656, 17704
    GATo = 17704
    WH, QK, SW, KR = 19752, 21288, 23336, 25384
    VT2, VTK2, EB, RROW, RB = 27432, 29480, 30536, 31048, 31560
    MWOB = 31816
    t1 = B.arb(T1o, 4096); t1k = B.keys(T1o, 2048)
    t2 = B.arb(T2o, 4096); t2k = B.keys(T2o, 2048)
    tri = B.arb(TRIo, 2048); trik = B.keys(TRIo, 1024)
    kmt = B.ar(KMTo, 256); kmtk = B.keys(KMTo, 256)
    rtr = B.arb(RTRo, 64); rtrk = B.keys(RTRo, 32)
    grow = B.arb(GROWo, 4096, 0, 1); growk = B.keys(GROWo, 2048)
    lo = B.ar(LOo, 8, 0, 8); lok = B.keys(LOo, 8)
    sct = B.ar(SCT, 4096, 0, 8); sctk = B.keys(SCT, 4096)
    cmpf = B.ar(CMPF, 4096, 0, 8); cmpk = B.keys(CMPF, 4096)
    km = B.arb(KMo, 4096, 0, 8); kmk = B.keys(KMo, 2048)
    gat = B.arb(GATo, 4096, 0, 8); gatk = B.keys(GATo, 2048)
    attk = lambda pr: B.keys(ATT + pr * 2048, 2048)
    attka = B.keys(ATT, 8192)
    msb = [B.arb(MSB + i * 1024, 2048) for i in range(2)]
    msbk = [B.keys(MSB + i * 1024, 1024) for i in range(2)]
    dwo = B.arb(DWOB, 4096); dwok = B.keys(DWOB, 2048)
    wh = [B.arb(WH + i * 768, 1536) for i in range(2)]
    whk = [B.keys(WH + i * 768, 768) for i in range(2)]
    qk = B.arb(QK, 4096); qkk = B.keys(QK, 2048)
    swk = B.keys(SW, 2048)
    kr = B.arb(KR, 4096, 0, 64); krk = B.keys(KR, 2048)
    vt2 = B.arb(VT2, 4096, 0, 64); vt2k = B.keys(VT2, 2048)
    EDEPTH = 5
    _ebo = [EB, EB + 256, 33864, 34120, 34376]
    ebuf = [B.arb(_ebo[i], 512) for i in range(EDEPTH)]
    ekc = [(f"f3e{i}",) for i in range(EDEPTH)]
    vtk = ("f3vt",)
    rrow = B.ar(RROW, 512, 0, 1); rrk = B.keys(RROW, 512)
    rbuf = B.arb(RB, 512, 0, 64); rbk = B.keys(RB, 256)
    mwo = B.arb(MWOB, 4096); mwok = B.keys(MWOB, 2048)

    chm = P.chan("l3f_misc", "pool")
    P.dma(chm, t1, aps["l3_t1"], writes=t1k)
    P.dma(chm, t2, aps["l3_t2"], writes=t2k)
    P.dma(chm, tri, aps["l3_tri"], writes=trik)
    P.dma(chm, rtr.rearrange("p (ct c) -> p ct c", c=8),
          aps["l3_router"].rearrange("(ct p) c -> p ct c", p=128), writes=rtrk)
    P.dma(chm, mwo.rearrange("p (ct o) -> p ct o", o=1024),
          aps["l3_mwo"].rearrange("(ct p) o -> p ct o", p=128), writes=mwok)

    def fresh_bank(avoid=()):
        b, k = B.bank()
        while k in avoid:
            b, k = B.bank()
        return b, k

    for tc in range(8):
        pb, pk = fresh_bank()
        for ct in range(NDT):
            P.op("pe", lambda t, c=ct, q=tc, b=pb: t.matmul(
                b[0:8, :], rtr.rearrange("p (ct c) -> p ct c", c=8)[:, c, :],
                ssl(B, c, q * 512, 512), start=(c == 0), stop=(c == NDT - 1)),
                reads=(f"st{ct}",) + rtrk, writes=(pk,), inc=(ct == NDT - 1))
        P.op("dve", lambda v, q=tc, b=pb: v.tensor_copy(
            sct[:, q * 512:(q + 1) * 512], b[0:8, :]), reads=(pk,), writes=sctk)
    P.op("dve", lambda v: v.tensor_reduce(lo[:, 0:1], sct, axis=AX.X, op=ALU.min),
         reads=sctk, writes=lok)
    P.op("dve", lambda v: v.reduce_max(lo[:, 1:2], sct, axis=AX.X), reads=sctk, writes=lok)
    P.op("dve", lambda v: v.tensor_scalar_add(lo[:, 0:1], lo[:, 0:1], -1.0), reads=lok, writes=lok)
    P.op("dve", lambda v: v.tensor_scalar_add(lo[:, 1:2], lo[:, 1:2], 1.0), reads=lok, writes=lok)
    for _ in range(42):
        P.op("dve", lambda v: v.tensor_add(lo[:, 2:3], lo[:, 0:1], lo[:, 1:2]), reads=lok, writes=lok)
        P.op("dve", lambda v: v.tensor_scalar_mul(lo[:, 2:3], lo[:, 2:3], 0.5), reads=lok, writes=lok)
        P.op("dve", lambda v: v.tensor_scalar(cmpf, sct, lo[:, 2:3], None, ALU.is_ge),
             reads=sctk + lok, writes=cmpk)
        P.op("dve", lambda v: v.reduce_sum(lo[:, 3:4], cmpf, axis=AX.X), reads=cmpk, writes=lok)
        P.op("dve", lambda v: v.tensor_scalar(lo[:, 4:5], lo[:, 3:4], 512.0, None, ALU.is_ge),
             reads=lok, writes=lok)
        P.op("dve", lambda v: v.tensor_sub(lo[:, 5:6], lo[:, 2:3], lo[:, 0:1]), reads=lok, writes=lok)
        P.op("dve", lambda v: v.tensor_mul(lo[:, 5:6], lo[:, 5:6], lo[:, 4:5]), reads=lok, writes=lok)
        P.op("dve", lambda v: v.tensor_add(lo[:, 0:1], lo[:, 0:1], lo[:, 5:6]), reads=lok, writes=lok)
        P.op("dve", lambda v: v.tensor_scalar(lo[:, 6:7], lo[:, 4:5], -1.0, 1.0, ALU.mult, ALU.add),
             reads=lok, writes=lok)
        P.op("dve", lambda v: v.tensor_sub(lo[:, 5:6], lo[:, 2:3], lo[:, 1:2]), reads=lok, writes=lok)
        P.op("dve", lambda v: v.tensor_mul(lo[:, 5:6], lo[:, 5:6], lo[:, 6:7]), reads=lok, writes=lok)
        P.op("dve", lambda v: v.tensor_add(lo[:, 1:2], lo[:, 1:2], lo[:, 5:6]), reads=lok, writes=lok)
    P.op("dve", lambda v: v.tensor_scalar(km, sct, lo[:, 0:1], None, ALU.is_ge),
         reads=sctk + lok, writes=kmk)
    P.op("act", lambda s2: s2.activation(gat, sct, AF.Sigmoid), reads=sctk, writes=gatk)
    P.op("dve", lambda v: v.tensor_mul(gat, gat, km), reads=gatk + kmk, writes=gatk)
    for kt in range(32):
        j = kt % 2
        pst = B.pst[j]
        P.op("pe", lambda t, q=kt, p=pst: t.transpose(
            p[0:128, 0:8], km[:, q * 128:(q + 1) * 128], B.idn[0:8, 0:8]),
            reads=kmk + ("idn",), writes=(f"pstb{j}",))
        P.op("dve", lambda v, q=kt, p=pst: v.tensor_copy(
            kmt[:, q * 8:(q + 1) * 8], p[0:128, 0:8]),
            reads=(f"pstb{j}",), writes=kmtk)

    SCH = 0.125
    chsw = P.chan("l3f_sw", "sp")

    def run_branch(wq_ap, mosa):
        for h in range(8):
            par = h % 2
            whr = wh[par].rearrange("p (ct c) -> p ct c", c=192)
            src = bass.AP(wq_ap.tensor, h * 1024 * 192,
                          [[192, 128], [192 * 128, 8], [1, 192]])
            P.dma(P.chan(f"l3f_w{par}", "pool"), whr, src, writes=whk[par])
            for tc in range(8):
                pb, pk = fresh_bank()
                for ct in range(8):
                    P.op("pe", lambda t, c=ct, q=tc, b=pb, w2=whr: t.matmul(
                        b[:, :], w2[:, c, 0:128], ssl(B, c, q * 512, 512),
                        start=(c == 0), stop=(c == 7)),
                        reads=(f"st{ct}",) + whk[par], writes=(pk,), inc=(ct == 7))
                P.op("act", lambda s2, q=tc, b=pb: s2.activation(
                    qk[:, q * 512:(q + 1) * 512], b[:, :], AF.Copy),
                    reads=(pk,), writes=qkk)
                pb2, pk2 = fresh_bank()
                for ct in range(8):
                    P.op("pe", lambda t, c=ct, q=tc, b=pb2, w2=whr: t.matmul(
                        b[0:64, :], w2[:, c, 128:192], ssl(B, c, q * 512, 512),
                        start=(c == 0), stop=(c == 7)),
                        reads=(f"st{ct}",) + whk[par], writes=(pk2,), inc=(ct == 7))
                P.op("act", lambda s2, q=tc, b=pb2: s2.activation(
                    vt2[:, q * 512:(q + 1) * 512], b[0:64, :], AF.Copy),
                    reads=(pk2,), writes=vt2k)
            for (d0, s0_) in ((0, 16), (16, 0), (64, 80), (80, 64)):
                P.dma(chsw, B.arb(SW, 4096, d0, 16), B.arb(QK, 4096, s0_, 16),
                      reads=qkk, writes=swk)
            P.op("dve", lambda v: v.tensor_mul(
                B.arb(SW, 4096, 0, 32), B.arb(SW, 4096, 0, 32), B.arb(T2o, 4096, 0, 32)),
                reads=swk + t2k, writes=swk)
            P.op("dve", lambda v: v.tensor_mul(
                B.arb(SW, 4096, 64, 32), B.arb(SW, 4096, 64, 32), B.arb(T2o, 4096, 64, 32)),
                reads=swk + t2k, writes=swk)
            P.op("dve", lambda v: v.tensor_mul(qk, qk, t1), reads=qkk + t1k, writes=qkk)
            P.op("dve", lambda v: v.tensor_add(
                B.arb(QK, 4096, 0, 32), B.arb(QK, 4096, 0, 32), B.arb(SW, 4096, 0, 32)),
                reads=qkk + swk, writes=qkk)
            P.op("dve", lambda v: v.tensor_add(
                B.arb(QK, 4096, 64, 32), B.arb(QK, 4096, 64, 32), B.arb(SW, 4096, 64, 32)),
                reads=qkk + swk, writes=qkk)
            P.dma(chsw, kr, B.arb(QK, 4096, 64, 64), reads=qkk, writes=krk)
            for kt in range(32):
                j = kt % 2
                pst = B.pst[j]
                P.op("pe", lambda t, q=kt, p=pst: t.transpose(
                    p[:, 0:64], vt2[:, q * 128:(q + 1) * 128], B.idn[0:64, 0:64]),
                    reads=vt2k + ("idn",), writes=(f"pstb{j}",))
                if mosa:
                    P.op("dve", lambda v, q=kt, p=pst: v.tensor_scalar(
                        B.arb(VTK2 + q * 33, 64), p[:, 0:64],
                        kmt[:, q * 8 + h:q * 8 + h + 1], None, ALU.mult),
                        reads=(f"pstb{j}",) + kmtk, writes=vtk)
                    P.op("dve", lambda v, q=kt: v.tensor_copy(
                        B.arb(VTK2 + q * 33 + 32, 2)[:, 0:1],
                        kmt[:, q * 8 + h:q * 8 + h + 1]),
                        reads=kmtk, writes=vtk)
                else:
                    P.op("dve", lambda v, q=kt, p=pst: v.tensor_copy(
                        B.arb(VTK2 + q * 33, 64), p[:, 0:64]),
                        reads=(f"pstb{j}",), writes=vtk)
                    if h == 0:
                        P.op("dve", lambda v, q=kt: v.memset(
                            B.arb(VTK2 + q * 33 + 32, 2)[:, 0:1], 1.0), writes=vtk)
            if mosa:
                P.dma(chsw, grow, gat[h:h + 1, :], reads=gatk, writes=growk)
            pair, prow = h // 2, (h % 2) * 64
            for qc in range(8):
                nb, nk = fresh_bank()
                nkt = qc * 4 + 4
                DEPTH = min(4, nkt - 1)
                for kt in range(nkt + DEPTH):
                    if kt < nkt:
                        sb, sk = fresh_bank((nk,))
                        P.op("pe", lambda t, k_=kt, q=qc, b=sb: t.matmul(
                            b[:, :], kr[:, k_ * 128:(k_ + 1) * 128],
                            B.arb(QK, 4096, 0, 64)[:, q * 512:(q + 1) * 512],
                            start=True, stop=True),
                            reads=krk + qkk, writes=(sk,))
                        P.op("act", lambda s2, k_=kt, b=sb: s2.activation(
                            ebuf[k_ % EDEPTH], b[:, :], AF.Exp, scale=SCH),
                            reads=(sk,), writes=ekc[kt % EDEPTH])
                        if kt >= qc * 4:
                            o = kt - qc * 4
                            P.op("dve", lambda v, k_=kt, o2=o: v.tensor_mul(
                                ebuf[k_ % EDEPTH], ebuf[k_ % EDEPTH], tri[:, o2 * 512:(o2 + 1) * 512]),
                                reads=ekc[kt % EDEPTH] + trik, writes=ekc[kt % EDEPTH])
                    if kt >= DEPTH:
                        k0 = kt - DEPTH
                        P.op("pe", lambda t, k_=k0, b=nb, n=nkt: t.matmul(
                            b[0:65, :], B.arb(VTK2 + k_ * 33, 65), ebuf[k_ % EDEPTH],
                            start=(k_ == 0), stop=(k_ == n - 1)),
                            reads=ekc[k0 % EDEPTH] + vtk, writes=(nk,), inc=(k0 == nkt - 1))
                if mosa:
                    P.op("dve", lambda v, b=nb: v.tensor_scalar_max(
                        rrow, b[64:65, :], 1e-30), reads=(nk,), writes=rrk)
                    P.op("dve", lambda v: v.reciprocal(rrow, rrow), reads=rrk, writes=rrk)
                    P.op("dve", lambda v, q=qc: v.tensor_mul(
                        rrow, rrow, grow[0:1, q * 512:(q + 1) * 512]),
                        reads=rrk + growk, writes=rrk)
                else:
                    P.op("dve", lambda v, b=nb: v.reciprocal(rrow, b[64:65, :]),
                         reads=(nk,), writes=rrk)
                bb, bbk2 = fresh_bank((nk,))
                P.op("pe", lambda t, b=bb: t.matmul(
                    b[0:64, :], B.onesrow[0:1, 0:64], rrow, start=True, stop=True),
                    reads=rrk + ("onesrow",), writes=(bbk2,))
                P.op("act", lambda s2, b=bb: s2.activation(rbuf, b[0:64, :], AF.Copy),
                     reads=(bbk2,), writes=rbk)
                P.op("dve", lambda v, b=nb, q=qc, pr=pair, pw=prow: v.tensor_mul(
                    B.arb(ATT + pr * 2048 + q * 256, 512, pw, 64), b[0:64, :], rbuf),
                    reads=(nk,) + rbk, writes=attk(pair))
        if mosa:
            for pr in range(4):
                P.dma(P.chan("l3f_ms", "sp"),
                      bass.AP(mspill.tensor, pr * 128 * T, [[T, 128], [1, T]]),
                      B.arb(ATT + pr * 2048, 4096), reads=attk(pr), writes=("msp",))

    run_branch(aps["l3_mqkv"], True)
    P.dma(P.chan("l3f_wo2", "pool"), dwo.rearrange("p (ct o) -> p ct o", o=1024),
          aps["l3_dwo"].rearrange("(ct p) o -> p ct o", p=128),
          reads=("msp",), writes=dwok)
    run_branch(aps["l3_dqkv"], False)
    for tc in range(8):
        par = tc % 2
        mr = msb[par].rearrange("p (ct c) -> p ct c", c=512)
        P.dma(P.chan(f"l3f_mr{par}", "pool"), mr,
              bass.AP(mspill.tensor, tc * 512, [[T, 128], [128 * T, 4], [1, 512]]),
              reads=("msp",), writes=msbk[par])
        for oc in range(8):
            pb, pk = fresh_bank()
            for ct in range(4):
                P.op("pe", lambda t, c=ct, o=oc, b=pb, m2=mr: t.matmul(
                    b[:, :], mwo[:, c * 1024 + o * 128:c * 1024 + o * 128 + 128],
                    m2[:, c, :], start=(c == 0), stop=False),
                    reads=msbk[par] + mwok, writes=(pk,), inc=False)
            for ct in range(4):
                P.op("pe", lambda t, c=ct, o=oc, q=tc, b=pb: t.matmul(
                    b[:, :], dwo[:, c * 1024 + o * 128:c * 1024 + o * 128 + 128],
                    B.arb(ATT + c * 2048 + q * 256, 512), start=False, stop=(c == 3)),
                    reads=attka + dwok, writes=(pk,), inc=(ct == 3))
            P.op("dve", lambda v, o=oc, q=tc, b=pb: v.tensor_add(
                ssl(B, o, q * 512, 512), ssl(B, o, q * 512, 512), b[:, :]),
                reads=(pk, f"st{oc}"), writes=(f"st{oc}",))
    P.op("dve", lambda v: v.tensor_copy(lo[:, 2:3], lo[:, 0:1]),
         reads=("f3e0", "f3e1", "f3vt") + lok,
         writes=B.keys(VTK2, 1056) + B.keys(EB, 512) + lok)


SLOT_EMIT = {
    "l0_attn": emit_l0_attn,
    "l0_ffn": emit_l0_ffn,
    "l1_attn": emit_l1_attn,
    "l1_ffn": emit_l1_ffn,
    "l2_attn": lambda P, B, aps: emit_l2_attn(P, B, aps, B._l2_spills),
    "l2_ffn": lambda P, B, aps: emit_l2_ffn(P, B, aps, B._l2_xwspill),
    "l3_attn": emit_l3_attn,
    "l3_ffn": lambda P, B, aps: emit_l3_ffn(P, B, aps, B._l3_mspill),
}


DBG = {}


def build_program(slots, shapes):
    nc = bass.Bass("TRN2", target_bir_lowering=False)
    x_hbm = nc.dram_tensor("x", [T, D], F32, kind="ExternalInput").ap()
    y_hbm = nc.dram_tensor("y", [T, D], F32, kind="ExternalOutput").ap()
    need = set(SLOT_INPUTS["prologue"])
    for s in slots:
        if s != "ln":
            need.update(SLOT_INPUTS[s])
    aps = {}
    for name in sorted(need):
        sh, npdt = shapes[name]
        dt = {"float32": F32, "int32": I32}[str(npdt)]
        aps[name] = nc.dram_tensor(name, list(sh), dt, kind="ExternalInput").ap()
    with ExitStack() as ctx:
        P = Prog(nc)
        ctx.enter_context(P.ctx)
        B = Bufs(nc, ctx)
        if "l2_attn" in slots:
            B._l2_spills = (nc.dram_tensor("l2_uspill", [T, 2048], BF16).ap(),
                            nc.dram_tensor("l2_vspill", [T, 2048], BF16).ap())
        if "l2_ffn" in slots:
            B._l2_xwspill = nc.dram_tensor("l2_xwspill", [24 * 128, T], BF16).ap()
        if "l3_ffn" in slots:
            B._l3_mspill = nc.dram_tensor("l3_mspill", [4 * 128, T], BF16).ap()
        emit_prologue(P, B, aps, x_hbm)
        for s in slots:
            if s == "ln":
                emit_ln(P, B)
            else:
                SLOT_EMIT[s](P, B, aps)
        emit_epilogue(P, B, y_hbm)
        P.finalize()
        DBG["P"] = P
        DBG["semmap"] = {k: str(v) for k, v in P.sem.items()}
    return nc, sorted(need)


_PLAN = [["l0_attn", "ln", "l0_ffn", "ln",
          "l1_attn", "ln", "l1_ffn", "ln",
          "l2_attn", "ln", "l2_ffn", "ln",
          "l3_attn", "ln", "l3_ffn", "ln"]]


def kernel(**inputs):
    g = host_prep(inputs)
    shapes = {k: (v.shape, v.dtype) for k, v in g.items()}
    x = np.asarray(inputs["L0_attn_x"], dtype=np.float32)
    cur = [np.ascontiguousarray(x[c]) for c in range(NCORES)]
    for slots in _PLAN:
        nc, names = build_program(slots, shapes)
        in_maps = [dict({n: g[n] for n in names}, x=cur[c]) for c in range(NCORES)]
        res = run_bass_kernel_spmd(nc, in_maps, core_ids=list(range(NCORES)))
        cur = [np.asarray(res.results[c]["y"]) for c in range(NCORES)]
    return np.stack(cur).astype(np.float32)


# version 07 (final)
# speedup vs baseline: 1.2007x; 1.0627x over previous
import numpy as np
import ml_dtypes
from contextlib import ExitStack
import concourse.bass as bass
import concourse.mybir as mybir
from concourse.bass_utils import run_bass_kernel_spmd

F32 = mybir.dt.float32
BF16 = mybir.dt.bfloat16
I32 = mybir.dt.int32
AF = mybir.ActivationFunctionType
ALU = mybir.AluOpType
AX = mybir.AxisListType
BF16NP = ml_dtypes.bfloat16

T = 4096
D = 1024
NDT = 8
NCORES = 8
ARENA_F32 = 34816
LN_OFF = 20480

ENGS = ("pe", "dve", "act", "pool", "sp")


class Prog:
    def __init__(self, nc):
        self.nc = nc
        self.ctx = ExitStack()
        self.eng = {"pe": nc.tensor, "dve": nc.vector, "act": nc.scalar,
                    "pool": nc.gpsimd, "sp": nc.sync}
        self.sem = {}
        self.cnt = {}
        for e in ENGS:
            self.sem[e] = self.ctx.enter_context(nc.semaphore(f"sem_{e}"))
            self.cnt[e] = 0
        self.chan_eng = {}
        self.ch_waited = {}
        self.ch_version = {}
        self.watermark = {e: {} for e in ENGS}
        self.last_write = {}
        self.readers = {}
        self.pending = {e: False for e in ENGS}

    def chan(self, name, eng="pool"):
        key = "ch_" + name
        if key not in self.sem:
            self.sem[key] = self.ctx.enter_context(self.nc.semaphore(key))
            self.cnt[key] = 0
            self.chan_eng[key] = eng
        return key

    def _wait(self, eng, token):
        src, val = token
        if src.startswith("ch_"):
            self.ch_waited[src] = True
        if src == eng:
            if eng == "pe":
                return
            if val > self.cnt[eng]:
                raise RuntimeError(
                    f"same-engine dep on pending op ({eng}); producer must inc")
        if src.startswith("ch_"):
            val = self.cnt[src]
        wm = self.watermark[eng]
        if wm.get(src, 0) >= val:
            return
        self.eng[eng].wait_ge(self.sem[src], val)
        wm[src] = val

    def _deps(self, eng, reads, writes):
        for b in reads:
            t = self.last_write.get(b)
            if t is not None:
                self._wait(eng, t)
        for b in writes:
            t = self.last_write.get(b)
            if t is not None:
                self._wait(eng, t)
            for t in self.readers.get(b, ()):
                self._wait(eng, t)

    def _record(self, token, reads, writes):
        for b in writes:
            self.last_write[b] = token
            self.readers[b] = []
        for b in reads:
            self.readers.setdefault(b, []).append(token)

    def op(self, eng, emit, reads=(), writes=(), inc=True):
        self._deps(eng, reads, writes)
        inst = emit(self.eng[eng])
        if inc:
            inst.then_inc(self.sem[eng], 1)
            self.cnt[eng] += 1
            self.pending[eng] = False
            token = (eng, self.cnt[eng])
        else:
            self.pending[eng] = True
            token = (eng, self.cnt[eng] + 1)
        self._record(token, reads, writes)
        return token

    def _fresh_chan(self, ch):
        if "misc" not in ch:
            return ch
        if not self.ch_waited.get(ch):
            return ch
        base = ch.split("#")[0]
        v = self.ch_version.get(base, 0) + 1
        self.ch_version[base] = v
        eng = self.chan_eng[ch]
        newk = f"{base}#{v}"
        if newk not in self.sem:
            self.sem[newk] = self.ctx.enter_context(self.nc.semaphore(newk))
            self.cnt[newk] = 0
            self.chan_eng[newk] = eng
        return newk

    def dma(self, ch, out_ap, in_ap, reads=(), writes=(), **kw):
        ch = self._fresh_chan(ch)
        eng = self.chan_eng[ch]
        self._deps(eng, reads, writes)
        self.eng[eng].dma_start(out_ap, in_ap, **kw).then_inc(self.sem[ch], 16)
        self.cnt[ch] += 16
        token = (ch, self.cnt[ch])
        self._record(token, reads, writes)
        return token

    def idma(self, ch, out_ap, off_out, in_ap, off_in, reads=(), writes=()):
        ch = self._fresh_chan(ch)
        eng = self.chan_eng[ch]
        self._deps(eng, reads, writes)
        self.eng[eng].indirect_dma_start(out_ap, off_out, in_ap, off_in).then_inc(self.sem[ch], 16)
        self.cnt[ch] += 16
        token = (ch, self.cnt[ch])
        self._record(token, reads, writes)
        return token

    def flush(self, eng, reads=(), writes=()):
        self._deps(eng, reads, writes)

    def finalize(self):
        for e in ENGS:
            if self.pending[e]:
                self.op(e, lambda x: x.nop(), inc=True)
        for key, v in self.cnt.items():
            if key.startswith("ch_") and v > 0:
                self.eng["sp"].wait_ge(self.sem[key], v)
        for e in ENGS:
            if e != "sp" and self.cnt[e] > 0:
                self.eng["sp"].wait_ge(self.sem[e], self.cnt[e])


class Bufs:
    def __init__(self, nc, ctx):
        self.stream = ctx.enter_context(nc.sbuf_tensor("stream", [128, NDT * T], BF16))
        self.arena = ctx.enter_context(nc.sbuf_tensor("arena", [128, ARENA_F32], F32))
        self.arena_b = self.arena[:, :].bitcast(BF16)
        self.idn = ctx.enter_context(nc.sbuf_tensor("idn_sb", [128, 128], BF16))
        self.onesb = ctx.enter_context(nc.sbuf_tensor("onesb", [128, 2], BF16))
        self.onesrow = ctx.enter_context(nc.sbuf_tensor("onesrow", [1, 128], F32))
        self.onesbrow = ctx.enter_context(nc.sbuf_tensor("onesbrow", [1, 128], BF16))
        self.epsc = ctx.enter_context(nc.sbuf_tensor("epsc", [1, 2], F32))
        self.bias = ctx.enter_context(nc.sbuf_tensor("biasarena", [128, 96], F32))
        self.idx32 = ctx.enter_context(nc.sbuf_tensor("idx32", [128, 64], I32))
        self.ps = [ctx.enter_context(nc.psum_tensor(f"psum{i}", [128, 512], F32)) for i in range(6)]
        self.pst = [ctx.enter_context(nc.psum_tensor(f"psumt{i}", [128, 1024], BF16)) for i in range(2)]
        self._bank = 0

    def bank(self):
        i = self._bank
        self._bank = (self._bank + 1) % 6
        return self.ps[i], f"psb{i}"

    def ar(self, off, cols, p0=0, pn=128):
        assert 0 <= off and off + cols <= ARENA_F32, (off, cols)
        return self.arena[p0:p0 + pn, off:off + cols]

    def arb(self, offf32, cols_bf16, p0=0, pn=128):
        off = offf32 * 2
        assert 0 <= off and off + cols_bf16 <= ARENA_F32 * 2, (offf32, cols_bf16)
        return self.arena_b[p0:p0 + pn, off:off + cols_bf16]

    @staticmethod
    def keys(off, cols_f32):
        b0 = off // 2048
        b1 = (off + max(cols_f32, 1) - 1) // 2048
        return tuple(f"blk{i}" for i in range(b0, b1 + 1))


def ssl(B, dt, t0, n):
    return B.stream[:, dt * T + t0:dt * T + t0 + n]


def _pf(v):
    v = np.ascontiguousarray(np.asarray(v), dtype=np.float32)
    n = v.shape[0] // 128
    return np.ascontiguousarray(v.reshape(n, 128).T)


def host_prep(inputs):
    g = {}
    f32 = np.float32
    c = lambda x: np.ascontiguousarray(np.asarray(x), dtype=f32)

    g["l0_wqkv"] = c(inputs["L0_attn_w_qkv"])
    g["l0_bqkv"] = _pf(inputs["L0_attn_b_qkv"])
    g["l0_wproj"] = c(inputs["L0_attn_w_proj"])
    g["l0_bproj"] = _pf(inputs["L0_attn_b_proj"])
    g["l0_g1"] = np.stack([
        c(np.transpose(np.asarray(inputs["L0_ffn_G1_R"])[0], (1, 2, 0))),
        c(np.transpose(np.asarray(inputs["L0_ffn_G1_I"])[0], (1, 2, 0)))])
    g["l0_g2"] = np.stack([
        c(np.transpose(np.asarray(inputs["L0_ffn_G2_R"])[:, :, 0, :], (0, 2, 1))),
        c(np.transpose(np.asarray(inputs["L0_ffn_G2_I"])[:, :, 0, :], (0, 2, 1)))])
    g["l0_fbias"] = _pf(np.concatenate([c(inputs["L0_ffn_bias_real"]),
                                        c(inputs["L0_ffn_bias_imag"])]))
    w = np.asarray(inputs["L1_attn_w_qkv"], dtype=f32).reshape(3, 128, 8, 1024)
    g["l1_wqkv"] = c(np.transpose(w, (3, 0, 2, 1)).reshape(1024, 3072))
    b = np.asarray(inputs["L1_attn_b_qkv"], dtype=f32).reshape(3, 128, 8)
    g["l1_bqkv"] = c(np.transpose(b, (0, 2, 1)).reshape(24, 128).T)
    g["l1_bqkv_row"] = c(np.transpose(b, (0, 2, 1)).reshape(1, 3072))
    g["l1_wo"] = c(np.asarray(inputs["L1_attn_w_o"], dtype=f32).T)
    g["l1_bo"] = _pf(inputs["L1_attn_b_o"])
    g["l1_wg"] = c(inputs["L1_ffn_Wg"])
    g["l1_w1"] = c(inputs["L1_ffn_W1"])
    g["l1_w2"] = c(inputs["L1_ffn_W2"])
    g["l1_w3"] = c(inputs["L1_ffn_W3"])
    g["l2_uvw"] = c(np.asarray(inputs["L2_attn_uv_w"], dtype=f32).T)
    g["l2_uvb"] = _pf(inputs["L2_attn_uv_b"])
    g["l2_lnw"] = _pf(inputs["L2_attn_ln_w"])
    g["l2_lnb"] = _pf(inputs["L2_attn_ln_b"])
    gam = np.asarray(inputs["L2_attn_gamma"], dtype=f32)
    bet = np.asarray(inputs["L2_attn_beta"], dtype=f32)
    g["l2_gb"] = c(np.stack([gam[0], bet[0], gam[1], bet[1]], axis=1))
    g["l2_ow"] = c(np.asarray(inputs["L2_attn_o_w"], dtype=f32).T)
    g["l2_ob"] = _pf(inputs["L2_attn_o_b"])
    w_rel = np.asarray(inputs["L2_attn_w"], dtype=f32)
    wx = w_rel[np.clip(np.arange(-4224, 4736), 0, 1022)]
    g["l2_wxr"] = c(wx[::-1])
    dj = np.arange(128)
    idx = np.zeros((128, 63), np.int32)
    for dd in range(-31, 32):
        idx[:, dd + 31] = 4224 - 128 * dd - dj
    g["l2_bidx"] = idx
    half = 64
    inv = (np.float32(10000.0) ** (np.arange(half, dtype=f32) / np.float32(half))).astype(f32)
    ang = (np.arange(T, dtype=f32)[:, None] * inv[None, :]).astype(f32)
    cosv = np.cos(ang.astype(np.float64)).astype(f32).T
    sinv = np.sin(ang.astype(np.float64)).astype(f32).T
    g["l2_tab1"] = c(np.concatenate([cosv, cosv], axis=0))
    g["l2_tab2s"] = c(np.concatenate([-sinv, sinv], axis=0))
    g["l2_ctxw"] = c(np.asarray(inputs["L2_ffn_ctx_w"], dtype=f32).T)
    g["l2_ctxb"] = _pf(inputs["L2_ffn_ctx_b"])
    g["l2_gih"] = c(np.asarray(inputs["L2_ffn_gru_w_ih"], dtype=f32).T)
    g["l2_ghh"] = c(np.asarray(inputs["L2_ffn_gru_w_hh"], dtype=f32).T)
    g["l2_exprw"] = c(np.asarray(inputs["L2_ffn_expr_w"], dtype=f32).T)
    gw = np.asarray(inputs["L2_ffn_gate_w"], dtype=f32)
    uw = np.asarray(inputs["L2_ffn_up_w"], dtype=f32)
    dw = np.asarray(inputs["L2_ffn_down_w"], dtype=f32)
    g["l2_gate"] = c(np.concatenate([np.transpose(gw, (0, 2, 1)),
                                     np.asarray(inputs["L2_ffn_shared_gate_w"], dtype=f32).T[None]], axis=0))
    g["l2_up"] = c(np.concatenate([np.transpose(uw, (0, 2, 1)),
                                   np.asarray(inputs["L2_ffn_shared_up_w"], dtype=f32).T[None]], axis=0))
    g["l2_down"] = c(np.concatenate([np.transpose(dw, (0, 2, 1)),
                                     np.asarray(inputs["L2_ffn_shared_down_w"], dtype=f32).T[None]], axis=0))
    g["l3_nm1"] = c(inputs["L3_attn_nm_w1"]); g["l3_nm1b"] = _pf(inputs["L3_attn_nm_b1"])
    g["l3_nm2"] = c(inputs["L3_attn_nm_w2"]); g["l3_nm2b"] = _pf(inputs["L3_attn_nm_b2"])
    g["l3_qw"] = c(inputs["L3_attn_q_w"]); g["l3_qb"] = _pf(inputs["L3_attn_q_b"])
    g["l3_kw"] = c(inputs["L3_attn_k_w"]); g["l3_kb"] = _pf(inputs["L3_attn_k_b"])
    g["l3_vw"] = c(inputs["L3_attn_v_w"]); g["l3_vb"] = _pf(inputs["L3_attn_v_b"])
    masks = np.zeros((3, 128, 192), dtype=f32)
    for var, i0 in ((0, 0), (1, 128), (2, T - 128)):
        s0 = max(0, min(i0 - 32, T - 192))
        for di in range(128):
            i = i0 + di
            for w in range(-32, 33):
                j = i + w
                if 0 <= j < T and 0 <= j - s0 < 192:
                    masks[var, di, j - s0] = 1.0
    g["l3_masks"] = c(masks.reshape(3 * 128, 192))
    mq = np.asarray(inputs["L3_ffn_mosa_wqkv"], dtype=f32)
    dq = np.asarray(inputs["L3_ffn_dense_wqkv"], dtype=f32)
    g["l3_mqkv"] = c(np.transpose(mq, (1, 0, 2)))
    g["l3_dqkv"] = c(np.transpose(dq, (1, 0, 2)))
    g["l3_mwo"] = c(inputs["L3_ffn_mosa_wo"])
    g["l3_dwo"] = c(inputs["L3_ffn_dense_wo"])
    g["l3_router"] = c(inputs["L3_ffn_mosa_router_w"])
    halfr = 16
    invr = (np.float32(10000.0) ** (-(np.arange(halfr, dtype=f32)) / np.float32(halfr))).astype(f32)
    angr = (np.arange(T, dtype=f32)[:, None] * invr[None, :]).astype(f32)
    cr = np.cos(angr.astype(np.float64)).astype(f32).T
    sr = np.sin(angr.astype(np.float64)).astype(f32).T
    one = np.ones((32, T), dtype=f32); zer = np.zeros((32, T), dtype=f32)
    g["l3_t1"] = c(np.concatenate([cr, cr, one, cr, cr, one], axis=0))
    g["l3_t2"] = c(np.concatenate([-sr, sr, zer, -sr, sr, zer], axis=0))
    tri = np.zeros((128, 4, 512), dtype=f32)
    dk = np.arange(128)[:, None]; dq = np.arange(512)[None, :]
    for o in range(4):
        tri[:, o, :] = (dk <= dq - o * 128) * 1.0
    g["l3_tri"] = c(tri.reshape(128, 4 * 512))
    eye = np.zeros((128, 64), dtype=f32)
    for e in range(8):
        eye[:, e * 8 + e] = 1.0
    g["l2_eye"] = eye
    bas = np.zeros((8, 1024), dtype=f32)
    for e in range(8):
        bas[e, e * 128:(e + 1) * 128] = 1.0
    g["bas8"] = bas
    g["idn"] = np.eye(128, dtype=f32)
    return g


SLOT_INPUTS = {
    "prologue": ["idn"],
    "l0_attn": ["l0_wqkv", "l0_bqkv", "l0_wproj", "l0_bproj"],
    "l0_ffn": ["l0_g1", "l0_g2", "l0_fbias"],
    "l1_attn": ["l1_wqkv", "l1_bqkv", "l1_bqkv_row", "l1_wo", "l1_bo"],
    "l1_ffn": ["l1_wg", "l1_w1", "l1_w2", "l1_w3", "bas8"],
    "l2_attn": ["l2_uvw", "l2_uvb", "l2_lnw", "l2_lnb", "l2_gb", "l2_ow", "l2_ob",
                "l2_wxr", "l2_bidx", "l2_tab1", "l2_tab2s"],
    "l2_ffn": ["l2_ctxw", "l2_ctxb", "l2_gih", "l2_ghh", "l2_exprw", "l2_gate",
               "l2_up", "l2_down", "l2_eye", "bas8"],
    "l3_attn": ["l3_nm1", "l3_nm1b", "l3_nm2", "l3_nm2b", "l3_qw", "l3_qb",
                "l3_kw", "l3_kb", "l3_vw", "l3_vb", "l3_masks"],
    "l3_ffn": ["l3_mqkv", "l3_dqkv", "l3_mwo", "l3_dwo", "l3_router",
               "l3_t1", "l3_t2", "l3_tri"],
}


def emit_prologue(P, B, gap, x_hbm):
    P.dma(P.chan("pro_misc", "pool"), B.idn[:], gap["idn"], writes=("idn",))
    P.op("dve", lambda v: v.memset(B.onesb[:], 1.0), writes=("onesb",))
    P.op("dve", lambda v: v.memset(B.onesrow[:], 1.0), writes=("onesrow",))
    P.op("dve", lambda v: v.memset(B.onesbrow[:], 1.0), writes=("onesbrow",))
    P.op("dve", lambda v: v.memset(B.epsc[:], 1e-5), writes=("epsc",))
    for tt in range(T // 128):
        i = tt % 2
        stg = B.arb(i * 2048, 1024)
        skey = f"blk{i}"
        P.dma(P.chan(f"xstg{i}", "pool"), stg, x_hbm[tt * 128:(tt + 1) * 128, :], writes=(skey,))
        for dt in range(NDT):
            pk = f"pstb{dt % 2}"
            pst = B.pst[dt % 2]
            P.op("pe", lambda t, s=stg, d=dt, p=pst: t.transpose(
                p[:, 0:128], s[:, d * 128:(d + 1) * 128], B.idn[:]),
                reads=(skey, "idn"), writes=(pk,))
            P.op("dve", lambda v, d=dt, q=tt, p=pst: v.tensor_copy(
                ssl(B, d, q * 128, 128), p[:, 0:128]),
                reads=(pk,), writes=(f"st{dt}",))


def emit_epilogue(P, B, y_hbm):
    for tt in range(T // 128):
        i = tt % 2
        stg = B.ar(i * 2048, 1024)
        skey = f"blk{i}"
        for dt in range(NDT):
            pk = f"pstb{dt % 2}"
            pst = B.pst[dt % 2]
            P.op("pe", lambda t, d=dt, q=tt, p=pst: t.transpose(
                p[:, 0:128], ssl(B, d, q * 128, 128), B.idn[:]),
                reads=(f"st{dt}", "idn"), writes=(pk,))
            P.op("dve", lambda v, d=dt, p=pst, s=stg: v.tensor_copy(
                s[:, d * 128:(d + 1) * 128], p[:, 0:128]),
                reads=(pk,), writes=(skey,))
        P.dma(P.chan(f"ystg{i}", "sp"), y_hbm[tt * 128:(tt + 1) * 128, :], stg, reads=(skey,))


def emit_ln_stats(P, B, off, rboff, mboff, eps=1e-5):
    sq = B.arb(off, 4096)
    sqk = B.keys(off, 2048)
    r1 = off + 2048
    r1k = B.keys(r1, 4096)
    S1 = B.ar(r1, 4096, 0, 1)
    S2 = B.ar(r1, 4096, 32, 1)
    mu = B.ar(r1, 4096, 64, 1)
    r2 = r1 + 4096
    r2k = B.keys(r2, 4096)
    var = B.ar(r2, 4096, 0, 1)
    mu2 = B.ar(r2, 4096, 32, 1)
    bsc = B.ar(r2 + 4096, 512, 0, 1)
    bsck = B.keys(r2 + 4096, 512)
    rb = B.arb(rboff, 4096)
    rbk = B.keys(rboff, 2048)
    mb = B.arb(mboff, 4096)
    mbk = B.keys(mboff, 2048)

    for tc in range(8):
        b1, k1 = B.bank()
        for dt in range(NDT):
            P.op("pe", lambda t, d=dt, q=tc, b=b1: t.matmul(
                b[0:1, :], B.onesb[:, 0:1], ssl(B, d, q * 512, 512),
                start=(d == 0), stop=(d == NDT - 1)),
                reads=(f"st{dt}", "onesb"), writes=(k1,), inc=(dt == NDT - 1))
        P.op("dve", lambda v, q=tc, b=b1: v.tensor_copy(S1[:, q * 512:(q + 1) * 512], b[0:1, :]),
             reads=(k1,), writes=r1k)
        for dt in range(NDT):
            P.op("dve", lambda v, d=dt, q=tc: v.tensor_mul(
                sq[:, d * 512:(d + 1) * 512], ssl(B, d, q * 512, 512), ssl(B, d, q * 512, 512)),
                reads=(f"st{dt}",), writes=sqk)
        b2, k2 = B.bank()
        for dt in range(NDT):
            P.op("pe", lambda t, d=dt, b=b2: t.matmul(
                b[0:1, :], B.onesb[:, 0:1], sq[:, d * 512:(d + 1) * 512],
                start=(d == 0), stop=(d == NDT - 1)),
                reads=sqk + ("onesb",), writes=(k2,), inc=(dt == NDT - 1))
        P.op("dve", lambda v, q=tc, b=b2: v.tensor_copy(S2[:, q * 512:(q + 1) * 512], b[0:1, :]),
             reads=(k2,), writes=r1k)

    inv = 1.0 / D
    P.op("dve", lambda v: v.tensor_scalar_mul(mu, S1, inv), reads=r1k, writes=r1k)
    P.op("dve", lambda v: v.tensor_mul(mu2, mu, mu), reads=r1k, writes=r2k)
    P.op("dve", lambda v: v.scalar_tensor_tensor(var, S2, inv, mu2, ALU.mult, ALU.subtract),
         reads=r1k + r2k, writes=r2k)
    P.op("act", lambda s: s.activation(var, var, AF.Ln, bias=B.epsc[0:1, 0:1]),
         reads=r2k + ("epsc",), writes=r2k)
    P.op("act", lambda s: s.activation(var, var, AF.Exp, scale=-0.5), reads=r2k, writes=r2k)

    for tc in range(8):
        br, krr = B.bank()
        bm, kmm = B.bank()
        P.op("pe", lambda t, q=tc, b=br: t.matmul(
            b[:, :], B.onesrow[0:1, :], var[:, q * 512:(q + 1) * 512], start=True, stop=True),
            reads=r2k + ("onesrow",), writes=(krr,))
        P.op("dve", lambda v, q=tc: v.tensor_copy(bsc, mu[:, q * 512:(q + 1) * 512]),
             reads=r1k, writes=bsck)
        P.op("pe", lambda t, b=bm: t.matmul(
            b[:, :], B.onesrow[0:1, :], bsc, start=True, stop=True),
            reads=bsck + ("onesrow",), writes=(kmm,))
        P.op("act", lambda s, q=tc, b=br: s.activation(rb[:, q * 512:(q + 1) * 512], b[:, :], AF.Identity),
             reads=(krr,), writes=rbk)
        P.op("act", lambda s, q=tc, b=bm: s.activation(mb[:, q * 512:(q + 1) * 512], b[:, :], AF.Identity),
             reads=(kmm,), writes=mbk)
    return (rb, rbk, mb, mbk)


def emit_ln(P, B, eps=1e-5):
    off = LN_OFF
    rboff = off + 10752
    mboff = off
    rb, rbk, mb, mbk = emit_ln_stats(P, B, off, rboff, mboff, eps)
    nsc = B.arb(off + 12800, 1024)
    nsck = B.keys(off + 12800, 512)
    for dt in range(NDT):
        for tc in range(8):
            j = (dt * 8 + tc) % 2
            P.op("dve", lambda v, d=dt, q=tc, jj=j: v.tensor_sub(
                nsc[:, jj * 512:jj * 512 + 512], ssl(B, d, q * 512, 512),
                mb[:, q * 512:(q + 1) * 512]),
                reads=(f"st{dt}",) + mbk, writes=nsck)
            P.op("dve", lambda v, d=dt, q=tc, jj=j: v.tensor_mul(
                ssl(B, d, q * 512, 512), nsc[:, jj * 512:jj * 512 + 512],
                rb[:, q * 512:(q + 1) * 512]),
                reads=nsck + rbk, writes=(f"st{dt}",))


def emit_l0_attn(P, B, aps):
    ATTOUT, QK, VT, V, E, RROW, RB, WH, WP = (0, 16384, 20480, 22528, 23584,
                                              24096, 24608, 24864, 26400)
    attout = B.arb(ATTOUT, 32768)
    atk = lambda pair: B.keys(ATTOUT + pair * 2048, 2048)
    qk = B.arb(QK, 8192, 0, 64)
    qkk = B.keys(QK, 4096)
    vt = B.arb(VT, 4096, 0, 64)
    vtk = B.keys(VT, 2048)
    vre = B.arb(V, 2112)
    vk = B.keys(V, 1056)
    EDEPTH = 6
    ebuf = [B.arb(30496 + i * 256, 512) for i in range(EDEPTH)]
    ek = [(f"l0ae{i}",) for i in range(EDEPTH)]
    rrow = B.ar(RROW, 512, 0, 1)
    rrk = B.keys(RROW, 512)
    rbuf = B.arb(RB, 512, 0, 64)
    rbk = B.keys(RB, 256)
    wh = [B.arb(WH + i * 768, 1536) for i in range(2)]
    whk = [B.keys(WH + i * 768, 768) for i in range(2)]
    wp = B.arb(WP, 8192)
    wpk = B.keys(WP, 4096)

    chm = P.chan("l0a_misc", "pool")
    P.dma(chm, B.bias[:, 0:24], aps["l0_bqkv"], writes=("bias",))
    P.dma(chm, B.bias[:, 24:32], aps["l0_bproj"], writes=("bias",))
    P.dma(chm, wp.rearrange("p (ct o) -> p ct o", o=1024),
          aps["l0_wproj"].rearrange("(ct p) o -> p ct o", p=128), writes=wpk)
    wqkv = aps["l0_wqkv"]

    def load_head_w(h):
        dst = wh[h % 2]
        for comp in range(3):
            src = bass.AP(wqkv.tensor, comp * 1024 + h * 64,
                          [[3072, 128], [3072 * 128, 8], [1, 64]])
            P.dma(P.chan(f"l0a_w{h % 2}", "pool"),
                  dst.rearrange("p (ct c) -> p ct c", c=192)[:, :, comp * 64:(comp + 1) * 64],
                  src, writes=whk[h % 2])

    for tt in range(32):
        col = tt * 66 + 64
        P.op("dve", lambda v, c=col: v.memset(B.arb(V + c // 2, 1), 1.0),
             writes=vk, inc=True)

    load_head_w(0)
    for h in range(16):
        if h + 1 < 16:
            load_head_w(h + 1)
        w = wh[h % 2]
        wk = whk[h % 2]
        bq = B.bias[(h % 2) * 64:(h % 2) * 64 + 64, h // 2:h // 2 + 1]
        bk = B.bias[(h % 2) * 64:(h % 2) * 64 + 64, 8 + h // 2:8 + h // 2 + 1]
        bv = B.bias[(h % 2) * 64:(h % 2) * 64 + 64, 16 + h // 2:16 + h // 2 + 1]
        for comp, dstc, bias_ap in ((0, 0, bq), (1, 4096, bk), (2, None, bv)):
            for tc in range(8):
                pb, pk = B.bank()
                for ct in range(8):
                    P.op("pe", lambda t, c=ct, q=tc, b=pb, cp=comp: t.matmul(
                        b[0:64, :], w.rearrange("p (ct c) -> p ct c", c=192)[:, c, cp * 64:(cp + 1) * 64],
                        ssl(B, c, q * 512, 512), start=(c == 0), stop=(c == 7)),
                        reads=(f"st{ct}",) + wk, writes=(pk,), inc=(ct == 7))
                if comp < 2:
                    P.op("act", lambda s, q=tc, b=pb, dc=dstc, ba=bias_ap: s.activation(
                        qk[:, dc + q * 512:dc + q * 512 + 512], b[0:64, :], AF.Identity, bias=ba),
                        reads=(pk, "bias"), writes=qkk)
                else:
                    P.op("act", lambda s, q=tc, b=pb, ba=bias_ap: s.activation(
                        vt[:, q * 512:q * 512 + 512], b[0:64, :], AF.Identity, bias=ba),
                        reads=(pk, "bias"), writes=vtk)
        for tt in range(32):
            pj = tt % 2
            pst = B.pst[pj]
            P.op("pe", lambda t, q=tt, p=pst: t.transpose(
                p[:, 0:64], vt[:, q * 128:(q + 1) * 128], B.idn[0:64, 0:64]),
                reads=vtk + ("idn",), writes=(f"pstb{pj}",))
            P.op("dve", lambda v, q=tt, p=pst: v.tensor_copy(
                B.arb(V + q * 33, 64), p[:, 0:64]),
                reads=(f"pstb{pj}",), writes=vk)
        pair = h // 2
        prow = (h % 2) * 64
        for tc in range(8):
            nb, nk = B.bank()
            DEPTH = 4
            for kt in range(32 + DEPTH):
                if kt < 32:
                    sb, sk = B.bank()
                    if sk == nk:
                        sb, sk = B.bank()
                    P.op("pe", lambda t, k_=kt, q=tc, b=sb: t.matmul(
                        b[:, :], qk[:, 4096 + k_ * 128:4096 + k_ * 128 + 128],
                        qk[:, q * 512:q * 512 + 512],
                        start=True, stop=True),
                        reads=qkk, writes=(sk,))
                    P.op("act", lambda s, k_=kt, b=sb: s.activation(
                        ebuf[k_ % EDEPTH], b[:, :], AF.Exp, scale=0.125),
                        reads=(sk,), writes=ek[kt % EDEPTH])
                if kt >= DEPTH:
                    k0 = kt - DEPTH
                    P.op("pe", lambda t, k_=k0, b=nb: t.matmul(
                        b[0:65, :], B.arb(V + k_ * 33, 65), ebuf[k_ % EDEPTH],
                        start=(k_ == 0), stop=(k_ == 31)),
                        reads=ek[k0 % EDEPTH] + vk, writes=(nk,), inc=(k0 == 31))
            P.op("dve", lambda v, b=nb: v.reciprocal(rrow, b[64:65, :]), reads=(nk,), writes=rrk)
            bb, bbk = B.bank()
            P.op("pe", lambda t, b=bb: t.matmul(
                b[0:64, :], B.onesrow[0:1, 0:64], rrow, start=True, stop=True),
                reads=rrk + ("onesrow",), writes=(bbk,))
            P.op("act", lambda s, b=bb: s.activation(rbuf, b[0:64, :], AF.Copy),
                 reads=(bbk,), writes=rbk)
            P.op("dve", lambda v, q=tc, b=nb: v.tensor_mul(
                B.arb(ATTOUT + pair * 2048 + q * 256, 512, prow, 64),
                b[0:64, :], rbuf),
                reads=(nk,) + rbk, writes=atk(pair))
    P.op("dve", lambda v: v.tensor_copy(rrow[0:1, 0:1], rrow[0:1, 0:1]),
         reads=tuple(x for t2 in ek for x in t2) + rrk,
         writes=B.keys(30496, 1536) + rrk)
    for tc in range(8):
        for oc in range(8):
            pb, pk = B.bank()
            for ct in range(8):
                P.op("pe", lambda t, c=ct, o=oc, q=tc, b=pb: t.matmul(
                    b[:, :], wp[:, c * 1024 + o * 128:c * 1024 + o * 128 + 128],
                    B.arb(ATTOUT + c * 2048 + q * 256, 512), start=(c == 0), stop=(c == 7)),
                    reads=atk(ct) + wpk, writes=(pk,), inc=(ct == 7))
            P.op("dve", lambda v, o=oc, q=tc, b=pb: v.scalar_tensor_tensor(
                ssl(B, o, q * 512, 512), b[:, :], B.bias[:, 24 + o:25 + o],
                ssl(B, o, q * 512, 512), ALU.add, ALU.add),
                reads=(pk, "bias", f"st{oc}"), writes=(f"st{oc}",))


def emit_l0_ffn(P, B, aps):
    A_, K1, FACC, M, ZST = 0, 128, 160, 672, 4768
    a_r = B.ar(A_, 128)
    ak = B.keys(A_, 128)
    k1 = B.arb(K1, 64)
    k1k = B.keys(K1, 32)
    facc = B.ar(FACC, 512)
    fk = B.keys(FACC, 512)
    m = B.arb(M, 8192)
    mk = lambda ct: B.keys(M + ct * 512, 512)
    mkall = B.keys(M, 4096)
    zst = B.arb(ZST, 4096)
    zk = B.keys(ZST, 2048)

    chw = P.chan("l0f_misc", "pool")
    P.dma(chw, B.bias[:, 32:40], aps["l0_fbias"], writes=("bias",))
    g1, g2 = aps["l0_g1"], aps["l0_g2"]
    chw = P.chan("l0f_rep", "pool")

    for var in range(2):
        for d in range(8):
            P.dma(chw, k1[d * 16:(d + 1) * 16, :].rearrange("g (c e) -> g c e", e=16),
                  g2[var].rearrange("c g e -> g c e"), writes=k1k)
        for pt in range(4):
            for d in range(8):
                P.dma(chw, a_r[d * 16:(d + 1) * 16, :].rearrange("g (c b) -> g c b", b=32),
                      g1[var][:, pt * 8 + d, :].unsqueeze(0).broadcast_to([16, 4, 32]),
                      writes=ak)
            for b_ in range(32):
                for c_ in range(4):
                    if c_ == 0:
                        P.op("dve", lambda v, b2=b_, c2=c_: v.tensor_scalar(
                            facc[:, b2 * 16:(b2 + 1) * 16], k1[:, c2 * 16:(c2 + 1) * 16],
                            a_r[:, c2 * 32 + b2:c2 * 32 + b2 + 1], None, ALU.mult),
                            reads=ak + k1k, writes=fk)
                    else:
                        P.op("dve", lambda v, b2=b_, c2=c_: v.scalar_tensor_tensor(
                            facc[:, b2 * 16:(b2 + 1) * 16], k1[:, c2 * 16:(c2 + 1) * 16],
                            a_r[:, c2 * 32 + b2:c2 * 32 + b2 + 1],
                            facc[:, b2 * 16:(b2 + 1) * 16], ALU.mult, ALU.add),
                            reads=ak + k1k + fk, writes=fk)
            if var == 0:
                P.op("dve", lambda v, p2=pt: v.tensor_copy(
                    m[:, p2 * 1024:p2 * 1024 + 512], facc), reads=fk, writes=mk(p2 if False else pt))
                P.op("dve", lambda v, p2=pt: v.tensor_copy(
                    m[:, (4 + p2) * 1024 + 512:(4 + p2) * 1024 + 1024], facc),
                    reads=fk, writes=mk(4 + pt))
            else:
                P.op("dve", lambda v, p2=pt: v.tensor_copy(
                    m[:, p2 * 1024 + 512:p2 * 1024 + 1024], facc),
                    reads=fk, writes=mk(pt))
                P.op("dve", lambda v, p2=pt: v.tensor_scalar_mul(
                    m[:, (4 + p2) * 1024:(4 + p2) * 1024 + 512], facc, -1.0),
                    reads=fk, writes=mk(4 + pt))
    for tc in range(8):
        for oc in range(8):
            pb, pk = B.bank()
            for ct in range(8):
                P.op("pe", lambda t, c=ct, o=oc, q=tc, b=pb: t.matmul(
                    b[:, :], m[:, c * 1024 + o * 128:c * 1024 + o * 128 + 128],
                    ssl(B, c, q * 512, 512), start=(c == 0), stop=(c == 7)),
                    reads=(f"st{ct}",) + mkall, writes=(pk,), inc=(ct == 7))
            P.op("dve", lambda v, o=oc, q=tc, b=pb: v.scalar_tensor_tensor(
                zst[:, o * 512:(o + 1) * 512], b[:, :], B.bias[:, 32 + o:33 + o],
                ssl(B, o, q * 512, 512), ALU.add, ALU.add),
                reads=(pk, "bias", f"st{oc}"), writes=zk)
        for oc in range(8):
            P.op("dve", lambda v, o=oc, q=tc: v.tensor_copy(
                ssl(B, o, q * 512, 512), zst[:, o * 512:(o + 1) * 512]),
                reads=zk, writes=(f"st{oc}",))


def emit_l1_attn(P, B, aps):
    ATTOUT, QT, KT, VT, G_, WH, WO = 0, 16384, 18432, 20480, 22528, 22592, 25664
    atk = lambda h: B.keys(ATTOUT + h * 2048, 2048)
    qt = B.arb(QT, 4096)
    qtk = B.keys(QT, 2048)
    kt = B.arb(KT, 4096)
    ktk = B.keys(KT, 2048)
    vt = B.arb(VT, 4096)
    vtk = B.keys(VT, 2048)
    gb = B.arb(G_, 128)
    gk = B.keys(G_, 64)
    wh = [B.arb(WH + i * 1536, 3072) for i in range(2)]
    whk = [B.keys(WH + i * 1536, 1536) for i in range(2)]
    wo = B.arb(WO, 8192)
    wok = B.keys(WO, 4096)
    BR = WO + 4096
    brow = [B.arb(BR + i * 128, 256, 0, 1) for i in range(2)]
    browk = [B.keys(BR + i * 128, 128) for i in range(2)]

    chm = P.chan("l1a_misc", "pool")
    P.dma(chm, B.bias[:, 0:24], aps["l1_bqkv"], writes=("bias",))
    P.dma(chm, B.bias[:, 24:32], aps["l1_bo"], writes=("bias",))
    P.dma(chm, wo.rearrange("p (ct o) -> p ct o", o=1024),
          aps["l1_wo"].rearrange("(ct p) o -> p ct o", p=128), writes=wok)
    wqkv = aps["l1_wqkv"]

    def load_head_w(h):
        dst = wh[h % 2]
        for comp in range(3):
            src = bass.AP(wqkv.tensor, comp * 1024 + h * 128,
                          [[3072, 128], [3072 * 128, 8], [1, 128]])
            P.dma(P.chan(f"l1a_w{h % 2}", "pool"),
                  dst.rearrange("p (ct c) -> p ct c", c=384)[:, :, comp * 128:(comp + 1) * 128],
                  src, writes=whk[h % 2])
        P.dma(P.chan(f"l1a_w{h % 2}", "pool"), B.arb(BR + (h % 2) * 128, 128, 0, 1),
              bass.AP(aps["l1_bqkv_row"].tensor, 1024 + h * 128, [[1, 1], [1, 128]]),
              writes=browk[h % 2])
        P.dma(P.chan(f"l1a_w{h % 2}", "pool"), B.arb(BR + (h % 2) * 128 + 64, 128, 0, 1),
              bass.AP(aps["l1_bqkv_row"].tensor, 2048 + h * 128, [[1, 1], [1, 128]]),
              writes=browk[h % 2])

    load_head_w(0)
    for h in range(8):
        if h + 1 < 8:
            load_head_w(h + 1)
        w = wh[h % 2]
        wk = whk[h % 2]
        bq = B.bias[:, h:h + 1]
        bk = B.bias[:, 8 + h:8 + h + 1]
        bv = B.bias[:, 16 + h:16 + h + 1]
        for tc in range(8):
            pb, pk = B.bank()
            for ct in range(8):
                P.op("pe", lambda t, c=ct, q=tc, b=pb: t.matmul(
                    b[:, :], w.rearrange("p (ct c) -> p ct c", c=384)[:, c, 0:128],
                    ssl(B, c, q * 512, 512), start=(c == 0), stop=(c == 7)),
                    reads=(f"st{ct}",) + wk, writes=(pk,), inc=(ct == 7))
            P.op("act", lambda s, q=tc, b=pb: s.activation(
                qt[:, q * 512:q * 512 + 512], b[:, :], AF.Identity, bias=bq),
                reads=(pk, "bias"), writes=qtk)
        for tt in range(32):
            pb, pk = B.bank()
            for ct in range(8):
                P.op("pe", lambda t, c=ct, q=tt, b=pb: t.matmul(
                    b[:, 0:256], ssl(B, c, q * 128, 128),
                    w.rearrange("p (ct c) -> p ct c", c=384)[:, c, 128:384],
                    start=(c == 0), stop=False),
                    reads=(f"st{ct}",) + wk, writes=(pk,), inc=False)
            P.op("pe", lambda t, b=pb, hh=h: t.matmul(
                b[:, 0:256], B.onesbrow[0:1, :], brow[hh % 2], start=False, stop=True),
                reads=("onesbrow",) + browk[h % 2], writes=(pk,), inc=True)
            P.op("act", lambda s, q=tt, b=pb: s.activation(
                kt[:, q * 128:q * 128 + 128], b[:, 0:128], AF.Identity),
                reads=(pk,), writes=ktk)
            P.op("act", lambda s, q=tt, b=pb: s.activation(
                vt[:, q * 128:q * 128 + 128], b[:, 128:256], AF.Identity),
                reads=(pk,), writes=vtk)
        pb, pkk = B.bank()
        for tt in range(32):
            P.op("pe", lambda t, q=tt, b=pb: t.matmul(
                b[:, 0:128], kt[:, q * 128:q * 128 + 128], vt[:, q * 128:q * 128 + 128],
                start=(q == 0), stop=(q == 31)),
                reads=ktk + vtk, writes=(pkk,), inc=(tt == 31))
        P.op("dve", lambda v, b=pb: v.tensor_copy(gb, b[:, 0:128]), reads=(pkk,), writes=gk)
        for tc in range(8):
            pb2, pk2 = B.bank()
            P.op("pe", lambda t, q=tc, b=pb2: t.matmul(
                b[:, :], gb, qt[:, q * 512:q * 512 + 512], start=True, stop=True),
                reads=gk + qtk, writes=(pk2,))
            P.op("dve", lambda v, q=tc, b=pb2, hh=h: v.tensor_copy(
                B.arb(ATTOUT + hh * 2048 + q * 256, 512), b[:, :]),
                reads=(pk2,), writes=atk(h))
    for tc in range(8):
        for oc in range(8):
            pb, pk = B.bank()
            for ct in range(8):
                P.op("pe", lambda t, c=ct, o=oc, q=tc, b=pb: t.matmul(
                    b[:, :], wo[:, c * 1024 + o * 128:c * 1024 + o * 128 + 128],
                    B.arb(ATTOUT + c * 2048 + q * 256, 512), start=(c == 0), stop=(c == 7)),
                    reads=atk(ct) + wok, writes=(pk,), inc=(ct == 7))
            P.op("dve", lambda v, o=oc, q=tc, b=pb: v.scalar_tensor_tensor(
                ssl(B, o, q * 512, 512), b[:, :], B.bias[:, 24 + o:25 + o],
                ssl(B, o, q * 512, 512), ALU.add, ALU.add),
                reads=(pk, "bias", f"st{oc}"), writes=(f"st{oc}",))


def emit_moe_top2(P, B, logit_lhsT_fn, logit_reads, CWSTG, TOPS, LT, scale9=False):
    lt = B.arb(LT, 4096, 0, 8)
    ltk = B.keys(LT, 2048)
    for tc in range(8):
        pb, pk = B.bank()
        for ct in range(NDT):
            P.op("pe", lambda t, c=ct, q=tc, b=pb: t.matmul(
                b[0:8, :], logit_lhsT_fn(c), ssl(B, c, q * 512, 512),
                start=(c == 0), stop=(c == NDT - 1)),
                reads=(f"st{ct}",) + logit_reads, writes=(pk,), inc=(ct == NDT - 1))
        P.op("act", lambda s, q=tc, b=pb: s.activation(
            lt[:, q * 512:(q + 1) * 512], b[0:8, :], AF.Identity),
            reads=(pk,), writes=ltk)
    return emit_top2_from_lt(P, B, lt, ltk, CWSTG, TOPS)


def emit_top2_from_lt(P, B, lt, ltk, CWSTG, TOPS):
    cwstg = B.arb(CWSTG, 4096, 0, 8)
    cwk = B.keys(CWSTG, 2048)
    tf = B.ar(TOPS, 128)
    tfk = B.keys(TOPS, 128)
    ltok = B.arb(TOPS + 100, 8)
    cwb16 = B.arb(TOPS + 108, 8)
    for tt in range(32):
        j = tt % 2
        pst = B.pst[j]
        P.op("pe", lambda t, q=tt, p=pst: t.transpose(
            p[:, 0:8], lt[:, q * 128:(q + 1) * 128], B.idn[0:8, 0:8]),
            reads=ltk + ("idn",), writes=(f"pstb{j}",))
        P.op("dve", lambda v, p=pst: v.tensor_copy(ltok, p[:, 0:8]),
             reads=(f"pstb{j}",), writes=tfk)
        m1 = tf[:, 8:9]; m2 = tf[:, 9:10]; e2 = tf[:, 10:11]
        den = tf[:, 11:12]; w2 = tf[:, 12:13]
        l2 = tf[:, 0:8]; mask1 = tf[:, 16:24]; cwf = tf[:, 24:32]
        P.op("dve", lambda v: v.reduce_max(m1, ltok, axis=AX.X), reads=tfk, writes=tfk)
        P.op("dve", lambda v: v.tensor_tensor(mask1, ltok, m1.to_broadcast([128, 8]), ALU.is_equal),
             reads=tfk, writes=tfk)
        P.op("dve", lambda v: v.scalar_tensor_tensor(l2, mask1, -1e30, ltok, ALU.mult, ALU.add),
             reads=tfk, writes=tfk)
        P.op("dve", lambda v: v.reduce_max(m2, l2, axis=AX.X), reads=tfk, writes=tfk)
        P.op("dve", lambda v: v.tensor_sub(e2, m2, m1), reads=tfk, writes=tfk)
        P.op("act", lambda s: s.activation(e2, e2, AF.Exp), reads=tfk, writes=tfk)
        P.op("dve", lambda v: v.tensor_scalar_add(den, e2, 1.0), reads=tfk, writes=tfk)
        P.op("dve", lambda v: v.reciprocal(den, den), reads=tfk, writes=tfk)
        P.op("dve", lambda v: v.tensor_mul(w2, e2, den), reads=tfk, writes=tfk)
        P.op("dve", lambda v: v.tensor_scalar(cwf, mask1, den, None, ALU.mult),
             reads=tfk, writes=tfk)
        P.op("dve", lambda v: v.tensor_tensor(l2, l2, m2.to_broadcast([128, 8]), ALU.is_equal),
             reads=tfk, writes=tfk)
        P.op("dve", lambda v: v.tensor_scalar(l2, l2, w2, None, ALU.mult),
             reads=tfk, writes=tfk)
        P.op("dve", lambda v: v.tensor_add(cwf, cwf, l2), reads=tfk, writes=tfk)
        P.op("dve", lambda v: v.tensor_copy(cwb16, cwf), reads=tfk, writes=tfk)
        pst2 = B.pst[(tt + 1) % 2]
        P.op("pe", lambda t, p=pst2: t.transpose(p[0:8, 128:256], cwb16, B.idn[:]),
             reads=tfk + ("idn",), writes=(f"pstb{(tt + 1) % 2}",))
        P.op("dve", lambda v, q=tt, p=pst2: v.tensor_copy(
            cwstg[:, q * 128:(q + 1) * 128], p[0:8, 128:256]),
            reads=(f"pstb{(tt + 1) % 2}",), writes=cwk)
    return cwstg, cwk


def emit_l1_ffn(P, B, aps):
    Y, H_, W12, W3F, CWSTG, CWB, BAS, TOPS = (0, 8192, 16384, 20480, 28672,
                                              30720, 31232, 31744)
    yk = lambda oc: B.keys(Y + oc * 1024, 1024)
    ykall = B.keys(Y, 8192)
    hk = B.keys(H_, 8192)
    w12 = [(B.arb(W12 + i * 2048, 2048), B.arb(W12 + i * 2048 + 1024, 2048)) for i in range(2)]
    w12k = [B.keys(W12 + i * 2048, 2048) for i in range(2)]
    w3 = B.arb(W3F, 16384)
    w3k = B.keys(W3F, 8192)
    cwb = B.arb(CWB, 1024)
    cwbk = B.keys(CWB, 512)
    bas = B.arb(BAS, 1024, 0, 8)
    bask = B.keys(BAS, 512)

    chm = P.chan("l1f_misc", "pool")
    P.dma(chm, bas, aps["bas8"][0:8, :], writes=bask)
    wg = aps["l1_wg"]

    wgs = B.arb(TOPS + 128, 64)
    wgk = B.keys(TOPS + 128, 32)
    P.dma(chm, wgs.rearrange("p (ct c) -> p ct c", c=8),
          wg.rearrange("(ct p) c -> p ct c", p=128), writes=wgk)
    cwstg, cwk = emit_moe_top2(
        P, B, lambda ct: wgs[:, ct * 8:(ct + 1) * 8], wgk, CWSTG, TOPS, Y)

    emit_moe_mlp(P, B, "l1f", aps["l1_w1"], aps["l1_w2"], aps["l1_w3"], 8,
                 cwstg, cwk, (Y, H_, W12, W3F, CWB), bas, bask)


def emit_moe_mlp(P, B, tag, W1, W2, W3, nE, cwstg, cwk, offs, bas, bask):
    Y, H_, W12, W3F, CWB = offs
    yk = lambda oc: B.keys(Y + oc * 1024, 1024)
    ykall = B.keys(Y, 8192)
    hk = B.keys(H_, 8192)
    w12 = [(B.arb(W12 + i * 2048, 2048), B.arb(W12 + i * 2048 + 1024, 2048)) for i in range(2)]
    w12k = [B.keys(W12 + i * 2048, 2048) for i in range(2)]
    w3 = B.arb(W3F, 16384)
    w3k = B.keys(W3F, 8192)
    cwb = B.arb(CWB, 1024)
    cwbk = B.keys(CWB, 512)

    def load_w12(e, hc):
        par = hc % 2
        dsts = w12[par]
        for wi, W in ((0, W1), (1, W2)):
            src = bass.AP(W.tensor, e * 1024 * 2048 + hc * 256,
                          [[2048, 128], [2048 * 128, 8], [1, 256]])
            P.dma(P.chan(f"{tag}_w{par}", "pool"),
                  dsts[wi].rearrange("p (ct c) -> p ct c", c=256), src,
                  writes=w12k[par])

    def load_w3(e):
        src = bass.AP(W3.tensor, e * 2048 * 1024,
                      [[1024, 128], [1024 * 128, 16], [1, 1024]])
        P.dma(P.chan(f"{tag}_w3", "pool"),
              w3.rearrange("p (ht c) -> p ht c", c=1024), src, writes=w3k)

    for q4 in range(4):
        t0 = q4 * 1024
        for e in range(nE):
            shared = (e == nE - 1 and nE == 9)
            if not shared:
                for tc in range(2):
                    pb, pk = B.bank()
                    P.op("pe", lambda t, q=tc, b=pb, ee=e: t.matmul(
                        b[:, :], bas[:, ee * 128:(ee + 1) * 128],
                        cwstg[:, t0 + q * 512:t0 + q * 512 + 512], start=True, stop=True),
                        reads=bask + cwk, writes=(pk,))
                    P.op("act", lambda s, q=tc, b=pb: s.activation(
                        cwb[:, q * 512:(q + 1) * 512], b[:, :], AF.Identity),
                        reads=(pk,), writes=cwbk)
            load_w3(e)
            for hc in range(8):
                load_w12(e, hc)
                par = hc % 2
                for ot in range(2):
                    ht = hc * 2 + ot
                    for tc in range(2):
                        pbu, pku = B.bank()
                        for ct in range(8):
                            P.op("pe", lambda t, c=ct, o=ot, q=tc, b=pbu: t.matmul(
                                b[:, :], w12[par][0].rearrange("p (ct c) -> p ct c", c=256)[:, c, o * 128:(o + 1) * 128],
                                ssl(B, c, t0 + q * 512, 512), start=(c == 0), stop=(c == 7)),
                                reads=(f"st{ct}",) + w12k[par], writes=(pku,), inc=(ct == 7))
                        P.op("act", lambda s, b=pbu, h2=ht, q=tc: s.activation(
                            B.arb(H_ + h2 * 512 + q * 256, 512), b[:, :], AF.Silu),
                            reads=(pku,), writes=hk)
                        pbv, pkv = B.bank()
                        for ct in range(8):
                            P.op("pe", lambda t, c=ct, o=ot, q=tc, b=pbv: t.matmul(
                                b[:, :], w12[par][1].rearrange("p (ct c) -> p ct c", c=256)[:, c, o * 128:(o + 1) * 128],
                                ssl(B, c, t0 + q * 512, 512), start=(c == 0), stop=(c == 7)),
                                reads=(f"st{ct}",) + w12k[par], writes=(pkv,), inc=(ct == 7))
                        P.op("dve", lambda v, b=pbv, h2=ht, q=tc: v.tensor_mul(
                            B.arb(H_ + h2 * 512 + q * 256, 512),
                            B.arb(H_ + h2 * 512 + q * 256, 512), b[:, :]),
                            reads=(pkv,) + hk, writes=hk)
                        if not shared:
                            P.op("dve", lambda v, h2=ht, q=tc: v.tensor_mul(
                                B.arb(H_ + h2 * 512 + q * 256, 512),
                                B.arb(H_ + h2 * 512 + q * 256, 512),
                                cwb[:, q * 512:(q + 1) * 512]),
                                reads=hk + cwbk, writes=hk)
            for tch in range(4):
                for oc in range(8):
                    pb = B.ps[oc // 2]
                    pk = f"psb{oc // 2}"
                    sub = (oc % 2) * 256
                    for ht in range(16):
                        P.op("pe", lambda t, h2=ht, o=oc, q=tch, b=pb, sb=sub: t.matmul(
                            b[:, sb:sb + 256],
                            w3.rearrange("p (ht c) -> p ht c", c=1024)[:, h2, o * 128:(o + 1) * 128],
                            B.arb(H_ + h2 * 512 + q * 128, 256), start=(h2 == 0), stop=(h2 == 15)),
                            reads=hk + w3k, writes=(pk,), inc=(ht == 15))
                    if e == 0:
                        P.op("dve", lambda v, o=oc, q=tch, b=pb, sb=sub: v.tensor_copy(
                            B.ar(Y + o * 1024 + q * 256, 256), b[:, sb:sb + 256]),
                            reads=(pk,), writes=yk(oc))
                    else:
                        P.op("dve", lambda v, o=oc, q=tch, b=pb, sb=sub: v.tensor_add(
                            B.ar(Y + o * 1024 + q * 256, 256),
                            B.ar(Y + o * 1024 + q * 256, 256), b[:, sb:sb + 256]),
                            reads=(pk,) + yk(oc), writes=yk(oc))
        for oc in range(8):
            for tc in range(2):
                P.op("dve", lambda v, o=oc, q=tc: v.tensor_add(
                    ssl(B, o, t0 + q * 512, 512), ssl(B, o, t0 + q * 512, 512),
                    B.ar(Y + o * 1024 + q * 512, 512)),
                    reads=ykall + (f"st{oc}",), writes=(f"st{oc}",))


def emit_l2_attn(P, B, aps, spills):
    XN, WT, QT, KT = 0, 16384, 17408, 19456
    TAB1, TAB2 = 21504, 23552
    UBUF, TOKS = 25600, 27648
    RB, MB = 29696, 31744
    BSC2 = 33792
    BT, OWT, KERNT = 0, 8064, 21504
    V0, V1, UTOK, ZR, ZT, BIDX = 23552, 24576, 25600, 26624, 27648, 28928

    xn = B.arb(XN, 32768)
    xnk = lambda dt: B.keys(XN + dt * 2048, 2048)
    xnka = B.keys(XN, 16384)
    wt = [B.arb(WT + i * 512, 1024) for i in range(2)]
    wtk = [B.keys(WT + i * 512, 512) for i in range(2)]
    qt = B.arb(QT, 4096)
    qtk = B.keys(QT, 2048)
    kt = B.arb(KT, 4096)
    ktk = B.keys(KT, 2048)
    tab1 = B.arb(TAB1, 4096)
    t1k = B.keys(TAB1, 2048)
    tab2 = B.arb(TAB2, 4096)
    t2k = B.keys(TAB2, 2048)
    ubuf = B.arb(UBUF, 4096)
    ubk = B.keys(UBUF, 2048)
    toks = B.arb(TOKS, 4096)
    tok = B.keys(TOKS, 2048)

    chm = P.chan("l2a_misc", "pool")
    P.dma(chm, B.bias[:, 0:8], aps["l2_lnw"], writes=("bias",))
    P.dma(chm, B.bias[:, 8:16], aps["l2_lnb"], writes=("bias",))
    P.dma(chm, B.bias[:, 16:20], aps["l2_gb"], writes=("bias",))
    P.dma(chm, B.bias[:, 20:28], aps["l2_ob"], writes=("bias",))
    P.dma(chm, B.bias[:, 28:61], aps["l2_uvb"], writes=("bias",))
    P.dma(chm, tab1, aps["l2_tab1"], writes=t1k)
    P.dma(chm, tab2, aps["l2_tab2s"], writes=t2k)

    rb, rbk, mb, mbk = emit_ln_stats(P, B, 0, RB, MB)

    nsc = B.arb(BSC2, 512)
    nsck = B.keys(BSC2, 256)
    for dt in range(NDT):
        for tc in range(8):
            P.op("dve", lambda v, d=dt, q=tc: v.tensor_sub(
                nsc, ssl(B, d, q * 512, 512), mb[:, q * 512:(q + 1) * 512]),
                reads=(f"st{dt}",) + mbk, writes=nsck)
            P.op("dve", lambda v, d=dt, q=tc: v.tensor_mul(
                nsc, nsc, rb[:, q * 512:(q + 1) * 512]),
                reads=nsck + rbk, writes=nsck)
            P.op("dve", lambda v, d=dt, q=tc: v.tensor_scalar(
                xn[:, d * 4096 + q * 512:d * 4096 + q * 512 + 512], nsc,
                B.bias[:, d:d + 1], B.bias[:, 8 + d:9 + d], ALU.mult, ALU.add),
                reads=nsck + ("bias",), writes=xnk(dt)),
    uvw = aps["l2_uvw"]

    def load_wtile(oc, par):
        src = bass.AP(uvw.tensor, oc * 128,
                      [[4224, 128], [4224 * 128, 8], [1, 128]])
        P.dma(P.chan(f"l2a_w{par}", "pool"),
              wt[par].rearrange("p (ct c) -> p ct c", c=128), src, writes=wtk[par])

    uspill, vspill = spills
    load_wtile(0, 0)
    for ui, spill in ((0, uspill), (1, vspill)):
        for oc in range(16):
            gl = ui * 16 + oc
            par = gl % 2
            nxt = gl + 1
            if nxt < 32:
                load_wtile(nxt, nxt % 2)
            bias_col = B.bias[:, 28 + ui * 16 + oc:29 + ui * 16 + oc]
            for tc in range(8):
                pb, pk = B.bank()
                for ct in range(8):
                    P.op("pe", lambda t, c=ct, q=tc, b=pb, pr=par: t.matmul(
                        b[:, :], wt[pr].rearrange("p (ct c) -> p ct c", c=128)[:, c, :],
                        xn[:, c * 4096 + q * 512:c * 4096 + q * 512 + 512],
                        start=(c == 0), stop=(c == 7)),
                        reads=xnk(ct) + wtk[par], writes=(pk,), inc=(ct == 7))
                P.op("act", lambda s, q=tc, b=pb, bc=bias_col: s.activation(
                    ubuf[:, q * 512:(q + 1) * 512], b[:, :], AF.Silu, bias=bc),
                    reads=(pk, "bias"), writes=ubk)
            for tt in range(32):
                j = tt % 2
                pst = B.pst[j]
                P.op("pe", lambda t, q=tt, p=pst: t.transpose(
                    p[:, 0:128], ubuf[:, q * 128:(q + 1) * 128], B.idn[:]),
                    reads=ubk + ("idn",), writes=(f"pstb{j}",))
                P.op("dve", lambda v, q=tt, p=pst: v.tensor_copy(
                    toks[:, q * 128:(q + 1) * 128], p[:, 0:128]),
                    reads=(f"pstb{j}",), writes=tok)
            P.dma(P.chan(f"l2a_sp{par}", "pool"),
                  bass.AP(spill.tensor, oc * 128, [[2048, 128], [2048 * 128, 32], [1, 128]]),
                  toks.rearrange("p (tt c) -> p tt c", c=128), reads=tok,
                  writes=("l2a_uvsp",))
    load_wtile(32, 0)
    bch = B.arb(BSC2, 512)
    bchk = B.keys(BSC2, 256)
    pre = B.arb(BSC2 + 256, 512)
    prek = B.keys(BSC2 + 256, 256)
    sw = B.arb(BSC2 + 512, 512)
    swk = B.keys(BSC2 + 512, 256)
    ta = B.arb(BSC2 + 768, 512)
    tak = B.keys(BSC2 + 768, 256)
    for tc in range(8):
        pb, pk = B.bank()
        for ct in range(8):
            P.op("pe", lambda t, c=ct, q=tc, b=pb: t.matmul(
                b[:, :], wt[0].rearrange("p (ct c) -> p ct c", c=128)[:, c, :],
                xn[:, c * 4096 + q * 512:c * 4096 + q * 512 + 512],
                start=(c == 0), stop=(c == 7)),
                reads=xnk(ct) + wtk[0], writes=(pk,), inc=(ct == 7))
        P.op("act", lambda s, b=pb: s.activation(
            bch, b[:, :], AF.Silu, bias=B.bias[:, 60:61]),
            reads=(pk, "bias"), writes=bchk)
        for br, dst, dk in ((0, qt, qtk), (1, kt, ktk)):
            gcol = B.bias[:, 16 + 2 * br:17 + 2 * br]
            bcol = B.bias[:, 17 + 2 * br:18 + 2 * br]
            P.op("dve", lambda v, g2=gcol, b2=bcol: v.tensor_scalar(
                pre, bch, g2, b2, ALU.mult, ALU.add),
                reads=bchk + ("bias",), writes=prek)
            P.op("dve", lambda v: v.tensor_copy(sw[0:64, :], pre[64:128, :]),
                 reads=prek, writes=swk)
            P.op("dve", lambda v: v.tensor_copy(sw[64:128, :], pre[0:64, :]),
                 reads=prek, writes=swk)
            P.op("dve", lambda v, q=tc: v.tensor_mul(
                ta, pre, tab1[:, q * 512:(q + 1) * 512]),
                reads=prek + t1k, writes=tak)
            P.op("dve", lambda v, q=tc: v.tensor_mul(
                sw, sw, tab2[:, q * 512:(q + 1) * 512]),
                reads=swk + t2k, writes=swk)
            P.op("dve", lambda v, d2=dst, q=tc: v.tensor_add(
                d2[:, q * 512:(q + 1) * 512], ta, sw),
                reads=tak + swk, writes=dk)
    bidx = B.idx32[:, 0:63]
    bxk = ("idx32",)
    P.dma(chm, bidx, aps["l2_bidx"], writes=bxk)
    btk = B.keys(BT, 8064)
    wxr = aps["l2_wxr"].rearrange("(a b) -> a b", b=1)
    for dd in range(63):
        P.idma(P.chan("l2a_bt", "pool"), B.ar(BT + dd * 128, 128), None,
               wxr, bass.IndirectOffsetOnAxis(ap=bidx[:, dd:dd + 1], axis=0),
               reads=bxk, writes=btk)
    owt = B.arb(OWT, 16384)
    owk = B.keys(OWT, 8192)
    P.dma(chm, owt.rearrange("p (et o) -> p et o", o=1024),
          aps["l2_ow"].rearrange("(et p) o -> p et o", p=128), writes=owk)
    kern = B.arb(KERNT, 4096)
    kkk = B.keys(KERNT, 2048)
    vb4 = [B.arb(V0 + i * 512, 1024) for i in range(4)]
    vb4k = [(f"l2av{i}",) for i in range(4)]
    ut = B.arb(UTOK, 2048)
    utk = B.keys(UTOK, 1024)
    zr = B.arb(ZR, 2048)
    zrk = B.keys(ZR, 1024)
    zt = B.arb(ZT, 2048)
    ztk = B.keys(ZT, 1024)
    ksc = B.ar(ZR, 128)
    ksck = zrk
    SC = float(1.0 / np.sqrt(128.0))
    kernb = [kern, B.arb(29696, 4096)]
    kkkb = [kkk, B.keys(29696, 2048)]
    ksc2 = B.ar(31744, 128)
    ksc2k = B.keys(31744, 128)
    utb = [ut, B.arb(31872, 2048)]
    utbk = [utk, B.keys(31872, 1024)]
    loc = [(B.ps[4], "psb4"), (B.ps[5], "psb5")]

    def scores_phase(it):
        kern_, kk_ = kernb[it % 2], kkkb[it % 2]
        for jt in range(32):
            dd = jt - it + 31
            pb, pk = loc[jt % 2]
            P.op("pe", lambda t, j=jt, i2=it, b=pb: t.matmul(
                b[:, 0:128], kt[:, j * 128:(j + 1) * 128], qt[:, i2 * 128:(i2 + 1) * 128],
                start=True, stop=True),
                reads=qtk + ktk, writes=(pk,))
            P.op("dve", lambda v, b=pb, d2=dd: v.scalar_tensor_tensor(
                ksc2, b[:, 0:128], SC, B.ar(BT + d2 * 128, 128), ALU.mult, ALU.add),
                reads=(pk,) + btk, writes=ksc2k)
            P.op("dve", lambda v: v.tensor_scalar_max(ksc2, ksc2, 0.0),
                 reads=ksc2k, writes=ksc2k)
            P.op("dve", lambda v, j=jt, kn=kern_: v.tensor_mul(
                kn[:, j * 128:(j + 1) * 128], ksc2, ksc2),
                reads=ksc2k, writes=kk_)

    def kernv_phase(it):
        kern_, kk_ = kernb[it % 2], kkkb[it % 2]
        P.dma(P.chan(f"l2a_ut{it % 2}", "pool"), utb[it % 2],
              uspill[it * 128:(it + 1) * 128, :],
              reads=("l2a_uvsp",), writes=utbk[it % 2])
        for jt in range(32):
            for half in range(2):
                sl4 = (jt * 2 + half) % 4
                P.dma(P.chan(f"l2a_v{sl4}", "pool"), vb4[sl4],
                      vspill[jt * 128:(jt + 1) * 128, half * 1024:(half + 1) * 1024],
                      reads=("l2a_uvsp",), writes=vb4k[sl4])
                for ec2 in range(2):
                    ec = half * 2 + ec2
                    pbank = B.ps[ec]
                    P.op("pe", lambda t, j=jt, e2=ec2, e=ec, b=pbank, s4=sl4, kn=kern_: t.matmul(
                        b[:, :], kn[:, j * 128:(j + 1) * 128],
                        vb4[s4][:, e2 * 512:(e2 + 1) * 512],
                        start=(j == 0), stop=(j == 31)),
                        reads=kk_ + vb4k[sl4], writes=(f"psb{ec}",),
                        inc=(ec2 == 1))

    def epilogue_phase(it):
        for ec in range(4):
            P.op("dve", lambda v, e=ec, i2=it: v.tensor_mul(
                zr[:, e * 512:(e + 1) * 512], utb[i2 % 2][:, e * 512:(e + 1) * 512],
                B.ps[e][:, :]),
                reads=(f"psb{ec}", ) + utbk[it % 2], writes=zrk)
        for et in range(16):
            j = et % 2
            pst = B.pst[j]
            P.op("pe", lambda t, e=et, p=pst: t.transpose(
                p[:, 0:128], zr[:, e * 128:(e + 1) * 128], B.idn[:]),
                reads=zrk + ("idn",), writes=(f"pstb{j}",))
            P.op("dve", lambda v, e=et, p=pst: v.tensor_copy(
                zt[:, e * 128:(e + 1) * 128], p[:, 0:128]),
                reads=(f"pstb{j}",), writes=ztk)
        for oc in range(8):
            pb, pk = loc[oc % 2]
            for et in range(16):
                P.op("pe", lambda t, e=et, o=oc, b=pb: t.matmul(
                    b[:, 0:128], owt.rearrange("p (et o) -> p et o", o=1024)[:, e, o * 128:(o + 1) * 128],
                    zt[:, e * 128:(e + 1) * 128], start=(e == 0), stop=(e == 15)),
                    reads=ztk + owk, writes=(pk,), inc=(et == 15))
            P.op("dve", lambda v, o=oc, i2=it, b=pb: v.scalar_tensor_tensor(
                ksc, ssl(B, o, i2 * 128, 128), 2.0, b[:, 0:128],
                ALU.mult, ALU.add),
                reads=(pk, f"st{oc}") + ksck, writes=ksck)
            P.op("dve", lambda v, o=oc, i2=it: v.tensor_scalar(
                ssl(B, o, i2 * 128, 128), ksc, B.bias[:, 20 + o:21 + o], None,
                ALU.add),
                reads=ksck + ("bias",), writes=(f"st{oc}",))

    scores_phase(0)
    for it in range(33):
        if it + 1 < 32:
            scores_phase(it + 1)
        if it >= 1:
            epilogue_phase(it - 1)
        if it < 32:
            kernv_phase(it)
    P.op("dve", lambda v: v.tensor_copy(ksc2[0:1, 0:1], ksc2[0:1, 0:1]),
         reads=tuple(x for t2 in vb4k for x in t2) + ksc2k,
         writes=B.keys(V0, 2048) + ksc2k)


def emit_l2_ffn(P, B, aps, xwspill):
    import concourse.mybir as _mb
    HS, WHH, XWB = 0, 16384, 28672
    HW, RZ, SCR = 30208, 30240, 30256
    H0A, H0B, XM, XMB = 30264, 30268, 30272, 30280
    XWROW, WT = 30720, 32768
    CTXW = 0
    hs = B.arb(HS, 32768)
    hsk = B.keys(HS, 16384)
    whh = B.arb(WHH, 24576)
    whhk = B.keys(WHH, 12288)
    xwb = B.arb(XWB, 3072)
    hw = B.ar(HW, 24)
    rz = B.ar(RZ, 16)
    scr = B.ar(SCR, 8)
    h0a = B.arb(H0A, 8)
    h0b = B.arb(H0B, 8)
    xm = B.ar(XM, 8)
    xmb = B.arb(XMB, 8)
    smallk = B.keys(HW, 100)
    xwrow = B.arb(XWROW, 4096)
    xwrk = B.keys(XWROW, 2048)
    wt = [B.arb(WT + i * 512, 1024) for i in range(2)]
    wtk = [B.keys(WT + i * 512, 512) for i in range(2)]

    chm = P.chan("l2f_misc", "pool")
    P.dma(chm, B.bias[:, 0:8], aps["l2_ctxb"], writes=("bias",))
    for dt in range(NDT):
        P.op("dve", lambda v, d=dt: v.reduce_sum(xm[:, d:d + 1], ssl(B, d, 0, T), axis=AX.X),
             reads=(f"st{dt}",), writes=smallk)
    P.op("dve", lambda v: v.tensor_scalar_mul(xmb, xm, 1.0 / T), reads=smallk, writes=smallk)
    ctxw = B.arb(CTXW, 8192)
    ctxwk = B.keys(CTXW, 4096)
    P.dma(chm, ctxw.rearrange("p (ct o) -> p ct o", o=1024),
          aps["l2_ctxw"].rearrange("(ct p) o -> p ct o", p=128), writes=ctxwk)
    pb0, pk0 = B.bank()
    for ot in range(8):
        for ct in range(8):
            P.op("pe", lambda t, o=ot, c=ct, b=pb0: t.matmul(
                b[:, o:o + 1], ctxw[:, c * 1024 + o * 128:c * 1024 + o * 128 + 128],
                xmb[:, c:c + 1], start=(c == 0), stop=(c == 7)),
                reads=ctxwk + smallk, writes=(pk0,), inc=(ot == 7 and ct == 7))
    P.op("dve", lambda v, b=pb0: v.tensor_tensor(h0a, b[:, 0:8], B.bias[:, 0:8], ALU.add),
         reads=(pk0, "bias"), writes=smallk)
    def load_gih(ot, par):
        src = bass.AP(aps["l2_gih"].tensor, ot * 128,
                      [[3072, 128], [3072 * 128, 8], [1, 128]])
        P.dma(P.chan(f"l2f_g{par}", "pool"),
              wt[par].rearrange("p (ct c) -> p ct c", c=128), src, writes=wtk[par])
    load_gih(0, 0)
    for ot in range(24):
        par = ot % 2
        if ot + 1 < 24:
            load_gih(ot + 1, (ot + 1) % 2)
        for tc in range(8):
            pb, pk = B.bank()
            for ct in range(8):
                P.op("pe", lambda t, c=ct, q=tc, b=pb, pr=par: t.matmul(
                    b[:, :], wt[pr].rearrange("p (ct c) -> p ct c", c=128)[:, c, :],
                    ssl(B, c, q * 512, 512), start=(c == 0), stop=(c == 7)),
                    reads=(f"st{ct}",) + wtk[par], writes=(pk,), inc=(ct == 7))
            P.op("act", lambda s, q=tc, b=pb: s.activation(
                xwrow[:, q * 512:(q + 1) * 512], b[:, :], AF.Identity),
                reads=(pk,), writes=xwrk)
        P.dma(P.chan(f"l2f_xw{par}", "sp"),
              bass.AP(xwspill.tensor, ot * 128 * T, [[T, 128], [1, T]]),
              xwrow, reads=xwrk, writes=("xwsp",))
    P.dma(chm, whh.rearrange("p (it o) -> p it o", o=3072),
          aps["l2_ghh"].rearrange("(it p) o -> p it o", p=128), writes=whhk)
    P.flush("pe", reads=whhk + smallk, writes=("psb5", "psb4"))
    P.flush("dve", reads=smallk, writes=B.keys(HW, 100))
    P.flush("act", reads=B.keys(HW, 100))
    P.flush("pool", reads=("xwsp",))
    nc = P.nc
    gp = P.ctx.enter_context(nc.semaphore("gru_pe"))
    gd = P.ctx.enter_context(nc.semaphore("gru_dv"))
    ga = P.ctx.enter_context(nc.semaphore("gru_ac"))
    chx = P.ctx.enter_context(nc.semaphore("gru_chx"))
    t_e, v_e, s_e, p_e = nc.tensor, nc.vector, nc.scalar, nc.gpsimd
    ET = _mb.EngineType
    psb = B.ps[5]
    psb2 = B.ps[4]
    hsv = hs.rearrange("p (ot t) -> p ot t", t=T)
    xwv = xwb.rearrange("p (ot ts) -> p ot ts", ts=128)
    hbufs = [h0a, h0b]
    with nc.Fori(0, 32, engines=(ET.PE, ET.DVE, ET.Activation, ET.Pool)) as io:
        p_e.wait_ge(gd, io * 1024)
        p_e.dma_start(
            xwb.rearrange("p (ot ts) -> p ot ts", ts=128),
            bass.AP(xwspill.tensor, io * 128,
                    [[T, 128], [128 * T, 24], [1, 128]])).then_inc(chx, 16)
        t_e.wait_ge(chx, (io + 1) * 16)
        with nc.Fori(0, 64, engines=(ET.PE, ET.DVE, ET.Activation)) as ii:
            for sj in range(2):
                hin = hbufs[sj % 2]
                hout = hbufs[1 - sj % 2]
                t_e.wait_ge(gd, io * 1024 + ii * 16 + sj * 8)
                for ot in range(24):
                    pb_ = psb if ot < 16 else psb2
                    oc_ = ot if ot < 16 else ot - 16
                    for it in range(8):
                        mm = t_e.matmul(pb_[:, oc_:oc_ + 1],
                                        whh[:, it * 3072 + ot * 128:it * 3072 + ot * 128 + 128],
                                        hin[:, it:it + 1],
                                        start=(it == 0), stop=(it == 7))
                    if ot == 15:
                        mm.then_inc(gp, 1)
                mm.then_inc(gp, 1)
                v_e.wait_ge(gp, io * 256 + ii * 4 + sj * 2 + 1)
                v_e.tensor_copy(hw[:, 0:16], psb[:, 0:16]).then_inc(gd, 1)
                v_e.wait_ge(gd, io * 1024 + ii * 16 + sj * 8 + 1)
                v_e.tensor_tensor(rz, hw[:, 0:16], xwv[:, 0:16, ii * 2 + sj], ALU.add).then_inc(gd, 1)
                s_e.wait_ge(gd, io * 1024 + ii * 16 + sj * 8 + 2)
                s_e.activation(rz, rz, AF.Sigmoid).then_inc(ga, 1)
                v_e.wait_ge(ga, io * 256 + ii * 4 + sj * 2 + 1)
                v_e.wait_ge(gp, io * 256 + ii * 4 + sj * 2 + 2)
                v_e.tensor_tensor(scr, rz[:, 0:8], psb2[:, 0:8], ALU.mult).then_inc(gd, 1)
                v_e.wait_ge(gd, io * 1024 + ii * 16 + sj * 8 + 3)
                v_e.tensor_tensor(scr, scr, xwv[:, 16:24, ii * 2 + sj], ALU.add).then_inc(gd, 1)
                s_e.wait_ge(gd, io * 1024 + ii * 16 + sj * 8 + 4)
                s_e.activation(scr, scr, AF.Tanh).then_inc(ga, 1)
                v_e.wait_ge(ga, io * 256 + ii * 4 + sj * 2 + 2)
                v_e.tensor_tensor(hw[:, 0:8], hin, scr, ALU.subtract).then_inc(gd, 1)
                v_e.wait_ge(gd, io * 1024 + ii * 16 + sj * 8 + 5)
                v_e.tensor_tensor(hw[:, 0:8], rz[:, 8:16], hw[:, 0:8], ALU.mult).then_inc(gd, 1)
                v_e.wait_ge(gd, io * 1024 + ii * 16 + sj * 8 + 6)
                v_e.tensor_tensor(hout, scr, hw[:, 0:8], ALU.add).then_inc(gd, 1)
                v_e.wait_ge(gd, io * 1024 + ii * 16 + sj * 8 + 7)
                v_e.tensor_copy(hsv[:, :, io * 128 + ii * 2 + sj], hout).then_inc(gd, 1)
    P.op("dve", lambda v: v.nop(), writes=hsk + B.keys(H0A, 20) + ("psb5", "psb4") + whhk)
    import os as _os
    if _os.environ.get("L2FFN_STOP") == "gru":
        return
    EXPW, LT, SQS, SSs, NUMs, EEs, CWSTG, TOPS = (16384, 20480, 22528, 22784,
                                                  23296, 23808, 28672, 26624)
    expw = B.arb(EXPW, 8192)
    expwk = B.keys(EXPW, 4096)
    P.dma(chm, expw.rearrange("p (ct o) -> p ct o", o=1024),
          aps["l2_exprw"].rearrange("(ct p) o -> p ct o", p=128), writes=expwk)
    eye = B.ar(TOPS + 128, 64)
    eyek = B.keys(TOPS + 128, 64)
    P.dma(chm, eye, aps["l2_eye"], writes=eyek)
    lt = B.arb(LT, 4096, 0, 8)
    ltk = B.keys(LT, 2048)
    sqs = B.ar(SQS, 512)
    sqk2 = B.keys(SQS, 512)
    SSr = B.ar(SSs, 512, 0, 8)
    NUr = B.ar(NUMs, 512, 0, 8)
    EEr = B.ar(EEs, 512, 0, 8)
    rowk = B.keys(SSs, 1536)
    exps = B.arb(27136, 1024)
    expsk = B.keys(27136, 512)
    for tc in range(8):
        bss, kss = B.ps[0], "psb0"
        bnu, knu = B.ps[1], "psb1"
        bee, kee = B.ps[2], "psb2"
        for e in range(8):
            P.op("dve", lambda v, e2=e, q=tc: v.tensor_mul(
                sqs, hs[:, e2 * T + q * 512:e2 * T + q * 512 + 512],
                hs[:, e2 * T + q * 512:e2 * T + q * 512 + 512]),
                reads=hsk, writes=sqk2)
            P.op("pe", lambda t, e2=e, b=bss: t.matmul(
                b[0:8, :], eye[:, e2 * 8:(e2 + 1) * 8], sqs, start=(e2 == 0), stop=(e2 == 7)),
                reads=sqk2 + eyek, writes=(kss,), inc=True)
        for e in range(8):
            pbx, pkx = B.ps[3 + (e % 2)], f"psb{3 + (e % 2)}"
            for ct in range(8):
                P.op("pe", lambda t, c=ct, e2=e, q=tc, b=pbx: t.matmul(
                    b[:, :], expw[:, c * 1024 + e2 * 128:c * 1024 + e2 * 128 + 128],
                    ssl(B, c, q * 512, 512), start=(c == 0), stop=(c == 7)),
                    reads=(f"st{ct}",) + expwk, writes=(pkx,), inc=(ct == 7))
            P.op("act", lambda s2, b=pbx: s2.activation(exps[:, 0:512], b[:, :], AF.Identity),
                 reads=(pkx,), writes=expsk)
            P.op("dve", lambda v, e2=e, q=tc: v.tensor_mul(
                sqs, exps[:, 0:512], hs[:, e2 * T + q * 512:e2 * T + q * 512 + 512]),
                reads=expsk + hsk, writes=sqk2)
            P.op("pe", lambda t, e2=e, b=bnu: t.matmul(
                b[0:8, :], eye[:, e2 * 8:(e2 + 1) * 8], sqs, start=(e2 == 0), stop=(e2 == 7)),
                reads=sqk2 + eyek, writes=(knu,), inc=True)
            P.op("dve", lambda v: v.tensor_mul(sqs, exps[:, 0:512], exps[:, 0:512]),
                 reads=expsk, writes=sqk2)
            P.op("pe", lambda t, e2=e, b=bee: t.matmul(
                b[0:8, :], eye[:, e2 * 8:(e2 + 1) * 8], sqs, start=(e2 == 0), stop=(e2 == 7)),
                reads=sqk2 + eyek, writes=(kee,), inc=True)
        P.op("dve", lambda v, b=bss: v.tensor_copy(SSr, b[0:8, :]), reads=(kss,), writes=rowk)
        P.op("dve", lambda v, b=bnu: v.tensor_copy(NUr, b[0:8, :]), reads=(knu,), writes=rowk)
        P.op("dve", lambda v, b=bee: v.tensor_copy(EEr, b[0:8, :]), reads=(kee,), writes=rowk)
        P.op("act", lambda s2: s2.activation(SSr, SSr, AF.Sqrt), reads=rowk, writes=rowk)
        P.op("dve", lambda v: v.tensor_scalar_max(SSr, SSr, 1e-12), reads=rowk, writes=rowk)
        P.op("act", lambda s2: s2.activation(EEr, EEr, AF.Sqrt), reads=rowk, writes=rowk)
        P.op("dve", lambda v: v.tensor_scalar_max(EEr, EEr, 1e-8), reads=rowk, writes=rowk)
        P.op("dve", lambda v: v.tensor_mul(SSr, SSr, EEr), reads=rowk, writes=rowk)
        P.op("dve", lambda v: v.reciprocal(SSr, SSr), reads=rowk, writes=rowk)
        P.op("dve", lambda v: v.tensor_mul(NUr, NUr, SSr), reads=rowk, writes=rowk)
        P.op("dve", lambda v, q=tc: v.tensor_scalar(
            lt[:, q * 512:(q + 1) * 512], NUr, -9.0, 9.0, ALU.mult, ALU.add),
            reads=rowk, writes=ltk)
    if _os.environ.get("L2FFN_STOP") == "scores":
        return
    cwstg, cwk = emit_top2_from_lt(P, B, lt, ltk, CWSTG, TOPS)
    if _os.environ.get("L2FFN_STOP") == "top2":
        return
    BAS2 = 30720
    bas = B.arb(BAS2, 1024, 0, 8)
    bask = B.keys(BAS2, 512)
    P.dma(chm, bas, aps["bas8"][0:8, :], writes=bask)
    emit_moe_mlp(P, B, "l2f", aps["l2_gate"], aps["l2_up"], aps["l2_down"], 9,
                 cwstg, cwk, (0, 8192, 16384, 20480, 31232), bas, bask)


def emit_l3_attn(P, B, aps):
    MT, WA, WB, WC_, HT = 0, 16384, 20480, 24576, 28672
    QT, KT_, VT, VTOK = 28672, 29184, 29952, 30720
    ET_, ETT, MSK, DEN = 31776, 32800, 33824, 34112
    mt = B.arb(MT, 32768)
    mtk = lambda ct: B.keys(MT + ct * 2048, 2048)
    mtka = B.keys(MT, 16384)
    ht = B.arb(HT, 4096)
    htk = B.keys(HT, 2048)
    wa = B.arb(WA, 8192)
    wak = B.keys(WA, 4096)
    wb = B.arb(WB, 8192)
    wbk = B.keys(WB, 4096)
    qt = B.arb(QT, 1024)
    qtk = B.keys(QT, 512)
    kt = B.arb(KT_, 1536)
    ktk = B.keys(KT_, 768)
    vt = B.arb(VT, 1536)
    vtk = B.keys(VT, 768)
    vtok = B.arb(VTOK, 2112)
    vtokk = B.keys(VTOK, 1056)
    et = B.arb(ET_, 2048)
    etk = B.keys(ET_, 1024)
    ett = B.arb(ETT, 2048)
    ettk = B.keys(ETT, 1024)
    msk = B.arb(MSK, 576)
    mskk = B.keys(MSK, 288)
    den = B.ar(DEN, 512)
    denk = B.keys(DEN, 512)
    chm = P.chan("l3a_misc", "pool")
    P.dma(chm, B.bias[:, 0:8], aps["l3_nm1b"], writes=("bias",))
    P.dma(chm, B.bias[:, 8:16], aps["l3_nm2b"], writes=("bias",))
    P.dma(chm, B.bias[:, 16:24], aps["l3_qb"], writes=("bias",))
    P.dma(chm, B.bias[:, 24:32], aps["l3_kb"], writes=("bias",))
    P.dma(chm, B.bias[:, 32:40], aps["l3_vb"], writes=("bias",))
    P.dma(chm, msk.rearrange("p (v c) -> p v c", c=192),
          aps["l3_masks"].rearrange("(v p) c -> p v c", p=128), writes=mskk)
    P.dma(chm, wa.rearrange("p (ct o) -> p ct o", o=1024),
          aps["l3_nm1"].rearrange("(ct p) o -> p ct o", p=128), writes=wak)
    P.dma(chm, wb.rearrange("p (ct o) -> p ct o", o=1024),
          aps["l3_nm2"].rearrange("(ct p) o -> p ct o", p=128), writes=wbk)
    for tc in range(8):
        for oc in range(8):
            pb, pk = B.bank()
            for ct in range(8):
                P.op("pe", lambda t, c=ct, o=oc, q=tc, b=pb: t.matmul(
                    b[:, :], wa[:, c * 1024 + o * 128:c * 1024 + o * 128 + 128],
                    ssl(B, c, q * 512, 512), start=(c == 0), stop=(c == 7)),
                    reads=(f"st{ct}",) + wak, writes=(pk,), inc=(ct == 7))
            P.op("act", lambda s2, o=oc, b=pb: s2.activation(
                ht[:, o * 512:(o + 1) * 512], b[:, :], AF.Relu, bias=B.bias[:, o:o + 1]),
                reads=(pk, "bias"), writes=htk)
        for oc in range(8):
            pb, pk = B.bank()
            for ct in range(8):
                P.op("pe", lambda t, c=ct, o=oc, b=pb: t.matmul(
                    b[:, :], wb[:, c * 1024 + o * 128:c * 1024 + o * 128 + 128],
                    ht[:, c * 512:(c + 1) * 512], start=(c == 0), stop=(c == 7)),
                    reads=htk + wbk, writes=(pk,), inc=(ct == 7))
            P.op("act", lambda s2, o=oc, q=tc, b=pb: s2.activation(
                mt[:, o * 4096 + q * 512:o * 4096 + q * 512 + 512], b[:, :],
                AF.Identity, bias=B.bias[:, 8 + o:9 + o]),
                reads=(pk, "bias"), writes=mtk(oc)),
    P.dma(chm, wa.rearrange("p (ct o) -> p ct o", o=1024),
          aps["l3_qw"].rearrange("(ct p) o -> p ct o", p=128), writes=wak)
    P.dma(chm, wb.rearrange("p (ct o) -> p ct o", o=1024),
          aps["l3_kw"].rearrange("(ct p) o -> p ct o", p=128), writes=wbk)
    wc = B.arb(WC_, 8192)
    wck = B.keys(WC_, 4096)
    P.dma(chm, wc.rearrange("p (ct o) -> p ct o", o=1024),
          aps["l3_vw"].rearrange("(ct p) o -> p ct o", p=128), writes=wck)
    SC = float(1.0 / np.sqrt(1024.0))
    for it in range(32):
        i0 = it * 128
        s0 = max(0, min(i0 - 32, T - 192))
        var = 0 if it == 0 else (2 if it == 31 else 1)
        for oc in range(8):
            pb, pk = B.bank()
            for ct in range(8):
                P.op("pe", lambda t, c=ct, o=oc, b=pb: t.matmul(
                    b[:, 0:128], wa[:, c * 1024 + o * 128:c * 1024 + o * 128 + 128],
                    mt[:, c * 4096 + i0:c * 4096 + i0 + 128], start=(c == 0), stop=(c == 7)),
                    reads=mtka + wak, writes=(pk,), inc=(ct == 7))
            P.op("act", lambda s2, o=oc, b=pb: s2.activation(
                qt[:, o * 128:(o + 1) * 128], b[:, 0:128], AF.Identity,
                bias=B.bias[:, 16 + o:17 + o]),
                reads=(pk, "bias"), writes=qtk)
        for wsel, dst, dk2, bcol in ((1, kt, ktk, 24), (2, vt, vtk, 32)):
            ww = wb if wsel == 1 else wc
            wwk = wbk if wsel == 1 else wck
            for oc in range(8):
                pb, pk = B.bank()
                for ct in range(8):
                    P.op("pe", lambda t, c=ct, o=oc, b=pb, w2=ww: t.matmul(
                        b[:, 0:192], w2[:, c * 1024 + o * 128:c * 1024 + o * 128 + 128],
                        mt[:, c * 4096 + s0:c * 4096 + s0 + 192], start=(c == 0), stop=(c == 7)),
                        reads=mtka + wwk, writes=(pk,), inc=(ct == 7))
                P.op("act", lambda s2, o=oc, b=pb, d2=dst, bc=bcol: s2.activation(
                    d2[:, o * 192:o * 192 + 192], b[:, 0:192], AF.Identity,
                    bias=B.bias[:, bc + o:bc + o + 1]),
                    reads=(pk, "bias"), writes=dk2)
        for oc in range(8):
            for sub in range(2):
                w_ = 128 if sub == 0 else 64
                j = (oc * 2 + sub) % 2
                pst = B.pst[j]
                P.op("pe", lambda t, o=oc, sb=sub, p=pst, wz=w_: t.transpose(
                    p[0:wz, 0:128],
                    vt[:, o * 192 + sb * 128:o * 192 + sb * 128 + (128 if sb == 0 else 64)],
                    B.idn[:]),
                    reads=vtk + ("idn",), writes=(f"pstb{j}",))
                P.op("dve", lambda v, o=oc, sb=sub, p=pst, wz=w_: v.tensor_copy(
                    B.arb(VTOK + sb * 528 + o * 64, 128, 0, wz), p[0:wz, 0:128]),
                    reads=(f"pstb{j}",), writes=vtokk)
        pb, pk = B.bank()
        for ct in range(8):
            P.op("pe", lambda t, c=ct, b=pb: t.matmul(
                b[:, 0:192], qt[:, c * 128:(c + 1) * 128], kt[:, c * 192:c * 192 + 192],
                start=(c == 0), stop=(c == 7)),
                reads=qtk + ktk, writes=(pk,), inc=(ct == 7))
        P.op("act", lambda s2, b=pb: s2.activation(et[:, 0:192], b[:, 0:192], AF.Exp, scale=SC),
             reads=(pk,), writes=etk)
        P.op("dve", lambda v, vv=var: v.tensor_mul(
            et[:, 0:192], et[:, 0:192], msk.rearrange("p (v c) -> p v c", c=192)[:, vv, :]),
            reads=etk + mskk, writes=etk)
        for sub in range(2):
            w_ = 128 if sub == 0 else 64
            j = sub % 2
            pst = B.pst[j]
            P.op("pe", lambda t, sb=sub, p=pst, wz=w_: t.transpose(
                p[0:wz, 0:128], et[:, sb * 128:sb * 128 + wz], B.idn[:]),
                reads=etk + ("idn",), writes=(f"pstb{j}",))
            P.op("dve", lambda v, sb=sub, p=pst, wz=w_: v.tensor_copy(
                B.arb(ETT + sb * 512, 128, 0, wz), p[0:wz, 0:128]),
                reads=(f"pstb{j}",), writes=ettk)
        pbd, pkd = B.bank()
        P.op("pe", lambda t, b=pbd: t.matmul(
            b[0:1, 0:128], B.onesb[:, 0:1], B.arb(ETT, 128), start=True, stop=False),
            reads=ettk + ("onesb",), writes=(pkd,), inc=False)
        P.op("pe", lambda t, b=pbd: t.matmul(
            b[0:1, 0:128], B.onesb[0:64, 0:1], B.arb(ETT + 512, 128, 0, 64),
            start=False, stop=True),
            reads=ettk + ("onesb",), writes=(pkd,), inc=True)
        P.op("dve", lambda v, b=pbd: v.reciprocal(den[0:1, 0:128], b[0:1, 0:128]),
             reads=(pkd,), writes=denk)
        pbr, pkr = B.bank()
        P.op("pe", lambda t, b=pbr: t.matmul(
            b[:, 0:128], B.onesrow[0:1, :], den[0:1, 0:128], start=True, stop=True),
            reads=denk + ("onesrow",), writes=(pkr,))
        P.op("act", lambda s2, b=pbr: s2.activation(B.arb(DEN + 128, 128), b[:, 0:128], AF.Identity),
             reads=(pkr,), writes=denk)
        for oc in range(8):
            pb2, pk2 = B.bank()
            P.op("pe", lambda t, o=oc, b=pb2: t.matmul(
                b[:, 0:128], B.arb(VTOK + o * 64, 128), B.arb(ETT, 128),
                start=True, stop=False),
                reads=vtokk + ettk, writes=(pk2,), inc=False)
            P.op("pe", lambda t, o=oc, b=pb2: t.matmul(
                b[:, 0:128], B.arb(VTOK + 528 + o * 64, 128, 0, 64),
                B.arb(ETT + 512, 128, 0, 64), start=False, stop=True),
                reads=vtokk + ettk, writes=(pk2,), inc=True)
            P.op("dve", lambda v, o=oc, b=pb2: v.tensor_mul(
                et[:, 0:128], b[:, 0:128], B.arb(DEN + 128, 128)),
                reads=(pk2,) + denk + etk, writes=etk)
            P.op("dve", lambda v, o=oc, i2=it: v.tensor_add(
                ssl(B, o, i2 * 128, 128), ssl(B, o, i2 * 128, 128), et[:, 0:128]),
                reads=etk + (f"st{oc}",), writes=(f"st{oc}",))


def emit_l3_ffn(P, B, aps, mspill):
    T1o, T2o, TRIo, KMTo, RTRo, GROWo, LOo = 0, 2048, 4096, 5120, 5376, 5408, 7456
    SCT, CMPF, KMo = 7464, 11560, 11560
    ATT, MSB, DWOB = 7464, 15656, 17704
    GATo = 17704
    WH, QK, SW, KR = 19752, 21288, 23336, 25384
    VT2, VTK2, EB, RROW, RB = 27432, 29480, 30536, 31048, 31560
    MWOB = 31816
    t1 = B.arb(T1o, 4096); t1k = B.keys(T1o, 2048)
    t2 = B.arb(T2o, 4096); t2k = B.keys(T2o, 2048)
    tri = B.arb(TRIo, 2048); trik = B.keys(TRIo, 1024)
    kmt = B.ar(KMTo, 256); kmtk = B.keys(KMTo, 256)
    rtr = B.arb(RTRo, 64); rtrk = B.keys(RTRo, 32)
    grow = B.arb(GROWo, 4096, 0, 1); growk = B.keys(GROWo, 2048)
    lo = B.ar(LOo, 8, 0, 8); lok = B.keys(LOo, 8)
    sct = B.ar(SCT, 4096, 0, 8); sctk = B.keys(SCT, 4096)
    cmpf = B.ar(CMPF, 4096, 0, 8); cmpk = B.keys(CMPF, 4096)
    km = B.arb(KMo, 4096, 0, 8); kmk = B.keys(KMo, 2048)
    gat = B.arb(GATo, 4096, 0, 8); gatk = B.keys(GATo, 2048)
    attk = lambda pr: B.keys(ATT + pr * 2048, 2048)
    attka = B.keys(ATT, 8192)
    msb = [B.arb(MSB + i * 1024, 2048) for i in range(2)]
    msbk = [B.keys(MSB + i * 1024, 1024) for i in range(2)]
    dwo = B.arb(DWOB, 4096); dwok = B.keys(DWOB, 2048)
    wh = [B.arb(WH + i * 768, 1536) for i in range(2)]
    whk = [B.keys(WH + i * 768, 768) for i in range(2)]
    qk = B.arb(QK, 4096); qkk = B.keys(QK, 2048)
    swk = B.keys(SW, 2048)
    kr = B.arb(KR, 4096, 0, 64); krk = B.keys(KR, 2048)
    vt2 = B.arb(VT2, 4096, 0, 64); vt2k = B.keys(VT2, 2048)
    EDEPTH = 5
    _ebo = [EB, EB + 256, 33864, 34120, 34376]
    ebuf = [B.arb(_ebo[i], 512) for i in range(EDEPTH)]
    ekc = [(f"f3e{i}",) for i in range(EDEPTH)]
    vtk = ("f3vt",)
    rrow = B.ar(RROW, 512, 0, 1); rrk = B.keys(RROW, 512)
    rbuf = B.arb(RB, 512, 0, 64); rbk = B.keys(RB, 256)
    mwo = B.arb(MWOB, 4096); mwok = B.keys(MWOB, 2048)

    chm = P.chan("l3f_misc", "pool")
    P.dma(chm, t1, aps["l3_t1"], writes=t1k)
    P.dma(chm, t2, aps["l3_t2"], writes=t2k)
    P.dma(chm, tri, aps["l3_tri"], writes=trik)
    P.dma(chm, rtr.rearrange("p (ct c) -> p ct c", c=8),
          aps["l3_router"].rearrange("(ct p) c -> p ct c", p=128), writes=rtrk)
    P.dma(chm, mwo.rearrange("p (ct o) -> p ct o", o=1024),
          aps["l3_mwo"].rearrange("(ct p) o -> p ct o", p=128), writes=mwok)

    def fresh_bank(avoid=()):
        b, k = B.bank()
        while k in avoid:
            b, k = B.bank()
        return b, k

    for tc in range(8):
        pb, pk = fresh_bank()
        for ct in range(NDT):
            P.op("pe", lambda t, c=ct, q=tc, b=pb: t.matmul(
                b[0:8, :], rtr.rearrange("p (ct c) -> p ct c", c=8)[:, c, :],
                ssl(B, c, q * 512, 512), start=(c == 0), stop=(c == NDT - 1)),
                reads=(f"st{ct}",) + rtrk, writes=(pk,), inc=(ct == NDT - 1))
        P.op("dve", lambda v, q=tc, b=pb: v.tensor_copy(
            sct[:, q * 512:(q + 1) * 512], b[0:8, :]), reads=(pk,), writes=sctk)
    P.op("dve", lambda v: v.tensor_reduce(lo[:, 0:1], sct, axis=AX.X, op=ALU.min),
         reads=sctk, writes=lok)
    P.op("dve", lambda v: v.reduce_max(lo[:, 1:2], sct, axis=AX.X), reads=sctk, writes=lok)
    P.op("dve", lambda v: v.tensor_scalar_add(lo[:, 0:1], lo[:, 0:1], -1.0), reads=lok, writes=lok)
    P.op("dve", lambda v: v.tensor_scalar_add(lo[:, 1:2], lo[:, 1:2], 1.0), reads=lok, writes=lok)
    for _ in range(42):
        P.op("dve", lambda v: v.tensor_add(lo[:, 2:3], lo[:, 0:1], lo[:, 1:2]), reads=lok, writes=lok)
        P.op("dve", lambda v: v.tensor_scalar_mul(lo[:, 2:3], lo[:, 2:3], 0.5), reads=lok, writes=lok)
        P.op("dve", lambda v: v.tensor_scalar(cmpf, sct, lo[:, 2:3], None, ALU.is_ge),
             reads=sctk + lok, writes=cmpk)
        P.op("dve", lambda v: v.reduce_sum(lo[:, 3:4], cmpf, axis=AX.X), reads=cmpk, writes=lok)
        P.op("dve", lambda v: v.tensor_scalar(lo[:, 4:5], lo[:, 3:4], 512.0, None, ALU.is_ge),
             reads=lok, writes=lok)
        P.op("dve", lambda v: v.tensor_sub(lo[:, 5:6], lo[:, 2:3], lo[:, 0:1]), reads=lok, writes=lok)
        P.op("dve", lambda v: v.tensor_mul(lo[:, 5:6], lo[:, 5:6], lo[:, 4:5]), reads=lok, writes=lok)
        P.op("dve", lambda v: v.tensor_add(lo[:, 0:1], lo[:, 0:1], lo[:, 5:6]), reads=lok, writes=lok)
        P.op("dve", lambda v: v.tensor_scalar(lo[:, 6:7], lo[:, 4:5], -1.0, 1.0, ALU.mult, ALU.add),
             reads=lok, writes=lok)
        P.op("dve", lambda v: v.tensor_sub(lo[:, 5:6], lo[:, 2:3], lo[:, 1:2]), reads=lok, writes=lok)
        P.op("dve", lambda v: v.tensor_mul(lo[:, 5:6], lo[:, 5:6], lo[:, 6:7]), reads=lok, writes=lok)
        P.op("dve", lambda v: v.tensor_add(lo[:, 1:2], lo[:, 1:2], lo[:, 5:6]), reads=lok, writes=lok)
    P.op("dve", lambda v: v.tensor_scalar(km, sct, lo[:, 0:1], None, ALU.is_ge),
         reads=sctk + lok, writes=kmk)
    P.op("act", lambda s2: s2.activation(gat, sct, AF.Sigmoid), reads=sctk, writes=gatk)
    P.op("dve", lambda v: v.tensor_mul(gat, gat, km), reads=gatk + kmk, writes=gatk)
    for kt in range(32):
        j = kt % 2
        pst = B.pst[j]
        P.op("pe", lambda t, q=kt, p=pst: t.transpose(
            p[0:128, 0:8], km[:, q * 128:(q + 1) * 128], B.idn[0:8, 0:8]),
            reads=kmk + ("idn",), writes=(f"pstb{j}",))
        P.op("dve", lambda v, q=kt, p=pst: v.tensor_copy(
            kmt[:, q * 8:(q + 1) * 8], p[0:128, 0:8]),
            reads=(f"pstb{j}",), writes=kmtk)

    SCH = 0.125
    chsw = P.chan("l3f_sw", "sp")

    def run_branch(wq_ap, mosa):
        for h in range(8):
            par = h % 2
            whr = wh[par].rearrange("p (ct c) -> p ct c", c=192)
            src = bass.AP(wq_ap.tensor, h * 1024 * 192,
                          [[192, 128], [192 * 128, 8], [1, 192]])
            P.dma(P.chan(f"l3f_w{par}", "pool"), whr, src, writes=whk[par])
            for tc in range(8):
                pb, pk = fresh_bank()
                for ct in range(8):
                    P.op("pe", lambda t, c=ct, q=tc, b=pb, w2=whr: t.matmul(
                        b[:, :], w2[:, c, 0:128], ssl(B, c, q * 512, 512),
                        start=(c == 0), stop=(c == 7)),
                        reads=(f"st{ct}",) + whk[par], writes=(pk,), inc=(ct == 7))
                P.op("act", lambda s2, q=tc, b=pb: s2.activation(
                    qk[:, q * 512:(q + 1) * 512], b[:, :], AF.Copy),
                    reads=(pk,), writes=qkk)
                pb2, pk2 = fresh_bank()
                for ct in range(8):
                    P.op("pe", lambda t, c=ct, q=tc, b=pb2, w2=whr: t.matmul(
                        b[0:64, :], w2[:, c, 128:192], ssl(B, c, q * 512, 512),
                        start=(c == 0), stop=(c == 7)),
                        reads=(f"st{ct}",) + whk[par], writes=(pk2,), inc=(ct == 7))
                P.op("act", lambda s2, q=tc, b=pb2: s2.activation(
                    vt2[:, q * 512:(q + 1) * 512], b[0:64, :], AF.Copy),
                    reads=(pk2,), writes=vt2k)
            for (d0, s0_) in ((0, 16), (16, 0), (64, 80), (80, 64)):
                P.dma(chsw, B.arb(SW, 4096, d0, 16), B.arb(QK, 4096, s0_, 16),
                      reads=qkk, writes=swk)
            P.op("dve", lambda v: v.tensor_mul(
                B.arb(SW, 4096, 0, 32), B.arb(SW, 4096, 0, 32), B.arb(T2o, 4096, 0, 32)),
                reads=swk + t2k, writes=swk)
            P.op("dve", lambda v: v.tensor_mul(
                B.arb(SW, 4096, 64, 32), B.arb(SW, 4096, 64, 32), B.arb(T2o, 4096, 64, 32)),
                reads=swk + t2k, writes=swk)
            P.op("dve", lambda v: v.tensor_mul(qk, qk, t1), reads=qkk + t1k, writes=qkk)
            P.op("dve", lambda v: v.tensor_add(
                B.arb(QK, 4096, 0, 32), B.arb(QK, 4096, 0, 32), B.arb(SW, 4096, 0, 32)),
                reads=qkk + swk, writes=qkk)
            P.op("dve", lambda v: v.tensor_add(
                B.arb(QK, 4096, 64, 32), B.arb(QK, 4096, 64, 32), B.arb(SW, 4096, 64, 32)),
                reads=qkk + swk, writes=qkk)
            P.dma(chsw, kr, B.arb(QK, 4096, 64, 64), reads=qkk, writes=krk)
            for kt in range(32):
                j = kt % 2
                pst = B.pst[j]
                P.op("pe", lambda t, q=kt, p=pst: t.transpose(
                    p[:, 0:64], vt2[:, q * 128:(q + 1) * 128], B.idn[0:64, 0:64]),
                    reads=vt2k + ("idn",), writes=(f"pstb{j}",))
                if mosa:
                    P.op("dve", lambda v, q=kt, p=pst: v.tensor_scalar(
                        B.arb(VTK2 + q * 33, 64), p[:, 0:64],
                        kmt[:, q * 8 + h:q * 8 + h + 1], None, ALU.mult),
                        reads=(f"pstb{j}",) + kmtk, writes=vtk)
                    P.op("dve", lambda v, q=kt: v.tensor_copy(
                        B.arb(VTK2 + q * 33 + 32, 2)[:, 0:1],
                        kmt[:, q * 8 + h:q * 8 + h + 1]),
                        reads=kmtk, writes=vtk)
                else:
                    P.op("dve", lambda v, q=kt, p=pst: v.tensor_copy(
                        B.arb(VTK2 + q * 33, 64), p[:, 0:64]),
                        reads=(f"pstb{j}",), writes=vtk)
                    if h == 0:
                        P.op("dve", lambda v, q=kt: v.memset(
                            B.arb(VTK2 + q * 33 + 32, 2)[:, 0:1], 1.0), writes=vtk)
            if mosa:
                P.dma(chsw, grow, gat[h:h + 1, :], reads=gatk, writes=growk)
            pair, prow = h // 2, (h % 2) * 64
            for qc in range(8):
                nb, nk = fresh_bank()
                nkt = qc * 4 + 4
                DEPTH = min(4, nkt - 1)
                for kt in range(nkt + DEPTH):
                    if kt < nkt:
                        sb, sk = fresh_bank((nk,))
                        P.op("pe", lambda t, k_=kt, q=qc, b=sb: t.matmul(
                            b[:, :], kr[:, k_ * 128:(k_ + 1) * 128],
                            B.arb(QK, 4096, 0, 64)[:, q * 512:(q + 1) * 512],
                            start=True, stop=True),
                            reads=krk + qkk, writes=(sk,))
                        P.op("act", lambda s2, k_=kt, b=sb: s2.activation(
                            ebuf[k_ % EDEPTH], b[:, :], AF.Exp, scale=SCH),
                            reads=(sk,), writes=ekc[kt % EDEPTH])
                        if kt >= qc * 4:
                            o = kt - qc * 4
                            P.op("dve", lambda v, k_=kt, o2=o: v.tensor_mul(
                                ebuf[k_ % EDEPTH], ebuf[k_ % EDEPTH], tri[:, o2 * 512:(o2 + 1) * 512]),
                                reads=ekc[kt % EDEPTH] + trik, writes=ekc[kt % EDEPTH])
                    if kt >= DEPTH:
                        k0 = kt - DEPTH
                        P.op("pe", lambda t, k_=k0, b=nb, n=nkt: t.matmul(
                            b[0:65, :], B.arb(VTK2 + k_ * 33, 65), ebuf[k_ % EDEPTH],
                            start=(k_ == 0), stop=(k_ == n - 1)),
                            reads=ekc[k0 % EDEPTH] + vtk, writes=(nk,), inc=(k0 == nkt - 1))
                if mosa:
                    P.op("dve", lambda v, b=nb: v.tensor_scalar_max(
                        rrow, b[64:65, :], 1e-30), reads=(nk,), writes=rrk)
                    P.op("dve", lambda v: v.reciprocal(rrow, rrow), reads=rrk, writes=rrk)
                    P.op("dve", lambda v, q=qc: v.tensor_mul(
                        rrow, rrow, grow[0:1, q * 512:(q + 1) * 512]),
                        reads=rrk + growk, writes=rrk)
                else:
                    P.op("dve", lambda v, b=nb: v.reciprocal(rrow, b[64:65, :]),
                         reads=(nk,), writes=rrk)
                bb, bbk2 = fresh_bank((nk,))
                P.op("pe", lambda t, b=bb: t.matmul(
                    b[0:64, :], B.onesrow[0:1, 0:64], rrow, start=True, stop=True),
                    reads=rrk + ("onesrow",), writes=(bbk2,))
                P.op("act", lambda s2, b=bb: s2.activation(rbuf, b[0:64, :], AF.Copy),
                     reads=(bbk2,), writes=rbk)
                P.op("dve", lambda v, b=nb, q=qc, pr=pair, pw=prow: v.tensor_mul(
                    B.arb(ATT + pr * 2048 + q * 256, 512, pw, 64), b[0:64, :], rbuf),
                    reads=(nk,) + rbk, writes=attk(pair))
        if mosa:
            for pr in range(4):
                P.dma(P.chan("l3f_ms", "sp"),
                      bass.AP(mspill.tensor, pr * 128 * T, [[T, 128], [1, T]]),
                      B.arb(ATT + pr * 2048, 4096), reads=attk(pr), writes=("msp",))

    run_branch(aps["l3_mqkv"], True)
    P.dma(P.chan("l3f_wo2", "pool"), dwo.rearrange("p (ct o) -> p ct o", o=1024),
          aps["l3_dwo"].rearrange("(ct p) o -> p ct o", p=128),
          reads=("msp",), writes=dwok)
    run_branch(aps["l3_dqkv"], False)
    for tc in range(8):
        par = tc % 2
        mr = msb[par].rearrange("p (ct c) -> p ct c", c=512)
        P.dma(P.chan(f"l3f_mr{par}", "pool"), mr,
              bass.AP(mspill.tensor, tc * 512, [[T, 128], [128 * T, 4], [1, 512]]),
              reads=("msp",), writes=msbk[par])
        for oc in range(8):
            pb, pk = fresh_bank()
            for ct in range(4):
                P.op("pe", lambda t, c=ct, o=oc, b=pb, m2=mr: t.matmul(
                    b[:, :], mwo[:, c * 1024 + o * 128:c * 1024 + o * 128 + 128],
                    m2[:, c, :], start=(c == 0), stop=False),
                    reads=msbk[par] + mwok, writes=(pk,), inc=False)
            for ct in range(4):
                P.op("pe", lambda t, c=ct, o=oc, q=tc, b=pb: t.matmul(
                    b[:, :], dwo[:, c * 1024 + o * 128:c * 1024 + o * 128 + 128],
                    B.arb(ATT + c * 2048 + q * 256, 512), start=False, stop=(c == 3)),
                    reads=attka + dwok, writes=(pk,), inc=(ct == 3))
            P.op("dve", lambda v, o=oc, q=tc, b=pb: v.tensor_add(
                ssl(B, o, q * 512, 512), ssl(B, o, q * 512, 512), b[:, :]),
                reads=(pk, f"st{oc}"), writes=(f"st{oc}",))
    P.op("dve", lambda v: v.tensor_copy(lo[:, 2:3], lo[:, 0:1]),
         reads=("f3e0", "f3e1", "f3vt") + lok,
         writes=B.keys(VTK2, 1056) + B.keys(EB, 512) + lok)


SLOT_EMIT = {
    "l0_attn": emit_l0_attn,
    "l0_ffn": emit_l0_ffn,
    "l1_attn": emit_l1_attn,
    "l1_ffn": emit_l1_ffn,
    "l2_attn": lambda P, B, aps: emit_l2_attn(P, B, aps, B._l2_spills),
    "l2_ffn": lambda P, B, aps: emit_l2_ffn(P, B, aps, B._l2_xwspill),
    "l3_attn": emit_l3_attn,
    "l3_ffn": lambda P, B, aps: emit_l3_ffn(P, B, aps, B._l3_mspill),
}


DBG = {}


def build_program(slots, shapes):
    nc = bass.Bass("TRN2", target_bir_lowering=False)
    x_hbm = nc.dram_tensor("x", [T, D], F32, kind="ExternalInput").ap()
    y_hbm = nc.dram_tensor("y", [T, D], F32, kind="ExternalOutput").ap()
    need = set(SLOT_INPUTS["prologue"])
    for s in slots:
        if s != "ln":
            need.update(SLOT_INPUTS[s])
    aps = {}
    for name in sorted(need):
        sh, npdt = shapes[name]
        dt = {"float32": F32, "int32": I32}[str(npdt)]
        aps[name] = nc.dram_tensor(name, list(sh), dt, kind="ExternalInput").ap()
    with ExitStack() as ctx:
        P = Prog(nc)
        ctx.enter_context(P.ctx)
        B = Bufs(nc, ctx)
        if "l2_attn" in slots:
            B._l2_spills = (nc.dram_tensor("l2_uspill", [T, 2048], BF16).ap(),
                            nc.dram_tensor("l2_vspill", [T, 2048], BF16).ap())
        if "l2_ffn" in slots:
            B._l2_xwspill = nc.dram_tensor("l2_xwspill", [24 * 128, T], BF16).ap()
        if "l3_ffn" in slots:
            B._l3_mspill = nc.dram_tensor("l3_mspill", [4 * 128, T], BF16).ap()
        emit_prologue(P, B, aps, x_hbm)
        for s in slots:
            if s == "ln":
                emit_ln(P, B)
            else:
                SLOT_EMIT[s](P, B, aps)
        emit_epilogue(P, B, y_hbm)
        P.finalize()
        DBG["P"] = P
        DBG["semmap"] = {k: str(v) for k, v in P.sem.items()}
    return nc, sorted(need)


_PLAN = [["l0_attn", "ln", "l0_ffn", "ln",
          "l1_attn", "ln", "l1_ffn", "ln",
          "l2_attn", "ln", "l2_ffn", "ln",
          "l3_attn", "ln", "l3_ffn", "ln"]]


def kernel(**inputs):
    g = host_prep(inputs)
    shapes = {k: (v.shape, v.dtype) for k, v in g.items()}
    x = np.asarray(inputs["L0_attn_x"], dtype=np.float32)
    cur = [np.ascontiguousarray(x[c]) for c in range(NCORES)]
    for slots in _PLAN:
        nc, names = build_program(slots, shapes)
        in_maps = [dict({n: g[n] for n in names}, x=cur[c]) for c in range(NCORES)]
        res = run_bass_kernel_spmd(nc, in_maps, core_ids=list(range(NCORES)))
        cur = [np.asarray(res.results[c]["y"]) for c in range(NCORES)]
    return np.stack(cur).astype(np.float32)
